# version 09
# baseline (speedup 1.0000x reference)
import contextlib
import math

import numpy as np

import concourse.bass as bass
import concourse.mybir as mybir
from concourse.alu_op_type import AluOpType as OP
from concourse.bass_utils import run_bass_kernel_spmd

F32 = mybir.dt.float32
F16 = mybir.dt.float16
BF16 = mybir.dt.bfloat16
FP8 = mybir.dt.float8e4
I32 = mybir.dt.int32
AF = mybir.ActivationFunctionType
AX = mybir.AxisListType

C1 = 12582912.0
C2 = 6755399441055744.0
C3 = 6442450944.0

TP1 = float(np.float32(2 * math.pi))
TP2 = float(np.float32(2 * math.pi - TP1))
TP3 = float(2 * math.pi - TP1 - TP2)
INV_2PI = 1.0 / (2 * math.pi)
HALF_PI = math.pi / 2
PI_F = float(np.float32(math.pi))
NEGBIG = 1.0e7


class Cfg:
    def __init__(self, S=4096, E=2048, H=16, HKV=2, theta=10000.0):
        self.S, self.E, self.H, self.HKV, self.theta = S, E, H, HKV, theta
        self.DH = 128
        self.G = H // HKV
        self.KVD = HKV * self.DH
        self.HID = 4 * E
        self.EC = E // 128
        self.HC = self.HID // 128
        self.TB = 512
        self.NB = S // self.TB
        self.MYNB = self.NB // 2
        self.MYT = self.MYNB * self.TB
        self.eps = 1e-5

    def my_blocks(self, parity):
        n, q = self.NB, self.NB // 4
        if parity == 0:
            return list(range(q)) + list(range(n - q, n))
        return list(range(q, n - q))


def build_program(cfg, debug=()):
    nc = bass.Bass("TRN2", target_bir_lowering=False)
    S, E, H, HKV, DH = cfg.S, cfg.E, cfg.H, cfg.HKV, cfg.DH
    EC, HC, HID, KVD = cfg.EC, cfg.HC, cfg.HID, cfg.KVD
    TB, NB, MYNB, MYT = cfg.TB, cfg.NB, cfg.MYNB, cfg.MYT
    E6 = 6 * E
    GQ = cfg.G

    x_in = nc.dram_tensor("x", [S, E], F32, kind="ExternalInput").ap()
    c_in = nc.dram_tensor("c_row", [1, E], F32, kind="ExternalInput").ap()
    wmod_in = nc.dram_tensor("wmod", [E, E6], F32, kind="ExternalInput").ap()
    wq_in = nc.dram_tensor("wq", [E, E], F32, kind="ExternalInput").ap()
    wk_in = nc.dram_tensor("wk", [KVD, E], F32, kind="ExternalInput").ap()
    wv_in = nc.dram_tensor("wv", [KVD, E], F32, kind="ExternalInput").ap()
    wo_in = nc.dram_tensor("wo", [E, E], F32, kind="ExternalInput").ap()
    bq_in = nc.dram_tensor("bq", [1, E], F32, kind="ExternalInput").ap()
    bk_in = nc.dram_tensor("bk", [1, KVD], F32, kind="ExternalInput").ap()
    bv_in = nc.dram_tensor("bv", [1, KVD], F32, kind="ExternalInput").ap()
    gw_in = nc.dram_tensor("gate_w", [HID, E], F32, kind="ExternalInput").ap()
    vw_in = nc.dram_tensor("val_w", [HID, E], F32, kind="ExternalInput").ap()
    ow_in = nc.dram_tensor("out_w", [E, HID], F32, kind="ExternalInput").ap()
    gb_in = nc.dram_tensor("gate_b", [1, HID], F32, kind="ExternalInput").ap()
    vb_in = nc.dram_tensor("val_b", [1, HID], F32, kind="ExternalInput").ap()
    ob_in = nc.dram_tensor("ob", [1, E], F32, kind="ExternalInput").ap()
    par_in = nc.dram_tensor("parity", [1, 1], I32, kind="ExternalInput").ap()
    y_out = nc.dram_tensor("y", [MYT, E], F32, kind="ExternalOutput").ap()

    x16 = nc.dram_tensor("x16", [S, E], BF16, kind="Internal").ap()
    wq16 = nc.dram_tensor("wq16", [E, E], BF16, kind="Internal").ap()
    wk16 = nc.dram_tensor("wk16", [KVD, E], BF16, kind="Internal").ap()
    wv16 = nc.dram_tensor("wv16", [KVD, E], BF16, kind="Internal").ap()
    wo16 = nc.dram_tensor("wo16", [E, E], BF16, kind="Internal").ap()
    gw16 = nc.dram_tensor("gw16", [HID, E], F16, kind="Internal").ap()
    vw16 = nc.dram_tensor("vw16", [HID, E], F16, kind="Internal").ap()
    ow16 = nc.dram_tensor("ow16", [E, HID], F16, kind="Internal").ap()
    tg8 = nc.dram_tensor("tg8", [E, HID], FP8, kind="Internal").ap()
    tv8 = nc.dram_tensor("tv8", [E, HID], FP8, kind="Internal").ap()
    to8 = nc.dram_tensor("to8", [HID, E], FP8, kind="Internal").ap()
    modscr = nc.dram_tensor("modscr", [1, E6], F32, kind="Internal").ap()
    trigscr = nc.dram_tensor("trigscr", [128, 2 * S], BF16, kind="Internal").ap()
    h1scr = nc.dram_tensor("h1scr", [128, MYNB * EC * TB], BF16,
                           kind="Internal").ap()

    dbg = {}
    for spec_ in debug:
        name, shp = spec_[0], list(spec_[1])
        ddt = {"f32": F32, "bf16": BF16, "f16": F16}[spec_[2] if len(spec_) > 2 else "f32"]
        dbg[name] = nc.dram_tensor("dbg_" + name, shp, ddt,
                                   kind="ExternalOutput").ap()
    n_dbg_stores = [0]

    es0 = contextlib.ExitStack()

    def mk(stack, name, shape, dt):
        return stack.enter_context(nc.sbuf_tensor(name, shape, dt))

    xres = mk(es0, "xres", [128, EC * MYT], BF16)
    onesb = mk(es0, "onesb", [128, 128], BF16)
    onesf = mk(es0, "onesf", [128, 128], F32)
    identb = mk(es0, "identb", [128, 128], BF16)
    swapb = mk(es0, "swapb", [128, 128], BF16)
    modc = mk(es0, "modc", [128, 6 * EC], F32)
    scb = mk(es0, "scb", [128, EC], BF16)
    biasq = mk(es0, "biasq", [128, H], F32)
    biask = mk(es0, "biask", [128, HKV], F32)
    bvbank = mk(es0, "bvbank", [128, KVD], F32)
    biasg = mk(es0, "biasg", [128, HC], F32)
    biasvb = mk(es0, "biasvb", [128, HC], F32)
    biaso = mk(es0, "biaso", [128, EC], F32)
    gcols = mk(es0, "gcols", [128, 12], F32)
    cacol = mk(es0, "cacol", [128, EC], F32)
    bias2 = mk(es0, "bias2", [128, EC], F32)
    negB = mk(es0, "negB", [128, H], F32)
    qmax2 = mk(es0, "qmax2", [128, H], F32)
    kmax2 = mk(es0, "kmax2", [128, HKV], F32)
    tA = mk(es0, "tA", [128, TB], F32)
    tB = mk(es0, "tB", [128, TB], F32)
    tC = mk(es0, "tC", [128, TB], F32)
    tD = mk(es0, "tD", [128, TB], F32)
    tE16 = mk(es0, "tE16", [128, TB], BF16)
    tF16 = mk(es0, "tF16", [128, TB], BF16)
    colA = mk(es0, "colA", [128, 8], F32)
    parsb = mk(es0, "parsb", [1, 1], I32)

    pb = [es0.enter_context(nc.psum_tensor(f"pb{i}", [128, 512], F32))
          for i in range(7)]
    pbT = es0.enter_context(nc.psum_tensor("pbT", [128, 1024], BF16))

    sem_names = ["c_wm", "c_wm2", "c_x", "c_aw", "c_ff", "s_ld", "s_l2",
                 "s_wt", "s_pe", "s_dv", "s_ac", "s_gp", "s_st", "s_h1",
                 "s_qw", "s_tq", "s_tq2", "s_ko", "s_fb", "s_db", "s_x0",
                 "s_x1", "s_qt", "s_l3", "s_l4", "s_gv0", "s_gv1", "s_st2",
                 "s_wa", "s_wd", "s_wp", "s_tl0", "s_tl1", "s_wo0", "s_wo1"]
    sems = {n: es0.enter_context(nc.semaphore(n)) for n in sem_names}
    cnt = {n: 0 for n in sem_names}

    def inc(instr, name, n=1):
        cnt[name] += n
        instr.then_inc(sems[name], n)
        return instr

    def tap(eng, name, src_ap, rows, cols):
        if name not in dbg:
            return
        inc(eng.dma_start(dbg[name][0:rows, 0:cols], src_ap), "s_db", 16)
        n_dbg_stores[0] += 1

    def col_of(t1row, n):
        return bass.AP(t1row.tensor, 0, [[1, 128], [128, n]])

    pregs = {}
    _bias_ld = [0]

    def magic_round(v_, src_, dst, t1, t2, t3):
        v_.tensor_scalar(t1, src_, C1, -C1, op0=OP.add, op1=OP.add)
        v_.tensor_scalar(t2, src_, C2, -C2, op0=OP.add, op1=OP.add)
        v_.tensor_scalar(t3, src_, C3, -C3, op0=OP.add, op1=OP.add)
        v_.drain()
        v_.tensor_tensor(t1, t1, t2, op=OP.add)
        v_.drain()
        v_.tensor_tensor(dst, t1, t3, op=OP.subtract)

    def parity_reg(e):
        en = e.engine
        if en not in pregs:
            e.wait_ge(sems["s_ld"], _bias_ld[0])
            pregs[en] = e.alloc_register(f"preg{len(pregs)}")
            e.reg_load(pregs[en], parsb[:1, :1])
        return pregs[en]

    esKV = contextlib.ExitStack()
    Kres = mk(esKV, "Kres", [128, HKV * S], BF16)
    Vres = mk(esKV, "Vres", [128, HKV * S], BF16)

    esP0 = contextlib.ExitStack()
    strip = mk(esP0, "strip", [1, 2048], F32)
    trigb = mk(esP0, "trigb", [128, 2 * 2048], BF16)
    colB = mk(esP0, "colB", [128, 64], F32)
    wmodbuf = mk(esP0, "wmodbuf", [128, 2 * 2048], BF16)
    Tg1 = mk(esP0, "Tg1", [128, 2048], F32)
    Tg2 = mk(esP0, "Tg2", [128, 2048], F32)
    Tg3 = mk(esP0, "Tg3", [128, 2048], F32)
    Tg4 = mk(esP0, "Tg4", [128, 2048], F32)
    Tg5 = mk(esP0, "Tg5", [128, 2048], F32)
    ccol = mk(esP0, "ccol", [128, EC], F32)

    MSUP = 2048 if E6 % 2048 == 0 else 1024
    n_msup = E6 // MSUP
    n_msub = MSUP // 512

    esB = contextlib.ExitStack()
    blk = esB.enter_context(nc.Block("p0", no_gpsimd_drain=True))

    @blk.gpsimd
    def _(g):
        g.iota(tA[:, :128], [[-1, 128]], base=0, channel_multiplier=1,
               allow_small_or_imprecise_dtypes=True)
        inc(g.iota(colA[:, 0:1], [[0, 1]], base=0, channel_multiplier=1,
                   allow_small_or_imprecise_dtypes=True), "s_gp")
    iota_gp = cnt["s_gp"]

    @blk.sync
    def _(s):
        inc(s.dma_start(parsb[:, :], par_in), "s_ld", 16)

    @blk.vector
    def _(v):
        v.wait_ge(sems["s_gp"], iota_gp)
        v.memset(onesb[:, :], 1.0)
        v.memset(onesf[:, :], 1.0)
        v.memset(qmax2[:, :], 0.0)
        v.memset(kmax2[:, :], 0.0)
        v.tensor_single_scalar(identb[:, :], tA[:, :128], 0.0, op=OP.is_equal)
        v.tensor_single_scalar(tB[:, :128], tA[:, :128], 64.0, op=OP.is_equal)
        v.tensor_single_scalar(tC[:, :128], tA[:, :128], -64.0, op=OP.is_equal)
        v.tensor_scalar(colA[:, 1:2], colA[:, 0:1], 64.0, None, op0=OP.is_ge)
        v.drain()
        v.tensor_tensor(swapb[:, :], tB[:, :128], tC[:, :128], op=OP.add)
        v.scalar_tensor_tensor(colA[:, 2:3], colA[:, 1:2], -64.0, colA[:, 0:1],
                               op0=OP.mult, op1=OP.add)
        v.tensor_scalar(colA[:, 3:4], colA[:, 1:2], 2.0, -1.0,
                        op0=OP.mult, op1=OP.add)
        v.drain()
        inc(v.nop(), "s_dv")
    const_dv = cnt["s_dv"]

    @blk.sync
    def _(s):
        A_ = dict(allow_slow_non_contiguous=True)
        inc(s.dma_start(biasq[:, :], col_of(bq_in, H), **A_), "s_ld", 16)
        inc(s.dma_start(biask[:, :], col_of(bk_in, HKV), **A_), "s_ld", 16)
        inc(s.dma_start(biasg[:, :], col_of(gb_in, HC), **A_), "s_ld", 16)
        inc(s.dma_start(biasvb[:, :], col_of(vb_in, HC), **A_), "s_ld", 16)
        inc(s.dma_start(biaso[:, :], col_of(ob_in, EC), **A_), "s_ld", 16)
        inc(s.dma_start(ccol[:, :], col_of(c_in, EC), **A_), "s_ld", 16)
        inc(s.dma_start(strip[:1, :KVD], bv_in), "s_ld", 16)
    bias_ld = cnt["s_ld"]
    _bias_ld[0] = bias_ld

    @blk.scalar
    def _(sc):
        sc.wait_ge(sems["s_ld"], bias_ld)
        sc.activation(scb[:, :], ccol[:, :], AF.Silu)
        sc.wait_ge(sems["s_dv"], const_dv)
        inc(sc.activation(colA[:, 4:5], colA[:, 2:3], AF.Exp,
                          scale=-math.log(cfg.theta) / (DH // 2)), "s_ac")
    silu_ac = cnt["s_ac"]

    @blk.vector
    def _(v):
        v.wait_ge(sems["s_ld"], bias_ld)
        v.memset(tE16[:, :KVD], 0.0)
        v.drain()
        inc(v.tensor_copy(tE16[:1, :KVD], strip[:1, :KVD]), "s_dv")
    bv_dv = cnt["s_dv"]

    @blk.tensor
    def _(te):
        te.wait_ge(sems["s_dv"], bv_dv)
        inc(te.matmul(pb[6][:, :KVD], onesb[:, :], tE16[:, :KVD],
                      start=True, stop=True), "s_pe")
    bv_pe = cnt["s_pe"]

    @blk.vector
    def _(v):
        v.wait_ge(sems["s_pe"], bv_pe)
        inc(v.tensor_copy(bvbank[:, :], pb[6][:, :KVD]), "s_dv")

    mm_pe0 = cnt["s_pe"]
    n_wm_jobs = n_msup * EC

    side_jobs = []
    for b0 in range(NB):
        side_jobs.append(("c_x", x16[b0 * TB:(b0 + 1) * TB, :],
                          x_in[b0 * TB:(b0 + 1) * TB, :]))
    side_jobs.append(("c_aw", wk16, wk_in))
    side_jobs.append(("c_aw", wv16, wv_in))
    for i in range(4):
        side_jobs.append(("c_aw", wq16[i * (E // 4):(i + 1) * (E // 4), :],
                          wq_in[i * (E // 4):(i + 1) * (E // 4), :]))
    for i in range(4):
        side_jobs.append(("c_aw", wo16[i * (E // 4):(i + 1) * (E // 4), :],
                          wo_in[i * (E // 4):(i + 1) * (E // 4), :]))
    NFF = 8
    for w_in_, w16_, R in ((gw_in, gw16, HID), (vw_in, vw16, HID),
                           (ow_in, ow16, E)):
        for i in range(NFF):
            side_jobs.append(("c_ff",
                              w16_[i * (R // NFF):(i + 1) * (R // NFF), :],
                              w_in_[i * (R // NFF):(i + 1) * (R // NFF), :]))

    @blk.gpsimd
    def _(g):
        si = 0
        for j in range(n_wm_jobs):
            ms, c = divmod(j, EC)
            if j >= 2:
                g.wait_ge(sems["s_pe"], mm_pe0 + (j - 1))
            buf = (j % 2) * MSUP
            inc(g.dma_start(wmodbuf[:, buf:buf + MSUP],
                            wmod_in[c * 128:(c + 1) * 128,
                                    ms * MSUP:(ms + 1) * MSUP]),
                "c_wm" if j % 2 == 0 else "c_wm2", 16)
            while si * n_wm_jobs < (j + 1) * len(side_jobs):
                sj = side_jobs[si]
                inc(g.dma_start(sj[1], sj[2]), sj[0], 16)
                si += 1
        while si < len(side_jobs):
            sj = side_jobs[si]
            inc(g.dma_start(sj[1], sj[2]), sj[0], 16)
            si += 1
    cx_total = cnt["c_x"]
    caw_total = cnt["c_aw"]
    cff_total = cnt["c_ff"]
    caw_wkv = caw_total
    caw_wq = caw_total
    caw_wo = caw_total

    for ms in range(n_msup):
        @blk.tensor
        def _(te, ms=ms):
            te.wait_ge(sems["s_ac"], silu_ac)
            if ms > 0:
                te.wait_ge(sems["s_dv"], cnt["s_dv"])
            for c in range(EC):
                j = ms * EC + c
                te.wait_ge(sems["c_wm" if j % 2 == 0 else "c_wm2"],
                           16 * (j // 2 + 1))
                buf = (j % 2) * MSUP
                for msub in range(n_msub):
                    m_ = te.matmul(
                        pb[msub][:1, :], scb[:, c:c + 1],
                        wmodbuf[:, buf + msub * 512:buf + (msub + 1) * 512],
                        start=(c == 0), stop=(c == EC - 1))
                if True:
                    inc(m_, "s_pe")

        @blk.vector
        def _(v, ms=ms):
            v.wait_ge(sems["s_pe"], cnt["s_pe"])
            if ms > 0:
                v.wait_ge(sems["s_st"], 16 * ms)
            for msub in range(n_msub):
                v.tensor_copy(strip[:, msub * 512:(msub + 1) * 512],
                              pb[msub][:1, :])
            v.drain()
            inc(v.nop(), "s_dv")

        @blk.scalar
        def _(sc, ms=ms):
            sc.wait_ge(sems["s_dv"], cnt["s_dv"])
            inc(sc.dma_start(modscr[:, ms * MSUP:(ms + 1) * MSUP],
                             strip[:, :MSUP]), "s_st", 16)

    @blk.sync
    def _(s):
        s.wait_ge(sems["s_st"], 16 * n_msup)
        inc(s.dma_start(modc[:, :], col_of(modscr, 6 * EC),
                        allow_slow_non_contiguous=True), "s_ld", 16)
    modc_ld = cnt["s_ld"]

    @blk.scalar
    def _(sc):
        sc.wait_ge(sems["s_ld"], modc_ld)
        sc.activation(tB[:, 0:EC], modc[:, 2 * EC:3 * EC], AF.Sign)
        sc.activation(tB[:, EC:2 * EC], modc[:, 5 * EC:6 * EC], AF.Sign)
        sc.activation(tD[:, 0:EC], modc[:, 2 * EC:3 * EC], AF.Abs)
        inc(sc.activation(tD[:, EC:2 * EC], modc[:, 5 * EC:6 * EC], AF.Abs),
            "s_ac")
    sign_ac = cnt["s_ac"]

    @blk.vector
    def _(v):
        v.wait_ge(sems["s_ac"], sign_ac)
        v.tensor_scalar(tC[:, 0:EC], tD[:, 0:EC], 0.25, None, op0=OP.max)
        v.tensor_scalar(tC[:, EC:2 * EC], tD[:, EC:2 * EC], 0.25, None,
                        op0=OP.max)
        v.drain()
        v.tensor_tensor(modc[:, 2 * EC:3 * EC], tB[:, 0:EC], tC[:, 0:EC],
                        op=OP.mult)
        v.tensor_tensor(modc[:, 5 * EC:6 * EC], tB[:, EC:2 * EC],
                        tC[:, EC:2 * EC], op=OP.mult)
        v.drain()
        v.tensor_scalar(tC[:, 0:EC], modc[:, EC:2 * EC], 1.0, None, op0=OP.add)
        v.tensor_scalar(tC[:, EC:2 * EC], modc[:, 4 * EC:5 * EC], 1.0, None,
                        op0=OP.add)
        v.drain()
        v.tensor_copy(modc[:, EC:2 * EC], tC[:, 0:EC])
        v.tensor_copy(modc[:, 4 * EC:5 * EC], tC[:, EC:2 * EC])
        v.drain()
        inc(v.nop(), "s_dv")
    mod_done_dv = cnt["s_dv"]

    def cody3(v, k_ap, ang_ap, o1, o2, o3):
        v.scalar_tensor_tensor(o1, k_ap, -TP1, ang_ap, op0=OP.mult, op1=OP.add)
        v.drain()
        v.scalar_tensor_tensor(o2, k_ap, -TP2, o1, op0=OP.mult, op1=OP.add)
        v.drain()
        v.scalar_tensor_tensor(o3, k_ap, -TP3, o2, op0=OP.mult, op1=OP.add)
        v.drain()

    def ladder_round(v, a_ap, u_ap, k_ap, bit_ap, tmp_ap, nbits):
        v.tensor_copy(u_ap, a_ap)
        v.memset(k_ap, 0.0)
        v.drain()
        for j in range(nbits - 1, -1, -1):
            Bj = float(np.float32((1 << j) * 2 * math.pi))
            v.tensor_scalar(bit_ap, u_ap, Bj, None, op0=OP.is_ge)
            v.drain()
            v.scalar_tensor_tensor(tmp_ap, bit_ap, -Bj, u_ap,
                                   op0=OP.mult, op1=OP.add)
            v.drain()
            v.tensor_copy(u_ap, tmp_ap)
            v.drain()
            v.scalar_tensor_tensor(tmp_ap, bit_ap, float(1 << j), k_ap,
                                   op0=OP.mult, op1=OP.add)
            v.drain()
            v.tensor_copy(k_ap, tmp_ap)
            v.drain()
        v.tensor_scalar(bit_ap, u_ap, PI_F, None, op0=OP.is_ge)
        v.drain()
        v.tensor_tensor(tmp_ap, k_ap, bit_ap, op=OP.add)
        v.drain()
        v.tensor_copy(k_ap, tmp_ap)
        v.drain()

    def wrap_cos3(v, m_ap, c_ap, t1_ap, out_ap):
        v.tensor_scalar(t1_ap, m_ap, HALF_PI, None, op0=OP.add)
        v.tensor_scalar(c_ap, m_ap, HALF_PI, None, op0=OP.is_gt)
        v.drain()
        v.scalar_tensor_tensor(out_ap, c_ap, -TP1, t1_ap,
                               op0=OP.mult, op1=OP.add)
        v.drain()
        v.scalar_tensor_tensor(t1_ap, c_ap, -TP2, out_ap,
                               op0=OP.mult, op1=OP.add)
        v.drain()
        v.scalar_tensor_tensor(out_ap, c_ap, -TP3, t1_ap,
                               op0=OP.mult, op1=OP.add)
        v.drain()
        v.tensor_scalar(out_ap, out_ap, PI_F, -PI_F, op0=OP.min, op1=OP.max)
        v.drain()

    @blk.gpsimd
    def _(g, w=cnt["s_dv"]):
        g.wait_ge(sems["s_dv"], w)
        inc(g.iota(Tg1[:, :TB], [[1, TB]], base=0, channel_multiplier=0,
                   allow_small_or_imprecise_dtypes=True), "s_gp")

    @blk.vector
    def _(v, w=cnt["s_gp"]):
        v.wait_ge(sems["s_gp"], w)
        v.wait_ge(sems["s_ac"], silu_ac)
        v.tensor_scalar(Tg3[:, :TB], Tg1[:, :TB], colA[:, 4:5], None,
                        op0=OP.mult)
        v.drain()
        ladder_round(v, Tg3[:, :TB], Tg1[:, :TB], Tg4[:, :TB],
                     Tg2[:, :TB], Tg5[:, :TB], 7)
        cody3(v, Tg4[:, :TB], Tg3[:, :TB], Tg1[:, :TB], Tg2[:, :TB],
              Tg5[:, :TB])
        v.tensor_scalar(Tg4[:, :TB], Tg5[:, :TB], PI_F, -PI_F,
                        op0=OP.min, op1=OP.max)
        v.drain()
        wrap_cos3(v, Tg4[:, :TB], Tg3[:, :TB], Tg5[:, :TB], Tg1[:, :TB])
        inc(v.nop(), "s_dv")

    @blk.scalar
    def _(sc, w=cnt["s_dv"]):
        sc.wait_ge(sems["s_dv"], w)
        sc.activation(Tg2[:, :TB], Tg4[:, :TB], AF.Sin)
        inc(sc.activation(Tg5[:, :TB], Tg1[:, :TB], AF.Sin), "s_ac")
    lo_ac = cnt["s_ac"]

    n_tch = S // 2048
    n_tb = 2048 // TB
    CB = lambda i: colB[:, i * n_tb:(i + 1) * n_tb]
    for ch in range(n_tch):
        @blk.gpsimd
        def _(g, ch=ch, w=cnt["s_st"], w2=cnt["s_ac"]):
            g.wait_ge(sems["s_st"], w)
            g.wait_ge(sems["s_ac"], w2)
            inc(g.iota(CB(0), [[512, n_tb]], base=2048 * ch,
                       channel_multiplier=0,
                       allow_small_or_imprecise_dtypes=True), "s_gp")

        @blk.vector
        def _(v, ch=ch, w=cnt["s_gp"], wl=lo_ac):
            v.wait_ge(sems["s_gp"], w)
            v.wait_ge(sems["s_ac"], wl)
            v.tensor_scalar(CB(4), CB(0), colA[:, 4:5], None, op0=OP.mult)
            v.drain()
            v.tensor_copy(CB(0), CB(4))
            v.drain()
            ladder_round(v, CB(0), CB(1), CB(2), CB(3), CB(4), 10)
            cody3(v, CB(2), CB(0), CB(1), CB(3), CB(5))
            v.tensor_scalar(CB(5), CB(5), PI_F, -PI_F, op0=OP.min, op1=OP.max)
            v.drain()
            wrap_cos3(v, CB(5), CB(3), CB(4), CB(11))
            inc(v.nop(), "s_dv")

        @blk.scalar
        def _(sc, w=cnt["s_dv"]):
            sc.wait_ge(sems["s_dv"], w)
            sc.activation(CB(6), CB(5), AF.Sin)
            inc(sc.activation(CB(7), CB(11), AF.Sin), "s_ac")

        @blk.vector
        def _(v, ch=ch, w=cnt["s_ac"]):
            v.wait_ge(sems["s_ac"], w)
            v.tensor_scalar(CB(8), CB(6), -1.0, None, op0=OP.mult)
            v.tensor_scalar(CB(9), CB(6), colA[:, 3:4], None, op0=OP.mult)
            v.tensor_scalar(CB(10), CB(7), colA[:, 3:4], None, op0=OP.mult)
            v.drain()
            for j in range(n_tb):
                sl_c = slice(j * TB, (j + 1) * TB)
                v.tensor_scalar(Tg3[:, :TB], Tg5[:, :TB],
                                colB[:, 7 * n_tb + j:7 * n_tb + j + 1],
                                None, op0=OP.mult)
                v.drain()
                v.scalar_tensor_tensor(
                    trigb[:, j * TB:(j + 1) * TB], Tg2[:, :TB],
                    colB[:, 8 * n_tb + j:8 * n_tb + j + 1], Tg3[:, :TB],
                    op0=OP.mult, op1=OP.add)
                v.tensor_scalar(Tg4[:, :TB], Tg5[:, :TB],
                                colB[:, 9 * n_tb + j:9 * n_tb + j + 1],
                                None, op0=OP.mult)
                v.drain()
                v.scalar_tensor_tensor(
                    trigb[:, 2048 + j * TB:2048 + (j + 1) * TB], Tg2[:, :TB],
                    colB[:, 10 * n_tb + j:10 * n_tb + j + 1], Tg4[:, :TB],
                    op0=OP.mult, op1=OP.add)
                v.drain()
            inc(v.nop(), "s_dv")

        @blk.scalar
        def _(sc, ch=ch, w=cnt["s_dv"]):
            sc.wait_ge(sems["s_dv"], w)
            inc(sc.dma_start(trigscr[:, ch * 2048:(ch + 1) * 2048],
                             trigb[:, 0:2048]), "s_st", 16)
            inc(sc.dma_start(trigscr[:, S + ch * 2048:S + (ch + 1) * 2048],
                             trigb[:, 2048:4096]), "s_st", 16)
    trig_st = cnt["s_st"]

    if "modc" in dbg:
        @blk.scalar
        def _(sc):
            sc.wait_ge(sems["s_dv"], mod_done_dv)
            tap(sc, "modc", modc[:, :], 128, 6 * EC)


    esB.close()
    esP0.close()

    esP1 = contextlib.ExitStack()
    xt = mk(esP1, "xt", [128, 2 * EC * TB], BF16)
    h1t = mk(esP1, "h1t", [128, EC * TB], BF16)
    wkv = mk(esP1, "wkv", [128, EC * 2 * KVD], BF16)
    ktrig = mk(esP1, "ktrig", [128, 2 * TB], BF16)

    esB = contextlib.ExitStack()
    blk = esB.enter_context(nc.Block("p1a", no_gpsimd_drain=True))

    my_par = {p: cfg.my_blocks(p) for p in (0, 1)}
    h1_stores = {0: 0, 1: 0}
    dv_end = {}
    pe_end = {}

    for m in range(NB):
        bo = (m % 2) * EC * TB

        @blk.sync
        def _(s, m=m, bo=bo):
            s.wait_ge(sems["c_x"], cx_total)
            if m >= 2:
                s.wait_ge(sems["s_dv"], dv_end[m - 2])
            if m == 0:
                s.wait_ge(sems["s_st"], trig_st)
                s.wait_ge(sems["c_aw"], caw_wkv)
                for c in range(EC):
                    inc(s.dma_start_transpose(
                        wkv[:, c * 2 * KVD:c * 2 * KVD + KVD],
                        wk16[0:KVD, c * 128:(c + 1) * 128]), "s_wt", 16)
                    inc(s.dma_start_transpose(
                        wkv[:, c * 2 * KVD + KVD:(c + 1) * 2 * KVD],
                        wv16[0:KVD, c * 128:(c + 1) * 128]), "s_wt", 16)
            for c in range(EC):
                inc(s.dma_start_transpose(
                    xt[:, bo + c * TB:bo + (c + 1) * TB],
                    x16[m * TB:(m + 1) * TB, c * 128:(c + 1) * 128]),
                    "s_x0" if m % 2 == 0 else "s_x1", 16)
            if m > 0:
                s.wait_ge(sems["s_dv"], dv_end[m - 1])
            inc(s.dma_start(ktrig[:, 0:TB],
                            trigscr[:, m * TB:(m + 1) * TB]), "s_wt", 16)
            inc(s.dma_start(ktrig[:, TB:2 * TB],
                            trigscr[:, S + m * TB:S + (m + 1) * TB]),
                "s_wt", 16)
        xsem = "s_x0" if m % 2 == 0 else "s_x1"
        xld = cnt[xsem]
        ktrig_wt = cnt["s_wt"]

        @blk.scalar
        def _(sc, m=m, bo=bo, xld=xld, xsem=xsem):
            sc.wait_ge(sems[xsem], xld)
            if m > 0:
                sc.wait_ge(sems["s_pe"], pe_end[m - 1])
                reg = parity_reg(sc)
                with sc.If_eq(reg, 0):
                    sc.wait_ge(sems["s_h1"], 16 * h1_stores[0])
                with sc.Else():
                    sc.wait_ge(sems["s_h1"], 16 * h1_stores[1])
            for c in range(EC):
                a = sc.activation(h1t[:, c * TB:(c + 1) * TB],
                                  xt[:, bo + c * TB:bo + (c + 1) * TB],
                                  AF.Square)
            inc(a, "s_ac")
        sq_ac = cnt["s_ac"]

        @blk.tensor
        def _(te, m=m, bo=bo, sq_ac=sq_ac):
            te.wait_ge(sems["s_ac"], sq_ac)
            if m > 0:
                te.wait_ge(sems["s_dv"], dv_end[m - 1])
            for c in range(EC):
                mm = te.matmul(pb[0][:, :], onesb[:, :],
                               xt[:, bo + c * TB:bo + (c + 1) * TB],
                               start=(c == 0), stop=(c == EC - 1))
            inc(mm, "s_pe")
            for c in range(EC):
                mm = te.matmul(pb[1][:, :], onesb[:, :],
                               h1t[:, c * TB:(c + 1) * TB],
                               start=(c == 0), stop=(c == EC - 1))
            inc(mm, "s_pe")
        sums_pe = cnt["s_pe"]

        @blk.vector
        def _(v, sums_pe=sums_pe):
            v.wait_ge(sems["s_pe"], sums_pe)
            v.tensor_scalar(tA[:, :], pb[0][:, :], 1.0 / E, None, op0=OP.mult)
            v.drain()
            v.tensor_tensor(tB[:, :], tA[:, :], tA[:, :], op=OP.mult)
            v.drain()
            v.scalar_tensor_tensor(tC[:, :], pb[1][:, :], 1.0 / E, tB[:, :],
                                   op0=OP.mult, op1=OP.subtract)
            v.drain()
            inc(v.tensor_scalar(tB[:, :], tC[:, :], cfg.eps, None, op0=OP.add),
                "s_dv")
        var_dv = cnt["s_dv"]

        @blk.scalar
        def _(sc, var_dv=var_dv):
            sc.wait_ge(sems["s_dv"], var_dv)
            inc(sc.activation(tC[:, :], tB[:, :], AF.Sqrt), "s_ac")
        sqrt_ac = cnt["s_ac"]

        @blk.vector
        def _(v, m=m, bo=bo, sqrt_ac=sqrt_ac):
            v.wait_ge(sems["s_ac"], sqrt_ac)
            v.reciprocal(tB[:, :], tC[:, :])
            v.drain()
            for c in range(EC):
                v.tensor_tensor(tC[:, :], xt[:, bo + c * TB:bo + (c + 1) * TB],
                                tA[:, :], op=OP.subtract)
                v.drain()
                v.tensor_tensor(tD[:, :], tC[:, :], tB[:, :], op=OP.mult)
                v.drain()
                v.tensor_scalar(h1t[:, c * TB:(c + 1) * TB], tD[:, :],
                                modc[:, EC + c:EC + c + 1],
                                modc[:, c:c + 1], op0=OP.mult, op1=OP.add)
                v.drain()
            inc(v.nop(), "s_dv")
        h1_dv = cnt["s_dv"]

        @blk.vector
        def _(v, m=m, bo=bo):
            reg = parity_reg(v)
            for p in (0, 1):
                ctxm = v.If_eq(reg, 0) if p == 0 else v.Else()
                with ctxm:
                    if m in my_par[p]:
                        gslot = my_par[p].index(m)
                        for c in range(EC):
                            v.tensor_copy(
                                xres[:, c * MYT + gslot * TB:
                                     c * MYT + (gslot + 1) * TB],
                                xt[:, bo + c * TB:bo + (c + 1) * TB])
                        v.drain()
            inc(v.nop(), "s_dv")

        @blk.scalar
        def _(sc, m=m, h1_dv=h1_dv):
            sc.wait_ge(sems["s_dv"], h1_dv)
            reg = parity_reg(sc)
            for p in (0, 1):
                ctxm = sc.If_eq(reg, 0) if p == 0 else sc.Else()
                with ctxm:
                    if m in my_par[p]:
                        gslot = my_par[p].index(m)
                        sc.dma_start(
                            h1scr[:, gslot * EC * TB:(gslot + 1) * EC * TB],
                            h1t[:, :]).then_inc(sems["s_h1"], 16)
                        h1_stores[p] += 1

        k_chain_pe = {}
        v_chain_pe = {}

        @blk.tensor
        def _(te, m=m, h1_dv=h1_dv, ktw=ktrig_wt):
            te.wait_ge(sems["s_dv"], h1_dv)
            if m == 0:
                te.wait_ge(sems["s_wt"], ktw)
            for kv in range(HKV):
                for c in range(EC):
                    mm = te.matmul(
                        pb[kv][:, :],
                        wkv[:, c * 2 * KVD + kv * DH:c * 2 * KVD + (kv + 1) * DH],
                        h1t[:, c * TB:(c + 1) * TB],
                        start=(c == 0), stop=(c == EC - 1))
                inc(mm, "s_pe")
                k_chain_pe[kv] = cnt["s_pe"]
            for tc in range(2):
                for c in range(EC):
                    mm = te.matmul(
                        pb[2 + tc][:, :KVD],
                        h1t[:, c * TB + tc * 128:c * TB + (tc + 1) * 128],
                        wkv[:, c * 2 * KVD + KVD:(c + 1) * 2 * KVD],
                        start=(c == 0), stop=(c == EC - 1))
                inc(mm, "s_pe")
                v_chain_pe[tc] = cnt["s_pe"]

        def emit_krope(kv):
            @blk.vector
            def _(v, m=m, kv=kv, w=cnt["s_pe"]):
                v.wait_ge(sems["s_pe"], w)
                inc(v.tensor_scalar(tE16[:, :], pb[kv][:, :],
                                    biask[:, kv:kv + 1], None, op0=OP.add),
                    "s_dv")
            ksb_dv = cnt["s_dv"]

            @blk.tensor
            def _(te, ksb_dv=ksb_dv):
                te.wait_ge(sems["s_dv"], ksb_dv)
                inc(te.matmul(pb[4][:, :], swapb[:, :], tE16[:, :],
                              start=True, stop=True), "s_pe")
            swp_pe = cnt["s_pe"]

            @blk.vector
            def _(v, m=m, kv=kv, swp_pe=swp_pe, ktw=ktrig_wt):
                v.wait_ge(sems["s_wt"], ktw)
                v.tensor_tensor(tA[:, :], tE16[:, :],
                                ktrig[:, 0:TB], op=OP.mult)
                v.wait_ge(sems["s_pe"], swp_pe)
                v.tensor_tensor(tC[:, :], pb[4][:, :],
                                ktrig[:, TB:2 * TB], op=OP.mult)
                v.drain()
                inc(v.tensor_tensor(
                    Kres[:, kv * S + m * TB:kv * S + (m + 1) * TB],
                    tA[:, :], tC[:, :], op=OP.add), "s_dv")
            krope_dv = cnt["s_dv"]

            @blk.scalar
            def _(sc, m=m, kv=kv, krope_dv=krope_dv):
                sc.wait_ge(sems["s_dv"], krope_dv)
                inc(sc.activation(
                    tF16[:, :],
                    Kres[:, kv * S + m * TB:kv * S + (m + 1) * TB],
                    AF.Square), "s_ac")
            ksq_ac = cnt["s_ac"]

            @blk.tensor
            def _(te, ksq_ac=ksq_ac):
                te.wait_ge(sems["s_ac"], ksq_ac)
                inc(te.matmul(pb[5][:, :], onesb[:, :], tF16[:, :],
                              start=True, stop=True), "s_pe")
            ksum_pe = cnt["s_pe"]

            @blk.vector
            def _(v, kv=kv, ksum_pe=ksum_pe):
                v.wait_ge(sems["s_pe"], ksum_pe)
                v.tensor_reduce(colA[:, 5:6], pb[5][:, :], axis=AX.X,
                                op=OP.max)
                v.drain()
                v.tensor_tensor(colA[:, 6:7], kmax2[:, kv:kv + 1],
                                colA[:, 5:6], op=OP.max)
                v.drain()
                v.tensor_copy(kmax2[:, kv:kv + 1], colA[:, 6:7])
                v.drain()
                inc(v.nop(), "s_dv")

        def emit_vevac(tc):
            @blk.vector
            def _(v, m=m, tc=tc, vpe=cnt["s_pe"]):
                v.wait_ge(sems["s_pe"], vpe)
                v.drain()
                kt = 4 * m + tc
                v.tensor_tensor(tF16[:, :KVD], pb[2 + tc % 2][:, :KVD],
                                bvbank[:, :], op=OP.add)
                v.drain()
                v.tensor_copy(Vres[:, kt * 128:(kt + 1) * 128],
                              tF16[:, 0:DH])
                inc(v.tensor_copy(Vres[:, S + kt * 128:S + (kt + 1) * 128],
                                  tF16[:, DH:KVD]), "s_dv")

        emit_krope(0)
        emit_vevac(0)
        emit_krope(1)
        emit_vevac(1)
        evac01_dv = cnt["s_dv"]

        @blk.tensor
        def _(te, m=m, evac01_dv=evac01_dv):
            te.wait_ge(sems["s_dv"], evac01_dv)
            for tc in range(2, 4):
                for c in range(EC):
                    mm = te.matmul(
                        pb[2 + tc % 2][:, :KVD],
                        h1t[:, c * TB + tc * 128:c * TB + (tc + 1) * 128],
                        wkv[:, c * 2 * KVD + KVD:(c + 1) * 2 * KVD],
                        start=(c == 0), stop=(c == EC - 1))
                inc(mm, "s_pe")
                v_chain_pe[tc] = cnt["s_pe"]
        pe_end[m] = cnt["s_pe"]

        emit_vevac(2)
        emit_vevac(3)
        dv_end[m] = cnt["s_dv"]

    esB.close()

    if "h1scr" in dbg or "Kres" in dbg:
        esB = contextlib.ExitStack()
        blk = esB.enter_context(nc.Block("p1dbg", no_gpsimd_drain=True))

        @blk.scalar
        def _(sc):
            tap(sc, "Kres", Kres[:, :], 128, HKV * S)
            tap(sc, "Vres", Vres[:, :], 128, HKV * S)
            tap(sc, "xres", xres[:, :], 128, EC * MYT)
        esB.close()

    esP1.close()
    Qres = mk(esKV, "Qres", [128, H * MYT], BF16)
    esP1b = contextlib.ExitStack()
    h1tb = mk(esP1b, "h1tb", [128, EC * TB], BF16)
    wqb = mk(esP1b, "wqb", [128, 2 * TB], BF16)
    qtrig = mk(esP1b, "qtrig", [128, 2 * TB], BF16)

    esB = contextlib.ExitStack()
    blk = esB.enter_context(nc.Block("p1b", no_gpsimd_drain=True))

    for g in range(MYNB):
        @blk.sync
        def _(s, g=g, w=cnt["s_pe"], wdv=cnt["s_dv"]):
            s.wait_ge(sems["s_h1"], 16 * MYNB)
            if g > 0:
                s.wait_ge(sems["s_pe"], w)
                s.wait_ge(sems["s_dv"], wdv)
            inc(s.dma_start(h1tb[:, :],
                            h1scr[:, g * EC * TB:(g + 1) * EC * TB]),
                "s_ld", 16)
            r = parity_reg(s)
            for p in (0, 1):
                ctxm = s.If_eq(r, 0) if p == 0 else s.Else()
                mb = my_par[p][g]
                with ctxm:
                    s.dma_start(qtrig[:, 0:TB],
                                trigscr[:, mb * TB:(mb + 1) * TB]
                                ).then_inc(sems["s_qt"], 16)
                    s.dma_start(qtrig[:, TB:2 * TB],
                                trigscr[:, S + mb * TB:S + (mb + 1) * TB]
                                ).then_inc(sems["s_qt"], 16)
            cnt["s_qt"] += 32
        h1_ld = cnt["s_ld"]
        qtrig_wt = cnt["s_qt"]

        for qg in range(H // 4):
            pe_base = cnt["s_pe"]

            for d in range(EC):
                @blk.sync
                def _(s, g=g, qg=qg, d=d, pe_base=pe_base, w=cnt["s_dv"]):
                    if d == 0:
                        s.wait_ge(sems["c_aw"], caw_wq)
                        s.wait_ge(sems["s_dv"], w)
                    if d < 2:
                        s.wait_ge(sems["s_pe"], pe_base)
                    else:
                        s.wait_ge(sems["s_pe"], pe_base + d - 1)
                    inc(s.dma_start_transpose(
                        wqb[:, (d % 2) * TB:(d % 2 + 1) * TB],
                        wq16[qg * 512:(qg + 1) * 512, d * 128:(d + 1) * 128]),
                        "s_qw" if d % 2 == 0 else "s_wt", 16)

            qw_even = cnt["s_qw"]
            qw_odd = cnt["s_wt"]

            @blk.tensor
            def _(te, g=g, qg=qg, h1_ld=h1_ld, pe_base=pe_base,
                  qw_even=qw_even, qw_odd=qw_odd, w=cnt["s_dv"]):
                te.wait_ge(sems["s_ld"], h1_ld)
                te.wait_ge(sems["s_dv"], w)
                for d in range(EC):
                    if d % 2 == 0:
                        te.wait_ge(sems["s_qw"],
                                   qw_even - 16 * ((EC + 1) // 2 - d // 2 - 1))
                    else:
                        te.wait_ge(sems["s_wt"],
                                   qw_odd - 16 * (EC // 2 - d // 2 - 1))
                    for hh in range(4):
                        mm = te.matmul(
                            pb[hh][:, :],
                            wqb[:, (d % 2) * TB + hh * 128:
                                (d % 2) * TB + (hh + 1) * 128],
                            h1tb[:, d * TB:(d + 1) * TB],
                            start=(d == 0), stop=(d == EC - 1))
                    inc(mm, "s_pe")

            for hh in range(4):
                h = qg * 4 + hh
                kv = h // GQ
                qmm_pe = cnt["s_pe"]

                @blk.vector
                def _(v, h=h, hh=hh, qmm_pe=qmm_pe):
                    v.wait_ge(sems["s_pe"], qmm_pe)
                    inc(v.tensor_scalar(tE16[:, :], pb[hh][:, :],
                                        biasq[:, h:h + 1], None, op0=OP.add),
                        "s_dv")
                qsb_dv = cnt["s_dv"]

                @blk.tensor
                def _(te, qsb_dv=qsb_dv):
                    te.wait_ge(sems["s_dv"], qsb_dv)
                    inc(te.matmul(pb[4][:, :], swapb[:, :], tE16[:, :],
                                  start=True, stop=True), "s_pe")
                qswp_pe = cnt["s_pe"]

                @blk.vector
                def _(v, g=g, h=h, qswp_pe=qswp_pe, qtw=qtrig_wt):
                    v.wait_ge(sems["s_qt"], qtw)
                    v.tensor_tensor(tA[:, :], tE16[:, :], qtrig[:, 0:TB],
                                    op=OP.mult)
                    v.wait_ge(sems["s_pe"], qswp_pe)
                    v.tensor_tensor(tC[:, :], pb[4][:, :], qtrig[:, TB:2 * TB],
                                    op=OP.mult)
                    v.drain()
                    v.tensor_tensor(
                        Qres[:, h * MYT + g * TB:h * MYT + (g + 1) * TB],
                        tA[:, :], tC[:, :], op=OP.add)
                    v.drain()
                    inc(v.nop(), "s_dv")
                qrope_dv = cnt["s_dv"]

                @blk.scalar
                def _(sc, g=g, h=h, qrope_dv=qrope_dv):
                    sc.wait_ge(sems["s_dv"], qrope_dv)
                    inc(sc.activation(
                        tF16[:, :],
                        Qres[:, h * MYT + g * TB:h * MYT + (g + 1) * TB],
                        AF.Square), "s_ac")
                qsq_ac = cnt["s_ac"]

                @blk.tensor
                def _(te, qsq_ac=qsq_ac):
                    te.wait_ge(sems["s_ac"], qsq_ac)
                    inc(te.matmul(pb[5][:, :], onesb[:, :], tF16[:, :],
                                  start=True, stop=True), "s_pe")
                qsum_pe = cnt["s_pe"]

                @blk.vector
                def _(v, h=h, qsum_pe=qsum_pe):
                    v.wait_ge(sems["s_pe"], qsum_pe)
                    v.tensor_reduce(colA[:, 5:6], pb[5][:, :], axis=AX.X,
                                    op=OP.max)
                    v.drain()
                    v.tensor_tensor(colA[:, 6:7], qmax2[:, h:h + 1],
                                    colA[:, 5:6], op=OP.max)
                    v.drain()
                    v.tensor_copy(qmax2[:, h:h + 1], colA[:, 6:7])
                    v.drain()
                    inc(v.nop(), "s_dv")

    @blk.vector
    def _(v, w=cnt["s_dv"]):
        v.wait_ge(sems["s_dv"], w)
        for h in range(H):
            v.tensor_tensor(tB[:, h:h + 1], qmax2[:, h:h + 1],
                            kmax2[:, h // GQ:h // GQ + 1], op=OP.mult)
        v.drain()
        inc(v.nop(), "s_dv")
    nb_dv = cnt["s_dv"]

    @blk.scalar
    def _(sc, nb_dv=nb_dv):
        sc.wait_ge(sems["s_dv"], nb_dv)
        inc(sc.activation(tC[:, 0:H], tB[:, 0:H], AF.Sqrt, scale=1.0 / 128.0),
            "s_ac")
    nb_ac = cnt["s_ac"]

    @blk.vector
    def _(v, nb_ac=nb_ac):
        v.wait_ge(sems["s_ac"], nb_ac)
        v.tensor_scalar(negB[:, :], tC[:, 0:H], -1.0, None, op0=OP.mult)
        v.drain()
        inc(v.nop(), "s_dv")

    if "Qres" in dbg:
        @blk.scalar
        def _(sc, w=cnt["s_dv"]):
            sc.wait_ge(sems["s_dv"], w)
            tap(sc, "Qres", Qres[:, :], 128, H * MYT)
            tap(sc, "negB", negB[:, :], 128, H)

    esB.close()
    esP1b.close()

    esP2w = contextlib.ExitStack()
    PBW = min(512, E)
    PCW = 512
    wtb = mk(esP2w, "wtb", [128, 2 * PBW], F16)
    wtc = mk(esP2w, "wtc", [128, 2 * PCW], F16)
    tqc = mk(esP2w, "tqc", [128, 2 * PCW], BF16)
    ttmp = mk(esP2w, "ttmp", [128, PCW], BF16)

    esP2 = contextlib.ExitStack()
    wob = mk(esP2, "wob", [128, 2048], BF16)
    attst = mk(esP2, "attst", [128, H * TB], BF16)
    ptile = mk(esP2, "ptile", [128, 2 * TB], BF16)
    rsum = tC
    masks = mk(esP2, "masks", [128, 4 * TB], BF16)

    esB = contextlib.ExitStack()
    blk = esB.enter_context(nc.Block("p2a", no_gpsimd_drain=True))

    for r in range(4):
        @blk.gpsimd
        def _(g, r=r, w=cnt["s_dv"]):
            g.wait_ge(sems["s_dv"], w)
            inc(g.iota(tC[:, :TB], [[1, TB]], base=-128 * r,
                       channel_multiplier=-1,
                       allow_small_or_imprecise_dtypes=True), "s_gp")

        @blk.vector
        def _(v, r=r, w=cnt["s_gp"]):
            v.wait_ge(sems["s_gp"], w)
            v.drain()
            inc(v.tensor_scalar(masks[:, r * TB:(r + 1) * TB], tC[:, :TB],
                                0.0, NEGBIG, op0=OP.min, op1=OP.mult), "s_dv")

    cnt0 = dict(cnt)
    cnt1 = dict(cnt)
    CN = {0: cnt0, 1: cnt1}

    def binc(p, instr, name, n=1):
        CN[p][name] += n
        instr.then_inc(sems[name], n)

    def uinc(instr, name, n=1):
        CN[0][name] += n
        CN[1][name] += n
        instr.then_inc(sems[name], n)

    W3 = [(gw16, tg8, 0), (vw16, tv8, 1), (ow16, to8, 2)]
    gtot = HID * E

    @blk.vector
    def _(v):
        v.memset(tD[:, 0:3], 0.0)
        v.drain()
        inc(v.nop(), "s_wd")

    wa_marks, wd_marks, wd2_marks = [], [], []
    wo_q_marks, wo_qctr = [], [0]
    side_jobs = []

    def mk_p2b_tile(src_ap, mi, tg, wid):
        buf = tg % 2
        lsem = "s_ld" if buf == 0 else "s_l2"

        def emit():
            @blk.sync
            def _(s, war=(wd_marks[tg - 2] if tg >= 2 else None)):
                s.wait_ge(sems["c_ff"], cff_total)
                if war is not None:
                    s.wait_ge(sems["s_wd"], war)
                inc(s.dma_start(wtb[:, buf * PBW:buf * PBW + wid], src_ap),
                    lsem, 16)
            ldm = cnt[lsem]

            @blk.vector
            def _(v, ldm=ldm):
                v.wait_ge(sems[lsem], ldm)
                v.tensor_reduce(colA[:, 5 + buf:6 + buf],
                                wtb[:, buf * PBW:buf * PBW + wid],
                                axis=AX.X, op=OP.add,
                                apply_absolute_value=True)
                v.drain()
                v.tensor_tensor(tD[:, mi:mi + 1], tD[:, mi:mi + 1],
                                colA[:, 5 + buf:6 + buf], op=OP.add)
                v.drain()
                inc(v.nop(), "s_wd")
            wd_marks.append(cnt["s_wd"])
        return emit

    def mk_gcols(mi):
        def emit():
            @blk.tensor
            def _(te, w=cnt["s_wd"]):
                te.wait_ge(sems["s_wd"], w)
                inc(te.matmul(pb[6][:, 0:1], onesf[:, :], tD[:, mi:mi + 1],
                              start=True, stop=True), "s_wp")
            wpm = cnt["s_wp"]

            @blk.vector
            def _(v, wpm=wpm):
                v.wait_ge(sems["s_wp"], wpm)
                v.tensor_scalar(gcols[:, 3 + mi:4 + mi], pb[6][:, 0:1],
                                1.0 / gtot, 1e-5, op0=OP.mult, op1=OP.max)
                v.drain()
                v.tensor_scalar(gcols[:, mi:mi + 1],
                                gcols[:, 3 + mi:4 + mi], 0.5, None,
                                op0=OP.mult)
                v.tensor_scalar(gcols[:, 6 + mi:7 + mi],
                                gcols[:, 3 + mi:4 + mi], -0.5, None,
                                op0=OP.mult)
                v.tensor_scalar(gcols[:, 9 + mi:10 + mi],
                                gcols[:, 3 + mi:4 + mi], -1.0, None,
                                op0=OP.mult)
                v.drain()
                inc(v.nop(), "s_wd")
        return emit

    def mk_p2c_tile(job, tg2):
        w16_, t8_, mi, a0, colbase, wid = job
        buf = tg2 % 2
        lsem = "s_tl0" if buf == 0 else "s_tl1"
        qsem = "s_tq" if buf == 0 else "s_tq2"

        def emit():
            @blk.sync
            def _(s, war=(wd2_marks[tg2 - 2] if tg2 >= 2 else None)):
                if war is not None:
                    s.wait_ge(sems["s_wd"], war)
                src = w16_[colbase:colbase + wid,
                           a0 * 128:(a0 + 1) * 128]
                inc(s.dma_start_transpose(
                    wtc[:, buf * PCW:buf * PCW + wid], src), lsem, 16)
            ldm = cnt[lsem]

            @blk.vector
            def _(v, ldm=ldm, tqw=cnt[qsem], first=(tg2 < 2)):
                v.wait_ge(sems[lsem], ldm)
                if not first:
                    v.wait_ge(sems[qsem], tqw)
                v.tensor_scalar(ttmp[:, :wid],
                                wtc[:, buf * PCW:buf * PCW + wid],
                                gcols[:, mi:mi + 1], None, op0=OP.is_gt)
                v.drain()
                v.scalar_tensor_tensor(
                    tqc[:, buf * PCW:buf * PCW + wid],
                    wtc[:, buf * PCW:buf * PCW + wid],
                    gcols[:, 6 + mi:7 + mi], ttmp[:, :wid],
                    op0=OP.is_lt, op1=OP.subtract)
                v.drain()
                inc(v.nop(), "s_wd")
            wd2_marks.append(cnt["s_wd"])

            @blk.gpsimd
            def _(g, w=wd2_marks[-1]):
                g.wait_ge(sems["s_wd"], w)
                dst = t8_[a0 * 128:(a0 + 1) * 128, colbase:colbase + wid]
                inc(g.dma_start(dst,
                                tqc[:, buf * PCW:buf * PCW + wid]),
                    qsem, 16)
        return emit

    _tgc = [0]
    for w16_, _t8, mi in W3:
        if mi < 2:
            for t in range(HID // 128):
                for hf in range(max(1, E // PBW)):
                    wid = min(PBW, E)
                    src = w16_[t * 128:(t + 1) * 128,
                               hf * PBW:hf * PBW + wid]
                    side_jobs.append(mk_p2b_tile(src, mi, _tgc[0], wid))
                    _tgc[0] += 1
        else:
            for r4 in range(E // 128):
                for part in range(HID // PBW):
                    src = w16_[r4 * 128:(r4 + 1) * 128,
                               part * PBW:(part + 1) * PBW]
                    side_jobs.append(mk_p2b_tile(src, mi, _tgc[0], PBW))
                    _tgc[0] += 1
        side_jobs.append(mk_gcols(mi))
    tern_jobs = []
    for w16_, t8_, mi in W3:
        if mi < 2:
            for a0 in range(EC):
                for p_ in range(max(1, HID // PCW)):
                    tern_jobs.append((w16_, t8_, mi, a0, p_ * PCW,
                                      min(PCW, HID)))
        else:
            for a0 in range(HC):
                for p_ in range(max(1, E // PCW)):
                    tern_jobs.append((w16_, t8_, mi, a0, p_ * PCW,
                                      min(PCW, E)))
    for tg2, job in enumerate(tern_jobs):
        side_jobs.append(mk_p2c_tile(job, tg2))

    n_units = MYNB * H
    side_done = [0]

    def pump_side_jobs(unit_idx):
        want = (unit_idx + 1) * len(side_jobs) // n_units
        while side_done[0] < want:
            side_jobs[side_done[0]]()
            side_done[0] += 1

    SCL = 1.0 / math.sqrt(DH)
    for g in range(MYNB):
        for h in range(H):
            kv = h // GQ
            KT = {p: 4 * (my_par[p][g] + 1) for p in (0, 1)}
            bases = {p: dict(CN[p]) for p in (0, 1)}

            @blk.tensor
            def _(te, g=g, h=h, kv=kv, KT=KT, bases=bases):
                r = parity_reg(te)
                for p in (0, 1):
                    ctxm = te.If_eq(r, 0) if p == 0 else te.Else()
                    with ctxm:
                        B, n = bases[p], KT[p]
                        te.wait_ge(sems["s_dv"], B["s_dv"])
                        for j in range(n):
                            if j >= 2:
                                te.wait_ge(sems["s_ac"], B["s_ac"] + j - 1)
                            mm = te.matmul(
                                pb[j % 2][:, :],
                                Kres[:, kv * S + j * 128:kv * S + (j + 1) * 128],
                                Qres[:, h * MYT + g * TB:h * MYT + (g + 1) * TB],
                                start=True, stop=True)
                            binc(p, mm, "s_pe")
                            if j >= 1:
                                jj = j - 1
                                te.wait_ge(sems["s_ac"], B["s_ac"] + jj + 1)
                                mm = te.matmul(
                                    pb[2][:, :],
                                    Vres[:, kv * S + jj * 128:kv * S + (jj + 1) * 128],
                                    ptile[:, (jj % 2) * TB:(jj % 2 + 1) * TB],
                                    start=(jj == 0), stop=(jj == n - 1))
                                binc(p, mm, "s_ko")
                                mm = te.matmul(
                                    pb[3][:, :], onesb[:, :],
                                    ptile[:, (jj % 2) * TB:(jj % 2 + 1) * TB],
                                    start=(jj == 0), stop=(jj == n - 1))
                                if jj < n - 1:
                                    binc(p, mm, "s_ko")
                        jj = n - 1
                        te.wait_ge(sems["s_ac"], B["s_ac"] + n)
                        mm = te.matmul(
                            pb[2][:, :],
                            Vres[:, kv * S + jj * 128:kv * S + (jj + 1) * 128],
                            ptile[:, (jj % 2) * TB:(jj % 2 + 1) * TB],
                            start=(jj == 0), stop=True)
                        binc(p, mm, "s_ko")
                        mm = te.matmul(
                            pb[3][:, :], onesb[:, :],
                            ptile[:, (jj % 2) * TB:(jj % 2 + 1) * TB],
                            start=(jj == 0), stop=True)
                        binc(p, mm, "s_fb")

            @blk.vector
            def _(v, g=g, h=h, KT=KT, bases=bases):
                r = parity_reg(v)
                for p in (0, 1):
                    ctxm = v.If_eq(r, 0) if p == 0 else v.Else()
                    with ctxm:
                        B, n = bases[p], KT[p]
                        for ridx, j in enumerate(range(n - 4, n)):
                            v.wait_ge(sems["s_pe"], B["s_pe"] + j + 1)
                            mask_r = j - (n - 4)
                            tt = v.tensor_tensor(
                                pb[j % 2][:, :], pb[j % 2][:, :],
                                masks[:, mask_r * TB:(mask_r + 1) * TB],
                                op=OP.add)
                            binc(p, tt, "s_dv")
                        v.wait_ge(sems["s_fb"], B["s_fb"] + 1)
                        v.reciprocal(rsum[:, :], pb[3][:, :])
                        v.drain()
                        tt = v.tensor_tensor(
                            attst[:, h * TB:(h + 1) * TB],
                            pb[2][:, :], rsum[:, :], op=OP.mult)
                        binc(p, tt, "s_dv")

            @blk.scalar
            def _(sc, g=g, h=h, KT=KT, bases=bases):
                r = parity_reg(sc)
                for p in (0, 1):
                    ctxm = sc.If_eq(r, 0) if p == 0 else sc.Else()
                    with ctxm:
                        B, n = bases[p], KT[p]
                        for j in range(n):
                            band = j >= n - 4
                            if band:
                                sc.wait_ge(sems["s_dv"],
                                           B["s_dv"] + (j - (n - 4)) + 1)
                            else:
                                sc.wait_ge(sems["s_pe"], B["s_pe"] + j + 1)
                            if j >= 2:
                                sc.wait_ge(sems["s_ko"], B["s_ko"] + 2 * (j - 2) + 1)
                            a = sc.activation(
                                ptile[:, (j % 2) * TB:(j % 2 + 1) * TB],
                                pb[j % 2][:, :], AF.Exp, scale=SCL,
                                bias=negB[:, h:h + 1])
                            binc(p, a, "s_ac")

            pump_side_jobs(g * H + h)

        att_dv = CN[0]["s_dv"]
        assert att_dv == CN[1]["s_dv"]
        QN = max(1, EC // 4)
        for eg in range(EC // 2):
            qld = []
            for q in range(QN):
                slot = wo_qctr[0] % 2
                wsem = "s_wo0" if slot == 0 else "s_wo1"
                wo_qctr[0] += 1

                @blk.sync
                def _(s, g=g, eg=eg, q=q, slot=slot,
                      w=(wo_q_marks[-2] if len(wo_q_marks) >= 2 else None)):
                    s.wait_ge(sems["c_aw"], caw_wo)
                    if w is not None:
                        s.wait_ge(sems["s_wt"], w)
                    for cq in range(4):
                        c = q * 4 + cq
                        u = s.dma_start_transpose(
                            wob[:, slot * 1024 + cq * 256:
                                slot * 1024 + (cq + 1) * 256],
                            wo16[eg * 256:(eg + 1) * 256,
                                 c * 128:(c + 1) * 128])
                        uinc(u, wsem, 16)
                qld.append((wsem, CN[0][wsem]))
                wo_q_marks.append(CN[0]["s_wt"] + q + 1)

            @blk.tensor
            def _(te, g=g, eg=eg, att_dv=att_dv, qld=qld, qbase=wo_qctr[0] - QN,
                  w=CN[0]["s_dv"]):
                te.wait_ge(sems["s_dv"], max(att_dv, w))
                for q in range(QN):
                    te.wait_ge(sems[qld[q][0]], qld[q][1])
                    slot = (qbase + q) % 2
                    for ee in range(2):
                        for cq in range(4):
                            c = q * 4 + cq
                            mm = te.matmul(
                                pb[4 + ee][:, :],
                                wob[:, slot * 1024 + cq * 256 + ee * 128:
                                    slot * 1024 + cq * 256 + (ee + 1) * 128],
                                attst[:, c * TB:(c + 1) * TB],
                                start=(c == 0), stop=(c == EC - 1))
                    uinc(mm, "s_wt")
            wo_pe = CN[0]["s_wt"]

            for ee in range(2):
                e = eg * 2 + ee

                @blk.vector
                def _(v, g=g, e=e, ee=ee, wo_pe=wo_pe):
                    v.wait_ge(sems["s_wt"], wo_pe)
                    st = v.scalar_tensor_tensor(
                        xres[:, e * MYT + g * TB:e * MYT + (g + 1) * TB],
                        pb[4 + ee][:, :], modc[:, 2 * EC + e:2 * EC + e + 1],
                        xres[:, e * MYT + g * TB:e * MYT + (g + 1) * TB],
                        op0=OP.mult, op1=OP.add)
                    v.drain()
                    uinc(v.nop(), "s_dv")

    while side_done[0] < len(side_jobs):
        side_jobs[side_done[0]]()
        side_done[0] += 1

    assert cnt0 == cnt1, (cnt0, cnt1)
    for _k in ("s_pe", "s_dv", "s_ac", "s_ko", "s_fb", "s_wt", "s_qw"):
        cnt[_k] = cnt0[_k]

    if "gcols" in dbg:
        @blk.scalar
        def _(sc, w=cnt["s_wd"]):
            sc.wait_ge(sems["s_wd"], w)
            tap(sc, "gcols", gcols[:, :], 128, 12)

    if "attst" in dbg:
        @blk.scalar
        def _(sc, w=cnt["s_dv"]):
            sc.wait_ge(sems["s_dv"], w)
            tap(sc, "attst", attst[:, :], 128, H * TB)
    if "xmid" in dbg:
        @blk.scalar
        def _(sc, w=cnt["s_dv"]):
            sc.wait_ge(sems["s_dv"], w)
            tap(sc, "xmid", xres[:, :], 128, EC * MYT)

    @blk.gpsimd
    def _(g):
        g.wait_ge(sems["s_tq"], cnt["s_tq"])
        g.wait_ge(sems["s_tq2"], cnt["s_tq2"])

    cnt_tq = (cnt["s_tq"], cnt["s_tq2"])
    esB.close()

    esP2.close()
    esP2w.close()
    esKV.close()
    esP3 = contextlib.ExitStack()
    gvb = mk(esP3, "gvb", [128, HC * TB], F16)
    w8b = mk(esP3, "w8b", [128, 2 * EC * TB], FP8)
    h2b = mk(esP3, "h2b", [128, EC * TB], BF16)
    rqb = mk(esP3, "rqb", [128, EC * TB], BF16)
    sgam1 = mk(esP3, "sgam1", [128, TB], F32)
    bgam1 = mk(esP3, "bgam1", [128, TB], F32)
    sgam2 = sgam1
    bgam2 = bgam1
    ostg = mk(esP3, "ostg", [128, 2 * 512], F32)
    tA2 = mk(esP3, "tA2", [128, TB], F32)
    tB2 = mk(esP3, "tB2", [128, TB], F32)
    tC2 = mk(esP3, "tC2", [128, TB], F32)
    Gpad = mk(esP3, "Gpad", [128, 128], F32)
    stagb = ostg[:, 0:512]
    ident32 = mk(esP3, "ident32", [128, 128], F32)
    w8o = mk(esP3, "w8o", [128, 4 * TB], FP8)
    bC1c = mk(esP3, "bC1c", [128, 1], F32)
    bC2c = mk(esP3, "bC2c", [128, 1], F32)
    bC3c = mk(esP3, "bC3c", [128, 1], F32)

    esB = contextlib.ExitStack()
    blk = esB.enter_context(nc.Block("p3", no_gpsimd_drain=True))

    @blk.vector
    def _(v):
        v.tensor_scalar(cacol[:, :], modc[:, 5 * EC:6 * EC],
                        gcols[:, 5:6], None, op0=OP.mult)
        v.tensor_tensor(bias2[:, :], modc[:, 5 * EC:6 * EC], biaso[:, :],
                        op=OP.mult)
        v.memset(Gpad[:, :], 0.0)
        v.memset(stagb[:, :], 0.0)
        v.memset(bC1c[:, :], C1)
        v.memset(bC2c[:, :], C2)
        v.memset(bC3c[:, :], C3)
        v.tensor_copy(ident32[:, :], identb[:, :])
        v.drain()
        inc(v.nop(), "s_dv")

    def emit_gamma2(src_ap, sg_ap, bg_ap):
        @blk.tensor
        def _(te, w=cnt["s_dv"]):
            te.wait_ge(sems["s_dv"], w)
            for i in range(4):
                mm = te.transpose(pb[4][:, i * 128:(i + 1) * 128],
                                  src_ap[:, i * 128:(i + 1) * 128],
                                  ident32[:, :])
            inc(mm, "s_pe")

        @blk.vector
        def _(v, w=cnt["s_pe"]):
            v.wait_ge(sems["s_pe"], w)
            for i in range(4):
                v.tensor_reduce(Gpad[:, 32 * i:32 * i + 1],
                                pb[4][:, i * 128:(i + 1) * 128],
                                axis=AX.X, op=OP.max)
            v.drain()
            inc(v.nop(), "s_dv")

        @blk.tensor
        def _(te, w=cnt["s_dv"]):
            te.wait_ge(sems["s_dv"], w)
            inc(te.transpose(pb[5][:, 0:128], Gpad[:, :], ident32[:, :]),
                "s_pe")

        @blk.vector
        def _(v, w=cnt["s_pe"],
              stw=(st_marks[0][-1] if st_marks[0] else None)):
            v.wait_ge(sems["s_pe"], w)
            if stw is not None:
                v.wait_ge(sems["s_st"], stw)
            v.memset(stagb[:, :], 0.0)
            v.drain()
            for i in range(4):
                v.tensor_copy(stagb[32 * i:32 * i + 1, i * 128:(i + 1) * 128],
                              pb[5][32 * i:32 * i + 1, 0:128])
            v.drain()
            inc(v.nop(), "s_dv")

        @blk.tensor
        def _(te, w=cnt["s_dv"]):
            te.wait_ge(sems["s_dv"], w)
            inc(te.matmul(pb[6][:, :], onesf[:, :], stagb[:, :],
                          start=True, stop=True), "s_pe")

        @blk.vector
        def _(v, w=cnt["s_pe"]):
            v.wait_ge(sems["s_pe"], w)
            v.tensor_scalar(tA2[:, :], pb[6][:, :], 1e-5, None, op0=OP.max)
            v.drain()
            v.reciprocal(tB2[:, :], tA2[:, :])
            v.drain()
            v.tensor_scalar(sg_ap, tB2[:, :], 127.0, None, op0=OP.mult)
            inc(v.tensor_scalar(bg_ap, tA2[:, :], 1.0 / 127.0, None,
                                op0=OP.mult), "s_dv")

    tD2 = ostg[:, 512:1024]
    rnd_sets = [(tA, tB, tC, tD), (tA2, tB2, tC2, tD2)]
    gv_half_pe, gv_epi1, gv_silu, out_pe_marks = [], [], [], []
    st_marks, st_qctr = {0: [], 1: []}, [0]

    def emit_rounds(src_of, dst_of, sx_ap, n):
        marks, u_marks, ac_marks = [], [], []
        stw2 = st_marks[1][-1] if st_marks[1] else None
        for c in range(n):
            s0, s1, s2, s3 = rnd_sets[c % 2]

            @blk.vector
            def _(v, s0=s0, src=src_of(c),
                  war=(ac_marks[c - 2] if c >= 2 else None)):
                if war is not None:
                    v.wait_ge(sems["s_ac"], war)
                inc(v.tensor_tensor(s0[:, :], src, sx_ap, op=OP.mult),
                    "s_dv")
            u_marks.append(cnt["s_dv"])

            @blk.scalar
            def _(sc, s0=s0, s1=s1, s2=s2, s3=s3, w=u_marks[c],
                  stw2=(stw2 if c == 1 else None)):
                sc.wait_ge(sems["s_dv"], w)
                if stw2 is not None:
                    sc.wait_ge(sems["s_st2"], stw2)
                sc.activation(s1[:, :], s0[:, :], AF.Identity,
                              bias=bC1c[:, :1])
                sc.activation(s2[:, :], s0[:, :], AF.Identity,
                              bias=bC2c[:, :1])
                inc(sc.activation(s3[:, :], s0[:, :], AF.Identity,
                                  bias=bC3c[:, :1]), "s_ac")
            ac_marks.append(cnt["s_ac"])

            @blk.vector
            def _(v, s1=s1, s2=s2, s3=s3, dst=dst_of(c), w=ac_marks[c]):
                v.wait_ge(sems["s_ac"], w)
                v.scalar_tensor_tensor(s1[:, :], s2[:, :], -C2, s1[:, :],
                                       op0=OP.add, op1=OP.add)
                v.drain()
                inc(v.scalar_tensor_tensor(dst, s3[:, :], C1 - C3,
                                           s1[:, :], op0=OP.add,
                                           op1=OP.subtract), "s_dv")
            marks.append(cnt["s_dv"])
        return marks

    for g in range(MYNB):
        @blk.scalar
        def _(sc, g=g, w=cnt["s_dv"], wdb=cnt["s_db"]):
            sc.wait_ge(sems["s_dv"], w)
            sc.wait_ge(sems["s_db"], wdb)
            for c in range(EC):
                a = sc.activation(h2b[:, c * TB:(c + 1) * TB],
                                  xres[:, c * MYT + g * TB:c * MYT + (g + 1) * TB],
                                  AF.Square)
            inc(a, "s_ac")
        sq_ac = cnt["s_ac"]

        @blk.tensor
        def _(te, g=g, sq_ac=sq_ac, w=cnt["s_dv"]):
            te.wait_ge(sems["s_ac"], sq_ac)
            te.wait_ge(sems["s_dv"], w)
            for c in range(EC):
                mm = te.matmul(pb[0][:, :], onesb[:, :],
                               xres[:, c * MYT + g * TB:c * MYT + (g + 1) * TB],
                               start=(c == 0), stop=(c == EC - 1))
            inc(mm, "s_pe")
            for c in range(EC):
                mm = te.matmul(pb[1][:, :], onesb[:, :],
                               h2b[:, c * TB:(c + 1) * TB],
                               start=(c == 0), stop=(c == EC - 1))
            inc(mm, "s_pe")

        @blk.vector
        def _(v, w=cnt["s_pe"]):
            v.wait_ge(sems["s_pe"], w)
            v.tensor_scalar(tA[:, :], pb[0][:, :], 1.0 / E, None, op0=OP.mult)
            v.drain()
            v.tensor_tensor(tB[:, :], tA[:, :], tA[:, :], op=OP.mult)
            v.drain()
            v.scalar_tensor_tensor(tC[:, :], pb[1][:, :], 1.0 / E, tB[:, :],
                                   op0=OP.mult, op1=OP.subtract)
            v.drain()
            inc(v.tensor_scalar(tB[:, :], tC[:, :], cfg.eps, None, op0=OP.add),
                "s_dv")

        @blk.scalar
        def _(sc, w=cnt["s_dv"]):
            sc.wait_ge(sems["s_dv"], w)
            inc(sc.activation(tC[:, :], tB[:, :], AF.Sqrt), "s_ac")

        @blk.vector
        def _(v, g=g, w=cnt["s_ac"]):
            v.wait_ge(sems["s_ac"], w)
            v.reciprocal(tB[:, :], tC[:, :])
            v.drain()
            for c in range(EC):
                v.tensor_tensor(tC[:, :],
                                xres[:, c * MYT + g * TB:c * MYT + (g + 1) * TB],
                                tA[:, :], op=OP.subtract)
                v.drain()
                v.tensor_tensor(tD[:, :], tC[:, :], tB[:, :], op=OP.mult)
                v.drain()
                v.tensor_scalar(h2b[:, c * TB:(c + 1) * TB], tD[:, :],
                                modc[:, 4 * EC + c:4 * EC + c + 1],
                                modc[:, 3 * EC + c:3 * EC + c + 1],
                                op0=OP.mult, op1=OP.add)
                v.drain()
            v.tensor_reduce(tC[:, :],
                            h2b.rearrange("p (c t) -> p t c", c=EC),
                            axis=AX.X, op=OP.max, apply_absolute_value=True)
            v.drain()
            inc(v.nop(), "s_dv")

        emit_gamma2(tC[:, :], sgam1[:, :], bgam1[:, :])

        @blk.vector
        def _(v, wdb=cnt["s_db"]):
            v.wait_ge(sems["s_db"], wdb)
            v.drain()
            inc(v.nop(), "s_dv")
        rq_marks = emit_rounds(
            lambda c: h2b[:, c * TB:(c + 1) * TB],
            lambda c: rqb[:, c * TB:(c + 1) * TB],
            sgam1[:, :], EC)
        rq_dv = rq_marks[-1]

        if g == 0 and "rqb" in dbg:
            @blk.scalar
            def _(sc, w=rq_dv):
                sc.wait_ge(sems["s_dv"], w)
                tap(sc, "rqb", rqb[:, :], 128, EC * TB)
        if g == 0 and "h2b" in dbg:
            @blk.scalar
            def _(sc, w=rq_dv):
                sc.wait_ge(sems["s_dv"], w)
                tap(sc, "h2b", h2b[:, :], 128, EC * TB)
        if g == 0 and "sgam" in dbg:
            @blk.scalar
            def _(sc, w=rq_dv):
                sc.wait_ge(sems["s_dv"], w)
                tap(sc, "sgam", sgam1[:, :], 128, TB)

        HALF = EC * TB
        for hg in range(HC // 4):
            for k in range(2):
                idx = g * (HC // 2) + hg * 2 + k

                gsem = "s_gv0" if (len(gv_half_pe) % 2 == 0) else "s_gv1"

                @blk.sync
                def _(s, hg=hg, k=k, gsem=gsem,
                      w=(gv_half_pe[-2] if len(gv_half_pe) >= 2 else None),
                      first=(len(gv_half_pe) == 0)):
                    if first:
                        s.wait_ge(sems["s_tq"], cnt_tq[0])
                        s.wait_ge(sems["s_tq2"], cnt_tq[1])
                    if w is not None:
                        s.wait_ge(sems["s_pe"], w)
                    for d in range(EC):
                        inc(s.dma_start(
                            w8b[:, k * HALF + d * 256:k * HALF + d * 256 + 256],
                            tg8[d * 128:(d + 1) * 128,
                                hg * 512 + k * 256:hg * 512 + k * 256 + 256]),
                            gsem, 16)
                    for d in range(EC):
                        inc(s.dma_start(
                            w8b[:, k * HALF + (EC + d) * 256:
                                k * HALF + (EC + d) * 256 + 256],
                            tv8[d * 128:(d + 1) * 128,
                                hg * 512 + k * 256:hg * 512 + k * 256 + 256]),
                            gsem, 16)
                gv_ld = cnt[gsem]

                for hp in range(2):
                    h = hg * 4 + k * 2 + hp

                    @blk.tensor
                    def _(te, h=h, k=k, hp=hp, gv_ld=gv_ld, gsem=gsem,
                          rq_dv=rq_dv,
                          bankw=(gv_epi1[-2] if len(gv_epi1) >= 2 else None)):
                        te.wait_ge(sems[gsem], gv_ld)
                        if h == 0:
                            te.wait_ge(sems["s_dv"], rq_dv)
                        if bankw is not None:
                            te.wait_ge(sems["s_dv"], bankw)
                        pg = pb[2 * (h % 2)]
                        pv = pb[2 * (h % 2) + 1]
                        for d in range(EC):
                            mm = te.matmul(
                                pg[:, :],
                                w8b[:, k * HALF + d * 256 + hp * 128:
                                    k * HALF + d * 256 + (hp + 1) * 128],
                                rqb[:, d * TB:(d + 1) * TB],
                                start=(d == 0), stop=(d == EC - 1))
                        inc(mm, "s_pe")
                        for d in range(EC):
                            mm = te.matmul(
                                pv[:, :],
                                w8b[:, k * HALF + (EC + d) * 256 + hp * 128:
                                    k * HALF + (EC + d) * 256 + (hp + 1) * 128],
                                rqb[:, d * TB:(d + 1) * TB],
                                start=(d == 0), stop=(d == EC - 1))
                        inc(mm, "s_pe")
                    gv_pe = cnt["s_pe"]

                    @blk.vector
                    def _(v, h=h, gv_pe=gv_pe,
                          silu_war=(gv_silu[-2] if len(gv_silu) >= 2
                                    else None)):
                        v.wait_ge(sems["s_pe"], gv_pe)
                        if silu_war is not None:
                            v.wait_ge(sems["s_ac"], silu_war)
                        pg = pb[2 * (h % 2)]
                        pv = pb[2 * (h % 2) + 1]
                        gp = [tA, tC][h % 2]
                        vp = [tB, tD][h % 2]
                        v.scalar_tensor_tensor(gp[:, :], pg[:, :],
                                               gcols[:, 3:4], bgam1[:, :],
                                               op0=OP.mult, op1=OP.mult)
                        v.drain()
                        inc(v.scalar_tensor_tensor(vp[:, :], pv[:, :],
                                                   gcols[:, 4:5],
                                                   bgam1[:, :],
                                                   op0=OP.mult,
                                                   op1=OP.mult), "s_dv")
                    gv_epi1.append(cnt["s_dv"])

                    @blk.scalar
                    def _(sc, h=h, w=gv_epi1[-1]):
                        sc.wait_ge(sems["s_dv"], w)
                        inc(sc.activation([tE16, tF16][h % 2][:, :],
                                          [tA, tC][h % 2][:, :], AF.Silu,
                                          bias=biasg[:, h:h + 1]), "s_ac")
                    gv_silu.append(cnt["s_ac"])

                    @blk.vector
                    def _(v, h=h, w=gv_silu[-1]):
                        v.wait_ge(sems["s_ac"], w)
                        inc(v.scalar_tensor_tensor(
                            gvb[:, h * TB:(h + 1) * TB],
                            [tB, tD][h % 2][:, :], biasvb[:, h:h + 1],
                            [tE16, tF16][h % 2][:, :],
                            op0=OP.add, op1=OP.mult), "s_dv")
                gv_half_pe.append(gv_pe)

        if g == 0 and "gvb" in dbg:
            @blk.scalar
            def _(sc, w=cnt["s_dv"]):
                sc.wait_ge(sems["s_dv"], w)
                tap(sc, "gvb", gvb[:, :], 128, HC * TB)

        @blk.vector
        def _(v):
            v.drain()
            v.tensor_reduce(tC[:, :],
                            gvb.rearrange("p (c t) -> p t c", c=HC),
                            axis=AX.X, op=OP.max, apply_absolute_value=True)
            v.drain()
            inc(v.nop(), "s_dv")

        emit_gamma2(tC[:, :], sgam2[:, :], bgam2[:, :])

        @blk.vector
        def _(v, wdb=cnt["s_db"]):
            v.wait_ge(sems["s_db"], wdb)
            v.drain()
            inc(v.nop(), "s_dv")
        r2_marks = emit_rounds(
            lambda c: gvb[:, c * TB:(c + 1) * TB],
            lambda c: gvb[:, c * TB:(c + 1) * TB],
            sgam2[:, :], HC)
        r2_dv = r2_marks[-1]

        for sw in range(EC // 4):
            sw_pe0 = cnt["s_pe"]
            for h in range(HC):
                slot = h % 4
                osem = ["s_ld", "s_l3", "s_l2", "s_l4"][slot]
                war = (out_pe_marks[-4] if len(out_pe_marks) >= 4 else None)

                def _ld(e, h=h, sw=sw, slot=slot, osem=osem, war=war):
                    if war is not None:
                        e.wait_ge(sems["s_pe"], war)
                    inc(e.dma_start(
                        w8o[:, slot * TB:(slot + 1) * TB],
                        to8[h * 128:(h + 1) * 128, sw * 512:(sw + 1) * 512]),
                        osem, 16)
                if h % 2 == 0:
                    blk.sync(_ld)
                else:
                    blk.scalar(_ld)

                @blk.tensor
                def _(te, sw=sw, h=h, slot=slot, osem=osem, ld=cnt[osem],
                      r2m=(r2_marks[h] if sw == 0 else None),
                      evacw=(cnt["s_dv"] if h == 0 else None)):
                    te.wait_ge(sems[osem], ld)
                    if r2m is not None:
                        te.wait_ge(sems["s_dv"], r2m)
                    if evacw is not None:
                        te.wait_ge(sems["s_dv"], evacw)
                    for ee in range(4):
                        mm = te.matmul(
                            pb[ee][:, :],
                            w8o[:, slot * TB + ee * 128:
                                slot * TB + (ee + 1) * 128],
                            gvb[:, h * TB:(h + 1) * TB],
                            start=(h == 0), stop=(h == HC - 1))
                    inc(mm, "s_pe")
                out_pe_marks.append(cnt["s_pe"])

            osw_pe = cnt["s_pe"]
            for ee in range(4):
                e = sw * 4 + ee

                @blk.vector
                def _(v, g=g, e=e, ee=ee, osw_pe=osw_pe, wdb=cnt["s_db"]):
                    v.wait_ge(sems["s_pe"], osw_pe)
                    v.wait_ge(sems["s_db"], wdb)
                    v.scalar_tensor_tensor(tA[:, :], pb[ee][:, :],
                                           cacol[:, e:e + 1], bgam2[:, :],
                                           op0=OP.mult, op1=OP.mult)
                    v.drain()
                    v.scalar_tensor_tensor(
                        h2b[:, e * TB:(e + 1) * TB], tA[:, :],
                        bias2[:, e:e + 1],
                        xres[:, e * MYT + g * TB:e * MYT + (g + 1) * TB],
                        op0=OP.add, op1=OP.add)
                    v.drain()
                    inc(v.nop(), "s_dv")
        xout_dv = cnt["s_dv"]
        if g == 0 and "xoutb" in dbg:
            @blk.scalar
            def _(sc, w=xout_dv):
                sc.wait_ge(sems["s_dv"], w)
                tap(sc, "xoutb", h2b[:, :], 128, EC * TB)

        n_q = max(1, EC // 4)
        qt = min(4, EC)
        for sub in range(TB // 128):
            for q in range(n_q):
                pp = st_qctr[0] % 2
                st_qctr[0] += 1

                @blk.tensor
                def _(te, g=g, sub=sub, q=q, pp=pp, xout_dv=xout_dv,
                      w=cnt["s_dv"]):
                    te.wait_ge(sems["s_dv"], max(xout_dv, w))
                    for e4 in range(qt):
                        e = q * qt + e4
                        mm = te.transpose(
                            pbT[:, pp * 512 + e4 * 128:
                                pp * 512 + (e4 + 1) * 128],
                            h2b[:, e * TB + sub * 128:e * TB + (sub + 1) * 128],
                            identb[:, :])
                    inc(mm, "s_pe")

                @blk.vector
                def _(v, pp=pp, w=cnt["s_pe"],
                      stw=(st_marks[pp][-1] if st_marks[pp] else None)):
                    v.wait_ge(sems["s_pe"], w)
                    if stw is not None:
                        v.wait_ge(sems["s_st" if pp == 0 else "s_st2"],
                                  stw)
                    v.tensor_copy(ostg[:, pp * 512:(pp + 1) * 512],
                                  pbT[:, pp * 512:(pp + 1) * 512])
                    v.drain()
                    inc(v.nop(), "s_dv")

                @blk.scalar
                def _(sc, g=g, sub=sub, q=q, pp=pp, w=cnt["s_dv"]):
                    sc.wait_ge(sems["s_dv"], w)
                    inc(sc.dma_start(
                        y_out[g * TB + sub * 128:g * TB + (sub + 1) * 128,
                              q * qt * 128:(q + 1) * qt * 128],
                        ostg[:, pp * 512:pp * 512 + qt * 128]),
                        "s_st" if pp == 0 else "s_st2", 16)
                st_marks[pp].append(cnt["s_st" if pp == 0 else "s_st2"])

    @blk.sync
    def _(s):
        s.wait_ge(sems["s_st"], cnt["s_st"])
        s.wait_ge(sems["s_st2"], cnt["s_st2"])
        s.wait_ge(sems["s_db"], cnt["s_db"])

    esB.close()
    esP3.close()
    es0.close()
    return nc



def kernel(L0_attn_x, L0_attn_wq, L0_attn_wk, L0_attn_wv, L0_attn_wo,
           L0_attn_bq, L0_attn_bk, L0_attn_bv, L0_ffn_gate_w, L0_ffn_gate_b,
           L0_ffn_val_w, L0_ffn_val_b, L0_ffn_out_w, L0_ffn_out_b, L0_c,
           L0_wmod):
    cfg = Cfg()
    nc = build_program(cfg)
    A = np.ascontiguousarray

    def row(v):
        return A(np.asarray(v, dtype=np.float32).reshape(1, -1))

    shared = {
        "wmod": A(np.asarray(L0_wmod, np.float32)),
        "wq": A(np.asarray(L0_attn_wq, np.float32)),
        "wk": A(np.asarray(L0_attn_wk, np.float32)),
        "wv": A(np.asarray(L0_attn_wv, np.float32)),
        "wo": A(np.asarray(L0_attn_wo, np.float32)),
        "bq": row(L0_attn_bq), "bk": row(L0_attn_bk), "bv": row(L0_attn_bv),
        "gate_w": A(np.asarray(L0_ffn_gate_w, np.float32)),
        "val_w": A(np.asarray(L0_ffn_val_w, np.float32)),
        "out_w": A(np.asarray(L0_ffn_out_w, np.float32)),
        "gate_b": row(L0_ffn_gate_b), "val_b": row(L0_ffn_val_b),
        "ob": row(L0_ffn_out_b),
    }
    x_np = np.asarray(L0_attn_x, np.float32)
    c_np = np.asarray(L0_c, np.float32)
    B = x_np.shape[0]
    n_cores = 2 * B
    in_maps = []
    for core in range(n_cores):
        b, p = core // 2, core % 2
        im = dict(shared)
        im["x"] = A(x_np[b])
        im["c_row"] = A(c_np[b:b + 1])
        im["parity"] = np.array([[p]], np.int32)
        in_maps.append(im)

    res = run_bass_kernel_spmd(nc, in_maps, core_ids=list(range(n_cores)))

    out = np.empty((B, cfg.S, cfg.E), np.float32)
    for core in range(n_cores):
        b, p = core // 2, core % 2
        y = res.results[core]["y"]
        for gi, m in enumerate(cfg.my_blocks(p)):
            out[b, m * cfg.TB:(m + 1) * cfg.TB] = y[gi * cfg.TB:(gi + 1) * cfg.TB]
    return out


# version 10
# speedup vs baseline: 1.0517x; 1.0517x over previous
import contextlib
import math

import numpy as np

import concourse.bass as bass
import concourse.mybir as mybir
from concourse.alu_op_type import AluOpType as OP
from concourse.bass_utils import run_bass_kernel_spmd

F32 = mybir.dt.float32
F16 = mybir.dt.float16
BF16 = mybir.dt.bfloat16
FP8 = mybir.dt.float8e4
I32 = mybir.dt.int32
AF = mybir.ActivationFunctionType
AX = mybir.AxisListType

C1 = 12582912.0
C2 = 6755399441055744.0
C3 = 6442450944.0

TP1 = float(np.float32(2 * math.pi))
TP2 = float(np.float32(2 * math.pi - TP1))
TP3 = float(2 * math.pi - TP1 - TP2)
INV_2PI = 1.0 / (2 * math.pi)
HALF_PI = math.pi / 2
PI_F = float(np.float32(math.pi))
NEGBIG = 1.0e7


class Cfg:
    def __init__(self, S=4096, E=2048, H=16, HKV=2, theta=10000.0):
        self.S, self.E, self.H, self.HKV, self.theta = S, E, H, HKV, theta
        self.DH = 128
        self.G = H // HKV
        self.KVD = HKV * self.DH
        self.HID = 4 * E
        self.EC = E // 128
        self.HC = self.HID // 128
        self.TB = 512
        self.NB = S // self.TB
        self.MYNB = self.NB // 2
        self.MYT = self.MYNB * self.TB
        self.eps = 1e-5

    def my_blocks(self, parity):
        n, q = self.NB, self.NB // 4
        if parity == 0:
            return list(range(q)) + list(range(n - q, n))
        return list(range(q, n - q))


def build_program(cfg, debug=()):
    nc = bass.Bass("TRN2", target_bir_lowering=False)
    S, E, H, HKV, DH = cfg.S, cfg.E, cfg.H, cfg.HKV, cfg.DH
    EC, HC, HID, KVD = cfg.EC, cfg.HC, cfg.HID, cfg.KVD
    TB, NB, MYNB, MYT = cfg.TB, cfg.NB, cfg.MYNB, cfg.MYT
    E6 = 6 * E
    GQ = cfg.G

    x_in = nc.dram_tensor("x", [S, E], F32, kind="ExternalInput").ap()
    c_in = nc.dram_tensor("c_row", [1, E], F32, kind="ExternalInput").ap()
    wmod_in = nc.dram_tensor("wmod", [E, E6], F32, kind="ExternalInput").ap()
    wq_in = nc.dram_tensor("wq", [E, E], F32, kind="ExternalInput").ap()
    wk_in = nc.dram_tensor("wk", [KVD, E], F32, kind="ExternalInput").ap()
    wv_in = nc.dram_tensor("wv", [KVD, E], F32, kind="ExternalInput").ap()
    wo_in = nc.dram_tensor("wo", [E, E], F32, kind="ExternalInput").ap()
    bq_in = nc.dram_tensor("bq", [1, E], F32, kind="ExternalInput").ap()
    bk_in = nc.dram_tensor("bk", [1, KVD], F32, kind="ExternalInput").ap()
    bv_in = nc.dram_tensor("bv", [1, KVD], F32, kind="ExternalInput").ap()
    gw_in = nc.dram_tensor("gate_w", [HID, E], F32, kind="ExternalInput").ap()
    vw_in = nc.dram_tensor("val_w", [HID, E], F32, kind="ExternalInput").ap()
    ow_in = nc.dram_tensor("out_w", [E, HID], F32, kind="ExternalInput").ap()
    gb_in = nc.dram_tensor("gate_b", [1, HID], F32, kind="ExternalInput").ap()
    vb_in = nc.dram_tensor("val_b", [1, HID], F32, kind="ExternalInput").ap()
    ob_in = nc.dram_tensor("ob", [1, E], F32, kind="ExternalInput").ap()
    par_in = nc.dram_tensor("parity", [1, 1], I32, kind="ExternalInput").ap()
    y_out = nc.dram_tensor("y", [MYT, E], F32, kind="ExternalOutput").ap()

    x16 = nc.dram_tensor("x16", [S, E], BF16, kind="Internal").ap()
    wq16 = nc.dram_tensor("wq16", [E, E], BF16, kind="Internal").ap()
    wk16 = nc.dram_tensor("wk16", [KVD, E], BF16, kind="Internal").ap()
    wv16 = nc.dram_tensor("wv16", [KVD, E], BF16, kind="Internal").ap()
    wo16 = nc.dram_tensor("wo16", [E, E], BF16, kind="Internal").ap()
    gw16 = nc.dram_tensor("gw16", [HID, E], F16, kind="Internal").ap()
    vw16 = nc.dram_tensor("vw16", [HID, E], F16, kind="Internal").ap()
    ow16 = nc.dram_tensor("ow16", [E, HID], F16, kind="Internal").ap()
    tg8 = nc.dram_tensor("tg8", [E, HID], FP8, kind="Internal").ap()
    tv8 = nc.dram_tensor("tv8", [E, HID], FP8, kind="Internal").ap()
    to8 = nc.dram_tensor("to8", [HID, E], FP8, kind="Internal").ap()
    modscr = nc.dram_tensor("modscr", [1, E6], F32, kind="Internal").ap()
    trigscr = nc.dram_tensor("trigscr", [128, 2 * S], BF16, kind="Internal").ap()
    h1scr = nc.dram_tensor("h1scr", [128, MYNB * EC * TB], BF16,
                           kind="Internal").ap()

    dbg = {}
    for spec_ in debug:
        name, shp = spec_[0], list(spec_[1])
        ddt = {"f32": F32, "bf16": BF16, "f16": F16}[spec_[2] if len(spec_) > 2 else "f32"]
        dbg[name] = nc.dram_tensor("dbg_" + name, shp, ddt,
                                   kind="ExternalOutput").ap()
    n_dbg_stores = [0]

    es0 = contextlib.ExitStack()

    def mk(stack, name, shape, dt):
        return stack.enter_context(nc.sbuf_tensor(name, shape, dt))

    xres = mk(es0, "xres", [128, EC * MYT], BF16)
    onesb = mk(es0, "onesb", [128, 128], BF16)
    onesf = mk(es0, "onesf", [128, 128], F32)
    identb = mk(es0, "identb", [128, 128], BF16)
    swapb = mk(es0, "swapb", [128, 128], BF16)
    modc = mk(es0, "modc", [128, 6 * EC], F32)
    scb = mk(es0, "scb", [128, EC], BF16)
    biasq = mk(es0, "biasq", [128, H], F32)
    biask = mk(es0, "biask", [128, HKV], F32)
    bvbank = mk(es0, "bvbank", [128, KVD], F32)
    biasg = mk(es0, "biasg", [128, HC], F32)
    biasvb = mk(es0, "biasvb", [128, HC], F32)
    biaso = mk(es0, "biaso", [128, EC], F32)
    gcols = mk(es0, "gcols", [128, 12], F32)
    cacol = mk(es0, "cacol", [128, EC], F32)
    bias2 = mk(es0, "bias2", [128, EC], F32)
    negB = mk(es0, "negB", [128, H], F32)
    qmax2 = mk(es0, "qmax2", [128, H], F32)
    kmax2 = mk(es0, "kmax2", [128, HKV], F32)
    tA = mk(es0, "tA", [128, TB], F32)
    tB = mk(es0, "tB", [128, TB], F32)
    tC = mk(es0, "tC", [128, TB], F32)
    tD = mk(es0, "tD", [128, TB], F32)
    tE16 = mk(es0, "tE16", [128, TB], BF16)
    tF16 = mk(es0, "tF16", [128, TB], BF16)
    colA = mk(es0, "colA", [128, 8], F32)
    parsb = mk(es0, "parsb", [1, 1], I32)

    pb = [es0.enter_context(nc.psum_tensor(f"pb{i}", [128, 512], F32))
          for i in range(7)]
    pbT = es0.enter_context(nc.psum_tensor("pbT", [128, 1024], BF16))

    sem_names = ["c_wm", "c_wm2", "c_x", "c_aw", "c_ff", "s_ld", "s_l2",
                 "s_wt", "s_pe", "s_dv", "s_ac", "s_gp", "s_st", "s_h1",
                 "s_qw", "s_tq", "s_tq2", "s_ko", "s_fb", "s_db", "s_x0",
                 "s_x1", "s_qt", "s_l3", "s_l4", "s_gv0", "s_gv1", "s_st2",
                 "s_wa", "s_wd", "s_wp", "s_tl0", "s_tl1", "s_wo0", "s_wo1"]
    sems = {n: es0.enter_context(nc.semaphore(n)) for n in sem_names}
    cnt = {n: 0 for n in sem_names}

    def inc(instr, name, n=1):
        cnt[name] += n
        instr.then_inc(sems[name], n)
        return instr

    def tap(eng, name, src_ap, rows, cols):
        if name not in dbg:
            return
        inc(eng.dma_start(dbg[name][0:rows, 0:cols], src_ap), "s_db", 16)
        n_dbg_stores[0] += 1

    def col_of(t1row, n):
        return bass.AP(t1row.tensor, 0, [[1, 128], [128, n]])

    pregs = {}
    _bias_ld = [0]

    def magic_round(v_, src_, dst, t1, t2, t3):
        v_.tensor_scalar(t1, src_, C1, -C1, op0=OP.add, op1=OP.add)
        v_.tensor_scalar(t2, src_, C2, -C2, op0=OP.add, op1=OP.add)
        v_.tensor_scalar(t3, src_, C3, -C3, op0=OP.add, op1=OP.add)
        v_.drain()
        v_.tensor_tensor(t1, t1, t2, op=OP.add)
        v_.drain()
        v_.tensor_tensor(dst, t1, t3, op=OP.subtract)

    def parity_reg(e):
        en = e.engine
        if en not in pregs:
            e.wait_ge(sems["s_ld"], _bias_ld[0])
            pregs[en] = e.alloc_register(f"preg{len(pregs)}")
            e.reg_load(pregs[en], parsb[:1, :1])
        return pregs[en]

    esKV = contextlib.ExitStack()
    Kres = mk(esKV, "Kres", [128, HKV * S], BF16)
    Vres = mk(esKV, "Vres", [128, HKV * S], BF16)

    esP0 = contextlib.ExitStack()
    strip = mk(esP0, "strip", [1, 2048], F32)
    trigb = mk(esP0, "trigb", [128, 2 * 2048], BF16)
    colB = mk(esP0, "colB", [128, 64], F32)
    wmodbuf = mk(esP0, "wmodbuf", [128, 2 * 2048], BF16)
    Tg1 = mk(esP0, "Tg1", [128, 2048], F32)
    Tg2 = mk(esP0, "Tg2", [128, 2048], F32)
    Tg3 = mk(esP0, "Tg3", [128, 2048], F32)
    Tg4 = mk(esP0, "Tg4", [128, 2048], F32)
    Tg5 = mk(esP0, "Tg5", [128, 2048], F32)
    ccol = mk(esP0, "ccol", [128, EC], F32)

    MSUP = 2048 if E6 % 2048 == 0 else 1024
    n_msup = E6 // MSUP
    n_msub = MSUP // 512

    esB = contextlib.ExitStack()
    blk = esB.enter_context(nc.Block("p0", no_gpsimd_drain=True))

    @blk.gpsimd
    def _(g):
        g.iota(tA[:, :128], [[-1, 128]], base=0, channel_multiplier=1,
               allow_small_or_imprecise_dtypes=True)
        inc(g.iota(colA[:, 0:1], [[0, 1]], base=0, channel_multiplier=1,
                   allow_small_or_imprecise_dtypes=True), "s_gp")
    iota_gp = cnt["s_gp"]

    @blk.sync
    def _(s):
        inc(s.dma_start(parsb[:, :], par_in), "s_ld", 16)

    @blk.vector
    def _(v):
        v.wait_ge(sems["s_gp"], iota_gp)
        v.memset(onesb[:, :], 1.0)
        v.memset(onesf[:, :], 1.0)
        v.memset(qmax2[:, :], 0.0)
        v.memset(kmax2[:, :], 0.0)
        v.tensor_single_scalar(identb[:, :], tA[:, :128], 0.0, op=OP.is_equal)
        v.tensor_single_scalar(tB[:, :128], tA[:, :128], 64.0, op=OP.is_equal)
        v.tensor_single_scalar(tC[:, :128], tA[:, :128], -64.0, op=OP.is_equal)
        v.tensor_scalar(colA[:, 1:2], colA[:, 0:1], 64.0, None, op0=OP.is_ge)
        v.drain()
        v.tensor_tensor(swapb[:, :], tB[:, :128], tC[:, :128], op=OP.add)
        v.scalar_tensor_tensor(colA[:, 2:3], colA[:, 1:2], -64.0, colA[:, 0:1],
                               op0=OP.mult, op1=OP.add)
        v.tensor_scalar(colA[:, 3:4], colA[:, 1:2], 2.0, -1.0,
                        op0=OP.mult, op1=OP.add)
        v.drain()
        inc(v.nop(), "s_dv")
    const_dv = cnt["s_dv"]

    @blk.sync
    def _(s):
        A_ = dict(allow_slow_non_contiguous=True)
        inc(s.dma_start(biasq[:, :], col_of(bq_in, H), **A_), "s_ld", 16)
        inc(s.dma_start(biask[:, :], col_of(bk_in, HKV), **A_), "s_ld", 16)
        inc(s.dma_start(biasg[:, :], col_of(gb_in, HC), **A_), "s_ld", 16)
        inc(s.dma_start(biasvb[:, :], col_of(vb_in, HC), **A_), "s_ld", 16)
        inc(s.dma_start(biaso[:, :], col_of(ob_in, EC), **A_), "s_ld", 16)
        inc(s.dma_start(ccol[:, :], col_of(c_in, EC), **A_), "s_ld", 16)
        inc(s.dma_start(strip[:1, :KVD], bv_in), "s_ld", 16)
    bias_ld = cnt["s_ld"]
    _bias_ld[0] = bias_ld

    @blk.scalar
    def _(sc):
        sc.wait_ge(sems["s_ld"], bias_ld)
        sc.activation(scb[:, :], ccol[:, :], AF.Silu)
        sc.wait_ge(sems["s_dv"], const_dv)
        inc(sc.activation(colA[:, 4:5], colA[:, 2:3], AF.Exp,
                          scale=-math.log(cfg.theta) / (DH // 2)), "s_ac")
    silu_ac = cnt["s_ac"]

    @blk.vector
    def _(v):
        v.wait_ge(sems["s_ld"], bias_ld)
        v.memset(tE16[:, :KVD], 0.0)
        v.drain()
        inc(v.tensor_copy(tE16[:1, :KVD], strip[:1, :KVD]), "s_dv")
    bv_dv = cnt["s_dv"]

    @blk.tensor
    def _(te):
        te.wait_ge(sems["s_dv"], bv_dv)
        inc(te.matmul(pb[6][:, :KVD], onesb[:, :], tE16[:, :KVD],
                      start=True, stop=True), "s_pe")
    bv_pe = cnt["s_pe"]

    @blk.vector
    def _(v):
        v.wait_ge(sems["s_pe"], bv_pe)
        inc(v.tensor_copy(bvbank[:, :], pb[6][:, :KVD]), "s_dv")

    mm_pe0 = cnt["s_pe"]
    n_wm_jobs = n_msup * EC

    side_jobs = []
    for b0 in range(NB):
        side_jobs.append(("c_x", x16[b0 * TB:(b0 + 1) * TB, :],
                          x_in[b0 * TB:(b0 + 1) * TB, :]))
    side_jobs.append(("c_aw", wk16, wk_in))
    side_jobs.append(("c_aw", wv16, wv_in))
    for i in range(4):
        side_jobs.append(("c_aw", wq16[i * (E // 4):(i + 1) * (E // 4), :],
                          wq_in[i * (E // 4):(i + 1) * (E // 4), :]))
    for i in range(4):
        side_jobs.append(("c_aw", wo16[i * (E // 4):(i + 1) * (E // 4), :],
                          wo_in[i * (E // 4):(i + 1) * (E // 4), :]))
    NFF = 8
    for w_in_, w16_, R in ((gw_in, gw16, HID), (vw_in, vw16, HID),
                           (ow_in, ow16, E)):
        for i in range(NFF):
            side_jobs.append(("c_ff",
                              w16_[i * (R // NFF):(i + 1) * (R // NFF), :],
                              w_in_[i * (R // NFF):(i + 1) * (R // NFF), :]))

    @blk.gpsimd
    def _(g):
        si = 0
        for j in range(n_wm_jobs):
            ms, c = divmod(j, EC)
            if j >= 2:
                g.wait_ge(sems["s_pe"], mm_pe0 + (j - 1))
            buf = (j % 2) * MSUP
            inc(g.dma_start(wmodbuf[:, buf:buf + MSUP],
                            wmod_in[c * 128:(c + 1) * 128,
                                    ms * MSUP:(ms + 1) * MSUP]),
                "c_wm" if j % 2 == 0 else "c_wm2", 16)
            while si * n_wm_jobs < (j + 1) * len(side_jobs):
                sj = side_jobs[si]
                inc(g.dma_start(sj[1], sj[2]), sj[0], 16)
                si += 1
        while si < len(side_jobs):
            sj = side_jobs[si]
            inc(g.dma_start(sj[1], sj[2]), sj[0], 16)
            si += 1
    cx_total = cnt["c_x"]
    caw_total = cnt["c_aw"]
    cff_total = cnt["c_ff"]
    caw_wkv = caw_total
    caw_wq = caw_total
    caw_wo = caw_total

    for ms in range(n_msup):
        @blk.tensor
        def _(te, ms=ms):
            te.wait_ge(sems["s_ac"], silu_ac)
            if ms > 0:
                te.wait_ge(sems["s_dv"], cnt["s_dv"])
            for c in range(EC):
                j = ms * EC + c
                te.wait_ge(sems["c_wm" if j % 2 == 0 else "c_wm2"],
                           16 * (j // 2 + 1))
                buf = (j % 2) * MSUP
                for msub in range(n_msub):
                    m_ = te.matmul(
                        pb[msub][:1, :], scb[:, c:c + 1],
                        wmodbuf[:, buf + msub * 512:buf + (msub + 1) * 512],
                        start=(c == 0), stop=(c == EC - 1))
                if True:
                    inc(m_, "s_pe")

        @blk.vector
        def _(v, ms=ms):
            v.wait_ge(sems["s_pe"], cnt["s_pe"])
            if ms > 0:
                v.wait_ge(sems["s_st"], 16 * ms)
            for msub in range(n_msub):
                v.tensor_copy(strip[:, msub * 512:(msub + 1) * 512],
                              pb[msub][:1, :])
            v.drain()
            inc(v.nop(), "s_dv")

        @blk.scalar
        def _(sc, ms=ms):
            sc.wait_ge(sems["s_dv"], cnt["s_dv"])
            inc(sc.dma_start(modscr[:, ms * MSUP:(ms + 1) * MSUP],
                             strip[:, :MSUP]), "s_st", 16)

    @blk.sync
    def _(s):
        s.wait_ge(sems["s_st"], 16 * n_msup)
        inc(s.dma_start(modc[:, :], col_of(modscr, 6 * EC),
                        allow_slow_non_contiguous=True), "s_ld", 16)
    modc_ld = cnt["s_ld"]

    @blk.scalar
    def _(sc):
        sc.wait_ge(sems["s_ld"], modc_ld)
        sc.activation(tB[:, 0:EC], modc[:, 2 * EC:3 * EC], AF.Sign)
        sc.activation(tB[:, EC:2 * EC], modc[:, 5 * EC:6 * EC], AF.Sign)
        sc.activation(tD[:, 0:EC], modc[:, 2 * EC:3 * EC], AF.Abs)
        inc(sc.activation(tD[:, EC:2 * EC], modc[:, 5 * EC:6 * EC], AF.Abs),
            "s_ac")
    sign_ac = cnt["s_ac"]

    @blk.vector
    def _(v):
        v.wait_ge(sems["s_ac"], sign_ac)
        v.tensor_scalar(tC[:, 0:EC], tD[:, 0:EC], 0.25, None, op0=OP.max)
        v.tensor_scalar(tC[:, EC:2 * EC], tD[:, EC:2 * EC], 0.25, None,
                        op0=OP.max)
        v.drain()
        v.tensor_tensor(modc[:, 2 * EC:3 * EC], tB[:, 0:EC], tC[:, 0:EC],
                        op=OP.mult)
        v.tensor_tensor(modc[:, 5 * EC:6 * EC], tB[:, EC:2 * EC],
                        tC[:, EC:2 * EC], op=OP.mult)
        v.drain()
        v.tensor_scalar(tC[:, 0:EC], modc[:, EC:2 * EC], 1.0, None, op0=OP.add)
        v.tensor_scalar(tC[:, EC:2 * EC], modc[:, 4 * EC:5 * EC], 1.0, None,
                        op0=OP.add)
        v.drain()
        v.tensor_copy(modc[:, EC:2 * EC], tC[:, 0:EC])
        v.tensor_copy(modc[:, 4 * EC:5 * EC], tC[:, EC:2 * EC])
        v.drain()
        inc(v.nop(), "s_dv")
    mod_done_dv = cnt["s_dv"]

    def cody3(v, k_ap, ang_ap, o1, o2, o3):
        v.scalar_tensor_tensor(o1, k_ap, -TP1, ang_ap, op0=OP.mult, op1=OP.add)
        v.drain()
        v.scalar_tensor_tensor(o2, k_ap, -TP2, o1, op0=OP.mult, op1=OP.add)
        v.drain()
        v.scalar_tensor_tensor(o3, k_ap, -TP3, o2, op0=OP.mult, op1=OP.add)
        v.drain()

    def ladder_round(v, a_ap, u_ap, k_ap, bit_ap, tmp_ap, nbits):
        v.tensor_copy(u_ap, a_ap)
        v.memset(k_ap, 0.0)
        v.drain()
        for j in range(nbits - 1, -1, -1):
            Bj = float(np.float32((1 << j) * 2 * math.pi))
            v.tensor_scalar(bit_ap, u_ap, Bj, None, op0=OP.is_ge)
            v.drain()
            v.scalar_tensor_tensor(tmp_ap, bit_ap, -Bj, u_ap,
                                   op0=OP.mult, op1=OP.add)
            v.drain()
            v.tensor_copy(u_ap, tmp_ap)
            v.drain()
            v.scalar_tensor_tensor(tmp_ap, bit_ap, float(1 << j), k_ap,
                                   op0=OP.mult, op1=OP.add)
            v.drain()
            v.tensor_copy(k_ap, tmp_ap)
            v.drain()
        v.tensor_scalar(bit_ap, u_ap, PI_F, None, op0=OP.is_ge)
        v.drain()
        v.tensor_tensor(tmp_ap, k_ap, bit_ap, op=OP.add)
        v.drain()
        v.tensor_copy(k_ap, tmp_ap)
        v.drain()

    def wrap_cos3(v, m_ap, c_ap, t1_ap, out_ap):
        v.tensor_scalar(t1_ap, m_ap, HALF_PI, None, op0=OP.add)
        v.tensor_scalar(c_ap, m_ap, HALF_PI, None, op0=OP.is_gt)
        v.drain()
        v.scalar_tensor_tensor(out_ap, c_ap, -TP1, t1_ap,
                               op0=OP.mult, op1=OP.add)
        v.drain()
        v.scalar_tensor_tensor(t1_ap, c_ap, -TP2, out_ap,
                               op0=OP.mult, op1=OP.add)
        v.drain()
        v.scalar_tensor_tensor(out_ap, c_ap, -TP3, t1_ap,
                               op0=OP.mult, op1=OP.add)
        v.drain()
        v.tensor_scalar(out_ap, out_ap, PI_F, -PI_F, op0=OP.min, op1=OP.max)
        v.drain()

    @blk.gpsimd
    def _(g, w=cnt["s_dv"]):
        g.wait_ge(sems["s_dv"], w)
        inc(g.iota(Tg1[:, :TB], [[1, TB]], base=0, channel_multiplier=0,
                   allow_small_or_imprecise_dtypes=True), "s_gp")

    @blk.vector
    def _(v, w=cnt["s_gp"]):
        v.wait_ge(sems["s_gp"], w)
        v.wait_ge(sems["s_ac"], silu_ac)
        v.tensor_scalar(Tg3[:, :TB], Tg1[:, :TB], colA[:, 4:5], None,
                        op0=OP.mult)
        v.drain()
        ladder_round(v, Tg3[:, :TB], Tg1[:, :TB], Tg4[:, :TB],
                     Tg2[:, :TB], Tg5[:, :TB], 7)
        cody3(v, Tg4[:, :TB], Tg3[:, :TB], Tg1[:, :TB], Tg2[:, :TB],
              Tg5[:, :TB])
        v.tensor_scalar(Tg4[:, :TB], Tg5[:, :TB], PI_F, -PI_F,
                        op0=OP.min, op1=OP.max)
        v.drain()
        wrap_cos3(v, Tg4[:, :TB], Tg3[:, :TB], Tg5[:, :TB], Tg1[:, :TB])
        inc(v.nop(), "s_dv")

    @blk.scalar
    def _(sc, w=cnt["s_dv"]):
        sc.wait_ge(sems["s_dv"], w)
        sc.activation(Tg2[:, :TB], Tg4[:, :TB], AF.Sin)
        inc(sc.activation(Tg5[:, :TB], Tg1[:, :TB], AF.Sin), "s_ac")
    lo_ac = cnt["s_ac"]

    n_tch = S // 2048
    n_tb = 2048 // TB
    CB = lambda i: colB[:, i * n_tb:(i + 1) * n_tb]
    for ch in range(n_tch):
        @blk.gpsimd
        def _(g, ch=ch, w=cnt["s_st"], w2=cnt["s_ac"]):
            g.wait_ge(sems["s_st"], w)
            g.wait_ge(sems["s_ac"], w2)
            inc(g.iota(CB(0), [[512, n_tb]], base=2048 * ch,
                       channel_multiplier=0,
                       allow_small_or_imprecise_dtypes=True), "s_gp")

        @blk.vector
        def _(v, ch=ch, w=cnt["s_gp"], wl=lo_ac):
            v.wait_ge(sems["s_gp"], w)
            v.wait_ge(sems["s_ac"], wl)
            v.tensor_scalar(CB(4), CB(0), colA[:, 4:5], None, op0=OP.mult)
            v.drain()
            v.tensor_copy(CB(0), CB(4))
            v.drain()
            ladder_round(v, CB(0), CB(1), CB(2), CB(3), CB(4), 10)
            cody3(v, CB(2), CB(0), CB(1), CB(3), CB(5))
            v.tensor_scalar(CB(5), CB(5), PI_F, -PI_F, op0=OP.min, op1=OP.max)
            v.drain()
            wrap_cos3(v, CB(5), CB(3), CB(4), CB(11))
            inc(v.nop(), "s_dv")

        @blk.scalar
        def _(sc, w=cnt["s_dv"]):
            sc.wait_ge(sems["s_dv"], w)
            sc.activation(CB(6), CB(5), AF.Sin)
            inc(sc.activation(CB(7), CB(11), AF.Sin), "s_ac")

        @blk.vector
        def _(v, ch=ch, w=cnt["s_ac"]):
            v.wait_ge(sems["s_ac"], w)
            v.tensor_scalar(CB(8), CB(6), -1.0, None, op0=OP.mult)
            v.tensor_scalar(CB(9), CB(6), colA[:, 3:4], None, op0=OP.mult)
            v.tensor_scalar(CB(10), CB(7), colA[:, 3:4], None, op0=OP.mult)
            v.drain()
            for j in range(n_tb):
                sl_c = slice(j * TB, (j + 1) * TB)
                v.tensor_scalar(Tg3[:, :TB], Tg5[:, :TB],
                                colB[:, 7 * n_tb + j:7 * n_tb + j + 1],
                                None, op0=OP.mult)
                v.drain()
                v.scalar_tensor_tensor(
                    trigb[:, j * TB:(j + 1) * TB], Tg2[:, :TB],
                    colB[:, 8 * n_tb + j:8 * n_tb + j + 1], Tg3[:, :TB],
                    op0=OP.mult, op1=OP.add)
                v.tensor_scalar(Tg4[:, :TB], Tg5[:, :TB],
                                colB[:, 9 * n_tb + j:9 * n_tb + j + 1],
                                None, op0=OP.mult)
                v.drain()
                v.scalar_tensor_tensor(
                    trigb[:, 2048 + j * TB:2048 + (j + 1) * TB], Tg2[:, :TB],
                    colB[:, 10 * n_tb + j:10 * n_tb + j + 1], Tg4[:, :TB],
                    op0=OP.mult, op1=OP.add)
                v.drain()
            inc(v.nop(), "s_dv")

        @blk.scalar
        def _(sc, ch=ch, w=cnt["s_dv"]):
            sc.wait_ge(sems["s_dv"], w)
            inc(sc.dma_start(trigscr[:, ch * 2048:(ch + 1) * 2048],
                             trigb[:, 0:2048]), "s_st", 16)
            inc(sc.dma_start(trigscr[:, S + ch * 2048:S + (ch + 1) * 2048],
                             trigb[:, 2048:4096]), "s_st", 16)
    trig_st = cnt["s_st"]

    if "modc" in dbg:
        @blk.scalar
        def _(sc):
            sc.wait_ge(sems["s_dv"], mod_done_dv)
            tap(sc, "modc", modc[:, :], 128, 6 * EC)


    esB.close()
    esP0.close()

    esP1 = contextlib.ExitStack()
    xt = mk(esP1, "xt", [128, 2 * EC * TB], BF16)
    h1t = mk(esP1, "h1t", [128, EC * TB], BF16)
    wkv = mk(esP1, "wkv", [128, EC * 2 * KVD], BF16)
    ktrig = mk(esP1, "ktrig", [128, 2 * TB], BF16)

    esB = contextlib.ExitStack()
    blk = esB.enter_context(nc.Block("p1a", no_gpsimd_drain=True))

    my_par = {p: cfg.my_blocks(p) for p in (0, 1)}
    h1_stores = {0: 0, 1: 0}
    dv_end = {}
    pe_end = {}

    for m in range(NB):
        bo = (m % 2) * EC * TB

        @blk.sync
        def _(s, m=m, bo=bo):
            s.wait_ge(sems["c_x"], cx_total)
            if m >= 2:
                s.wait_ge(sems["s_dv"], dv_end[m - 2])
            if m == 0:
                s.wait_ge(sems["s_st"], trig_st)
                s.wait_ge(sems["c_aw"], caw_wkv)
                for c in range(EC):
                    inc(s.dma_start_transpose(
                        wkv[:, c * 2 * KVD:c * 2 * KVD + KVD],
                        wk16[0:KVD, c * 128:(c + 1) * 128]), "s_wt", 16)
                    inc(s.dma_start_transpose(
                        wkv[:, c * 2 * KVD + KVD:(c + 1) * 2 * KVD],
                        wv16[0:KVD, c * 128:(c + 1) * 128]), "s_wt", 16)
            for c in range(EC):
                inc(s.dma_start_transpose(
                    xt[:, bo + c * TB:bo + (c + 1) * TB],
                    x16[m * TB:(m + 1) * TB, c * 128:(c + 1) * 128]),
                    "s_x0" if m % 2 == 0 else "s_x1", 16)
            if m > 0:
                s.wait_ge(sems["s_dv"], dv_end[m - 1])
            inc(s.dma_start(ktrig[:, 0:TB],
                            trigscr[:, m * TB:(m + 1) * TB]), "s_wt", 16)
            inc(s.dma_start(ktrig[:, TB:2 * TB],
                            trigscr[:, S + m * TB:S + (m + 1) * TB]),
                "s_wt", 16)
        xsem = "s_x0" if m % 2 == 0 else "s_x1"
        xld = cnt[xsem]
        ktrig_wt = cnt["s_wt"]

        @blk.scalar
        def _(sc, m=m, bo=bo, xld=xld, xsem=xsem):
            sc.wait_ge(sems[xsem], xld)
            if m > 0:
                sc.wait_ge(sems["s_pe"], pe_end[m - 1])
                reg = parity_reg(sc)
                with sc.If_eq(reg, 0):
                    sc.wait_ge(sems["s_h1"], 16 * h1_stores[0])
                with sc.Else():
                    sc.wait_ge(sems["s_h1"], 16 * h1_stores[1])
            for c in range(EC):
                a = sc.activation(h1t[:, c * TB:(c + 1) * TB],
                                  xt[:, bo + c * TB:bo + (c + 1) * TB],
                                  AF.Square)
            inc(a, "s_ac")
        sq_ac = cnt["s_ac"]

        @blk.tensor
        def _(te, m=m, bo=bo, sq_ac=sq_ac):
            te.wait_ge(sems["s_ac"], sq_ac)
            if m > 0:
                te.wait_ge(sems["s_dv"], dv_end[m - 1])
            for c in range(EC):
                mm = te.matmul(pb[0][:, :], onesb[:, :],
                               xt[:, bo + c * TB:bo + (c + 1) * TB],
                               start=(c == 0), stop=(c == EC - 1))
            inc(mm, "s_pe")
            for c in range(EC):
                mm = te.matmul(pb[1][:, :], onesb[:, :],
                               h1t[:, c * TB:(c + 1) * TB],
                               start=(c == 0), stop=(c == EC - 1))
            inc(mm, "s_pe")
        sums_pe = cnt["s_pe"]

        @blk.vector
        def _(v, sums_pe=sums_pe):
            v.wait_ge(sems["s_pe"], sums_pe)
            v.tensor_scalar(tA[:, :], pb[0][:, :], 1.0 / E, None, op0=OP.mult)
            v.drain()
            v.tensor_tensor(tB[:, :], tA[:, :], tA[:, :], op=OP.mult)
            v.drain()
            v.scalar_tensor_tensor(tC[:, :], pb[1][:, :], 1.0 / E, tB[:, :],
                                   op0=OP.mult, op1=OP.subtract)
            v.drain()
            inc(v.tensor_scalar(tB[:, :], tC[:, :], cfg.eps, None, op0=OP.add),
                "s_dv")
        var_dv = cnt["s_dv"]

        @blk.scalar
        def _(sc, var_dv=var_dv):
            sc.wait_ge(sems["s_dv"], var_dv)
            inc(sc.activation(tC[:, :], tB[:, :], AF.Sqrt), "s_ac")
        sqrt_ac = cnt["s_ac"]

        @blk.vector
        def _(v, m=m, bo=bo, sqrt_ac=sqrt_ac):
            v.wait_ge(sems["s_ac"], sqrt_ac)
            v.reciprocal(tB[:, :], tC[:, :])
            v.drain()
            for c in range(EC):
                v.tensor_tensor(tC[:, :], xt[:, bo + c * TB:bo + (c + 1) * TB],
                                tA[:, :], op=OP.subtract)
                v.drain()
                v.tensor_tensor(tD[:, :], tC[:, :], tB[:, :], op=OP.mult)
                v.drain()
                v.tensor_scalar(h1t[:, c * TB:(c + 1) * TB], tD[:, :],
                                modc[:, EC + c:EC + c + 1],
                                modc[:, c:c + 1], op0=OP.mult, op1=OP.add)
                v.drain()
            inc(v.nop(), "s_dv")
        h1_dv = cnt["s_dv"]

        @blk.vector
        def _(v, m=m, bo=bo):
            reg = parity_reg(v)
            for p in (0, 1):
                ctxm = v.If_eq(reg, 0) if p == 0 else v.Else()
                with ctxm:
                    if m in my_par[p]:
                        gslot = my_par[p].index(m)
                        for c in range(EC):
                            v.tensor_copy(
                                xres[:, c * MYT + gslot * TB:
                                     c * MYT + (gslot + 1) * TB],
                                xt[:, bo + c * TB:bo + (c + 1) * TB])
                        v.drain()
            inc(v.nop(), "s_dv")

        @blk.scalar
        def _(sc, m=m, h1_dv=h1_dv):
            sc.wait_ge(sems["s_dv"], h1_dv)
            reg = parity_reg(sc)
            for p in (0, 1):
                ctxm = sc.If_eq(reg, 0) if p == 0 else sc.Else()
                with ctxm:
                    if m in my_par[p]:
                        gslot = my_par[p].index(m)
                        sc.dma_start(
                            h1scr[:, gslot * EC * TB:(gslot + 1) * EC * TB],
                            h1t[:, :]).then_inc(sems["s_h1"], 16)
                        h1_stores[p] += 1

        k_chain_pe = {}
        v_chain_pe = {}

        @blk.tensor
        def _(te, m=m, h1_dv=h1_dv, ktw=ktrig_wt):
            te.wait_ge(sems["s_dv"], h1_dv)
            if m == 0:
                te.wait_ge(sems["s_wt"], ktw)
            for kv in range(HKV):
                for c in range(EC):
                    mm = te.matmul(
                        pb[kv][:, :],
                        wkv[:, c * 2 * KVD + kv * DH:c * 2 * KVD + (kv + 1) * DH],
                        h1t[:, c * TB:(c + 1) * TB],
                        start=(c == 0), stop=(c == EC - 1))
                inc(mm, "s_pe")
                k_chain_pe[kv] = cnt["s_pe"]
            for tc in range(2):
                for c in range(EC):
                    mm = te.matmul(
                        pb[2 + tc][:, :KVD],
                        h1t[:, c * TB + tc * 128:c * TB + (tc + 1) * 128],
                        wkv[:, c * 2 * KVD + KVD:(c + 1) * 2 * KVD],
                        start=(c == 0), stop=(c == EC - 1))
                inc(mm, "s_pe")
                v_chain_pe[tc] = cnt["s_pe"]

        def emit_krope(kv):
            @blk.vector
            def _(v, m=m, kv=kv, w=cnt["s_pe"]):
                v.wait_ge(sems["s_pe"], w)
                inc(v.tensor_scalar(tE16[:, :], pb[kv][:, :],
                                    biask[:, kv:kv + 1], None, op0=OP.add),
                    "s_dv")
            ksb_dv = cnt["s_dv"]

            @blk.tensor
            def _(te, ksb_dv=ksb_dv):
                te.wait_ge(sems["s_dv"], ksb_dv)
                inc(te.matmul(pb[4][:, :], swapb[:, :], tE16[:, :],
                              start=True, stop=True), "s_pe")
            swp_pe = cnt["s_pe"]

            @blk.vector
            def _(v, m=m, kv=kv, swp_pe=swp_pe, ktw=ktrig_wt):
                v.wait_ge(sems["s_wt"], ktw)
                v.tensor_tensor(tA[:, :], tE16[:, :],
                                ktrig[:, 0:TB], op=OP.mult)
                v.wait_ge(sems["s_pe"], swp_pe)
                v.tensor_tensor(tC[:, :], pb[4][:, :],
                                ktrig[:, TB:2 * TB], op=OP.mult)
                v.drain()
                inc(v.tensor_tensor(
                    Kres[:, kv * S + m * TB:kv * S + (m + 1) * TB],
                    tA[:, :], tC[:, :], op=OP.add), "s_dv")
            krope_dv = cnt["s_dv"]

            @blk.scalar
            def _(sc, m=m, kv=kv, krope_dv=krope_dv):
                sc.wait_ge(sems["s_dv"], krope_dv)
                inc(sc.activation(
                    tF16[:, :],
                    Kres[:, kv * S + m * TB:kv * S + (m + 1) * TB],
                    AF.Square), "s_ac")
            ksq_ac = cnt["s_ac"]

            @blk.tensor
            def _(te, ksq_ac=ksq_ac):
                te.wait_ge(sems["s_ac"], ksq_ac)
                inc(te.matmul(pb[5][:, :], onesb[:, :], tF16[:, :],
                              start=True, stop=True), "s_pe")
            ksum_pe = cnt["s_pe"]

            @blk.vector
            def _(v, kv=kv, ksum_pe=ksum_pe):
                v.wait_ge(sems["s_pe"], ksum_pe)
                v.tensor_reduce(colA[:, 5:6], pb[5][:, :], axis=AX.X,
                                op=OP.max)
                v.drain()
                v.tensor_tensor(colA[:, 6:7], kmax2[:, kv:kv + 1],
                                colA[:, 5:6], op=OP.max)
                v.drain()
                v.tensor_copy(kmax2[:, kv:kv + 1], colA[:, 6:7])
                v.drain()
                inc(v.nop(), "s_dv")

        def emit_vevac(tc):
            @blk.vector
            def _(v, m=m, tc=tc, vpe=cnt["s_pe"]):
                v.wait_ge(sems["s_pe"], vpe)
                v.drain()
                kt = 4 * m + tc
                v.tensor_tensor(tF16[:, :KVD], pb[2 + tc % 2][:, :KVD],
                                bvbank[:, :], op=OP.add)
                v.drain()
                v.tensor_copy(Vres[:, kt * 128:(kt + 1) * 128],
                              tF16[:, 0:DH])
                inc(v.tensor_copy(Vres[:, S + kt * 128:S + (kt + 1) * 128],
                                  tF16[:, DH:KVD]), "s_dv")

        emit_krope(0)
        emit_vevac(0)
        emit_krope(1)
        emit_vevac(1)
        evac01_dv = cnt["s_dv"]

        @blk.tensor
        def _(te, m=m, evac01_dv=evac01_dv):
            te.wait_ge(sems["s_dv"], evac01_dv)
            for tc in range(2, 4):
                for c in range(EC):
                    mm = te.matmul(
                        pb[2 + tc % 2][:, :KVD],
                        h1t[:, c * TB + tc * 128:c * TB + (tc + 1) * 128],
                        wkv[:, c * 2 * KVD + KVD:(c + 1) * 2 * KVD],
                        start=(c == 0), stop=(c == EC - 1))
                inc(mm, "s_pe")
                v_chain_pe[tc] = cnt["s_pe"]
        pe_end[m] = cnt["s_pe"]

        emit_vevac(2)
        emit_vevac(3)
        dv_end[m] = cnt["s_dv"]

    esB.close()

    if "h1scr" in dbg or "Kres" in dbg:
        esB = contextlib.ExitStack()
        blk = esB.enter_context(nc.Block("p1dbg", no_gpsimd_drain=True))

        @blk.scalar
        def _(sc):
            tap(sc, "Kres", Kres[:, :], 128, HKV * S)
            tap(sc, "Vres", Vres[:, :], 128, HKV * S)
            tap(sc, "xres", xres[:, :], 128, EC * MYT)
        esB.close()

    esP1.close()
    Qres = mk(esKV, "Qres", [128, H * MYT], BF16)
    esP1b = contextlib.ExitStack()
    h1tb = mk(esP1b, "h1tb", [128, EC * TB], BF16)
    wqb = mk(esP1b, "wqb", [128, 2 * TB], BF16)
    qtrig = mk(esP1b, "qtrig", [128, 2 * TB], BF16)

    esB = contextlib.ExitStack()
    blk = esB.enter_context(nc.Block("p1b", no_gpsimd_drain=True))

    for g in range(MYNB):
        @blk.sync
        def _(s, g=g, w=cnt["s_pe"], wdv=cnt["s_dv"]):
            s.wait_ge(sems["s_h1"], 16 * MYNB)
            if g > 0:
                s.wait_ge(sems["s_pe"], w)
                s.wait_ge(sems["s_dv"], wdv)
            inc(s.dma_start(h1tb[:, :],
                            h1scr[:, g * EC * TB:(g + 1) * EC * TB]),
                "s_ld", 16)
            r = parity_reg(s)
            for p in (0, 1):
                ctxm = s.If_eq(r, 0) if p == 0 else s.Else()
                mb = my_par[p][g]
                with ctxm:
                    s.dma_start(qtrig[:, 0:TB],
                                trigscr[:, mb * TB:(mb + 1) * TB]
                                ).then_inc(sems["s_qt"], 16)
                    s.dma_start(qtrig[:, TB:2 * TB],
                                trigscr[:, S + mb * TB:S + (mb + 1) * TB]
                                ).then_inc(sems["s_qt"], 16)
            cnt["s_qt"] += 32
        h1_ld = cnt["s_ld"]
        qtrig_wt = cnt["s_qt"]

        for qg in range(H // 4):
            pe_base = cnt["s_pe"]

            for d in range(EC):
                @blk.sync
                def _(s, g=g, qg=qg, d=d, pe_base=pe_base, w=cnt["s_dv"]):
                    if d == 0:
                        s.wait_ge(sems["c_aw"], caw_wq)
                        s.wait_ge(sems["s_dv"], w)
                    if d < 2:
                        s.wait_ge(sems["s_pe"], pe_base)
                    else:
                        s.wait_ge(sems["s_pe"], pe_base + d - 1)
                    inc(s.dma_start_transpose(
                        wqb[:, (d % 2) * TB:(d % 2 + 1) * TB],
                        wq16[qg * 512:(qg + 1) * 512, d * 128:(d + 1) * 128]),
                        "s_qw" if d % 2 == 0 else "s_wt", 16)

            qw_even = cnt["s_qw"]
            qw_odd = cnt["s_wt"]

            @blk.tensor
            def _(te, g=g, qg=qg, h1_ld=h1_ld, pe_base=pe_base,
                  qw_even=qw_even, qw_odd=qw_odd, w=cnt["s_dv"]):
                te.wait_ge(sems["s_ld"], h1_ld)
                te.wait_ge(sems["s_dv"], w)
                for d in range(EC):
                    if d % 2 == 0:
                        te.wait_ge(sems["s_qw"],
                                   qw_even - 16 * ((EC + 1) // 2 - d // 2 - 1))
                    else:
                        te.wait_ge(sems["s_wt"],
                                   qw_odd - 16 * (EC // 2 - d // 2 - 1))
                    for hh in range(4):
                        mm = te.matmul(
                            pb[hh][:, :],
                            wqb[:, (d % 2) * TB + hh * 128:
                                (d % 2) * TB + (hh + 1) * 128],
                            h1tb[:, d * TB:(d + 1) * TB],
                            start=(d == 0), stop=(d == EC - 1))
                    inc(mm, "s_pe")

            for hh in range(4):
                h = qg * 4 + hh
                kv = h // GQ
                qmm_pe = cnt["s_pe"]

                @blk.vector
                def _(v, h=h, hh=hh, qmm_pe=qmm_pe):
                    v.wait_ge(sems["s_pe"], qmm_pe)
                    inc(v.tensor_scalar(tE16[:, :], pb[hh][:, :],
                                        biasq[:, h:h + 1], None, op0=OP.add),
                        "s_dv")
                qsb_dv = cnt["s_dv"]

                @blk.tensor
                def _(te, qsb_dv=qsb_dv):
                    te.wait_ge(sems["s_dv"], qsb_dv)
                    inc(te.matmul(pb[4][:, :], swapb[:, :], tE16[:, :],
                                  start=True, stop=True), "s_pe")
                qswp_pe = cnt["s_pe"]

                @blk.vector
                def _(v, g=g, h=h, qswp_pe=qswp_pe, qtw=qtrig_wt):
                    v.wait_ge(sems["s_qt"], qtw)
                    v.tensor_tensor(tA[:, :], tE16[:, :], qtrig[:, 0:TB],
                                    op=OP.mult)
                    v.wait_ge(sems["s_pe"], qswp_pe)
                    v.tensor_tensor(tC[:, :], pb[4][:, :], qtrig[:, TB:2 * TB],
                                    op=OP.mult)
                    v.drain()
                    v.tensor_tensor(
                        Qres[:, h * MYT + g * TB:h * MYT + (g + 1) * TB],
                        tA[:, :], tC[:, :], op=OP.add)
                    v.drain()
                    inc(v.nop(), "s_dv")
                qrope_dv = cnt["s_dv"]

                @blk.scalar
                def _(sc, g=g, h=h, qrope_dv=qrope_dv):
                    sc.wait_ge(sems["s_dv"], qrope_dv)
                    inc(sc.activation(
                        tF16[:, :],
                        Qres[:, h * MYT + g * TB:h * MYT + (g + 1) * TB],
                        AF.Square), "s_ac")
                qsq_ac = cnt["s_ac"]

                @blk.tensor
                def _(te, qsq_ac=qsq_ac):
                    te.wait_ge(sems["s_ac"], qsq_ac)
                    inc(te.matmul(pb[5][:, :], onesb[:, :], tF16[:, :],
                                  start=True, stop=True), "s_pe")
                qsum_pe = cnt["s_pe"]

                @blk.vector
                def _(v, h=h, qsum_pe=qsum_pe):
                    v.wait_ge(sems["s_pe"], qsum_pe)
                    v.tensor_reduce(colA[:, 5:6], pb[5][:, :], axis=AX.X,
                                    op=OP.max)
                    v.drain()
                    v.tensor_tensor(colA[:, 6:7], qmax2[:, h:h + 1],
                                    colA[:, 5:6], op=OP.max)
                    v.drain()
                    v.tensor_copy(qmax2[:, h:h + 1], colA[:, 6:7])
                    v.drain()
                    inc(v.nop(), "s_dv")

    @blk.vector
    def _(v, w=cnt["s_dv"]):
        v.wait_ge(sems["s_dv"], w)
        for h in range(H):
            v.tensor_tensor(tB[:, h:h + 1], qmax2[:, h:h + 1],
                            kmax2[:, h // GQ:h // GQ + 1], op=OP.mult)
        v.drain()
        inc(v.nop(), "s_dv")
    nb_dv = cnt["s_dv"]

    @blk.scalar
    def _(sc, nb_dv=nb_dv):
        sc.wait_ge(sems["s_dv"], nb_dv)
        inc(sc.activation(tC[:, 0:H], tB[:, 0:H], AF.Sqrt, scale=1.0 / 128.0),
            "s_ac")
    nb_ac = cnt["s_ac"]

    @blk.vector
    def _(v, nb_ac=nb_ac):
        v.wait_ge(sems["s_ac"], nb_ac)
        v.tensor_scalar(negB[:, :], tC[:, 0:H], -1.0, None, op0=OP.mult)
        v.drain()
        inc(v.nop(), "s_dv")

    if "Qres" in dbg:
        @blk.scalar
        def _(sc, w=cnt["s_dv"]):
            sc.wait_ge(sems["s_dv"], w)
            tap(sc, "Qres", Qres[:, :], 128, H * MYT)
            tap(sc, "negB", negB[:, :], 128, H)

    esB.close()
    esP1b.close()

    esP2w = contextlib.ExitStack()
    PBW = min(512, E)
    PCW = 512
    wtb = mk(esP2w, "wtb", [128, 2 * PBW], F16)
    wtc = mk(esP2w, "wtc", [128, 2 * PCW], F16)
    tqc = mk(esP2w, "tqc", [128, 2 * PCW], BF16)
    ttmp = mk(esP2w, "ttmp", [128, PCW], BF16)

    esP2 = contextlib.ExitStack()
    wob = mk(esP2, "wob", [128, 2048], BF16)
    attst = mk(esP2, "attst", [128, H * TB], BF16)
    ptile = mk(esP2, "ptile", [128, 2 * TB], BF16)
    rsum = tC
    masks = mk(esP2, "masks", [128, 4 * TB], BF16)

    esB = contextlib.ExitStack()
    blk = esB.enter_context(nc.Block("p2a", no_gpsimd_drain=True))

    for r in range(4):
        @blk.gpsimd
        def _(g, r=r, w=cnt["s_dv"]):
            g.wait_ge(sems["s_dv"], w)
            inc(g.iota(tC[:, :TB], [[1, TB]], base=-128 * r,
                       channel_multiplier=-1,
                       allow_small_or_imprecise_dtypes=True), "s_gp")

        @blk.vector
        def _(v, r=r, w=cnt["s_gp"]):
            v.wait_ge(sems["s_gp"], w)
            v.drain()
            inc(v.tensor_scalar(masks[:, r * TB:(r + 1) * TB], tC[:, :TB],
                                0.0, NEGBIG, op0=OP.min, op1=OP.mult), "s_dv")

    cnt0 = dict(cnt)
    cnt1 = dict(cnt)
    CN = {0: cnt0, 1: cnt1}

    def binc(p, instr, name, n=1):
        CN[p][name] += n
        instr.then_inc(sems[name], n)

    def uinc(instr, name, n=1):
        CN[0][name] += n
        CN[1][name] += n
        instr.then_inc(sems[name], n)

    W3 = [(gw16, tg8, 0), (vw16, tv8, 1), (ow16, to8, 2)]
    gtot = HID * E

    @blk.vector
    def _(v):
        v.memset(tD[:, 0:3], 0.0)
        v.drain()
        inc(v.nop(), "s_wd")

    wa_marks, wd_marks, wd2_marks = [], [], []
    wo_q_marks, wo_qctr = [], [0]
    side_jobs = []

    def mk_p2b_tile(src_ap, mi, tg, wid):
        buf = tg % 2
        lsem = "s_ld" if buf == 0 else "s_l2"

        def emit():
            @blk.sync
            def _(s, war=(wd_marks[tg - 2] if tg >= 2 else None)):
                s.wait_ge(sems["c_ff"], cff_total)
                if war is not None:
                    s.wait_ge(sems["s_wd"], war)
                inc(s.dma_start(wtb[:, buf * PBW:buf * PBW + wid], src_ap),
                    lsem, 16)
            ldm = cnt[lsem]

            @blk.vector
            def _(v, ldm=ldm):
                v.wait_ge(sems[lsem], ldm)
                v.tensor_reduce(colA[:, 5 + buf:6 + buf],
                                wtb[:, buf * PBW:buf * PBW + wid],
                                axis=AX.X, op=OP.add,
                                apply_absolute_value=True)
                v.drain()
                v.tensor_tensor(tD[:, mi:mi + 1], tD[:, mi:mi + 1],
                                colA[:, 5 + buf:6 + buf], op=OP.add)
                v.drain()
                inc(v.nop(), "s_wd")
            wd_marks.append(cnt["s_wd"])
        return emit

    def mk_gcols(mi):
        def emit():
            @blk.tensor
            def _(te, w=cnt["s_wd"]):
                te.wait_ge(sems["s_wd"], w)
                inc(te.matmul(pb[6][:, 0:1], onesf[:, :], tD[:, mi:mi + 1],
                              start=True, stop=True), "s_wp")
            wpm = cnt["s_wp"]

            @blk.vector
            def _(v, wpm=wpm):
                v.wait_ge(sems["s_wp"], wpm)
                v.tensor_scalar(gcols[:, 3 + mi:4 + mi], pb[6][:, 0:1],
                                1.0 / gtot, 1e-5, op0=OP.mult, op1=OP.max)
                v.drain()
                v.tensor_scalar(gcols[:, mi:mi + 1],
                                gcols[:, 3 + mi:4 + mi], 0.5, None,
                                op0=OP.mult)
                v.tensor_scalar(gcols[:, 6 + mi:7 + mi],
                                gcols[:, 3 + mi:4 + mi], -0.5, None,
                                op0=OP.mult)
                v.tensor_scalar(gcols[:, 9 + mi:10 + mi],
                                gcols[:, 3 + mi:4 + mi], -1.0, None,
                                op0=OP.mult)
                v.drain()
                inc(v.nop(), "s_wd")
        return emit

    def mk_p2c_tile(job, tg2):
        w16_, t8_, mi, a0, colbase, wid = job
        buf = tg2 % 2
        lsem = "s_tl0" if buf == 0 else "s_tl1"
        qsem = "s_tq" if buf == 0 else "s_tq2"

        def emit():
            @blk.sync
            def _(s, war=(wd2_marks[tg2 - 2] if tg2 >= 2 else None)):
                if war is not None:
                    s.wait_ge(sems["s_wd"], war)
                src = w16_[colbase:colbase + wid,
                           a0 * 128:(a0 + 1) * 128]
                inc(s.dma_start_transpose(
                    wtc[:, buf * PCW:buf * PCW + wid], src), lsem, 16)
            ldm = cnt[lsem]

            @blk.vector
            def _(v, ldm=ldm, tqw=cnt[qsem], first=(tg2 < 2)):
                v.wait_ge(sems[lsem], ldm)
                if not first:
                    v.wait_ge(sems[qsem], tqw)
                v.tensor_scalar(ttmp[:, :wid],
                                wtc[:, buf * PCW:buf * PCW + wid],
                                gcols[:, mi:mi + 1], None, op0=OP.is_gt)
                v.drain()
                v.scalar_tensor_tensor(
                    tqc[:, buf * PCW:buf * PCW + wid],
                    wtc[:, buf * PCW:buf * PCW + wid],
                    gcols[:, 6 + mi:7 + mi], ttmp[:, :wid],
                    op0=OP.is_lt, op1=OP.subtract)
                v.drain()
                inc(v.nop(), "s_wd")
            wd2_marks.append(cnt["s_wd"])

            @blk.gpsimd
            def _(g, w=wd2_marks[-1]):
                g.wait_ge(sems["s_wd"], w)
                dst = t8_[a0 * 128:(a0 + 1) * 128, colbase:colbase + wid]
                inc(g.dma_start(dst,
                                tqc[:, buf * PCW:buf * PCW + wid]),
                    qsem, 16)
        return emit

    _tgc = [0]
    for w16_, _t8, mi in W3:
        if mi < 2:
            for t in range(HID // 128):
                for hf in range(max(1, E // PBW)):
                    wid = min(PBW, E)
                    src = w16_[t * 128:(t + 1) * 128,
                               hf * PBW:hf * PBW + wid]
                    side_jobs.append(mk_p2b_tile(src, mi, _tgc[0], wid))
                    _tgc[0] += 1
        else:
            for r4 in range(E // 128):
                for part in range(HID // PBW):
                    src = w16_[r4 * 128:(r4 + 1) * 128,
                               part * PBW:(part + 1) * PBW]
                    side_jobs.append(mk_p2b_tile(src, mi, _tgc[0], PBW))
                    _tgc[0] += 1
        side_jobs.append(mk_gcols(mi))
    tern_jobs = []
    for w16_, t8_, mi in W3:
        if mi < 2:
            for a0 in range(EC):
                for p_ in range(max(1, HID // PCW)):
                    tern_jobs.append((w16_, t8_, mi, a0, p_ * PCW,
                                      min(PCW, HID)))
        else:
            for a0 in range(HC):
                for p_ in range(max(1, E // PCW)):
                    tern_jobs.append((w16_, t8_, mi, a0, p_ * PCW,
                                      min(PCW, E)))
    for tg2, job in enumerate(tern_jobs):
        side_jobs.append(mk_p2c_tile(job, tg2))

    n_units = MYNB * H
    side_done = [0]

    def pump_side_jobs(unit_idx):
        want = (unit_idx + 1) * len(side_jobs) // n_units
        while side_done[0] < want:
            side_jobs[side_done[0]]()
            side_done[0] += 1

    SCL = 1.0 / math.sqrt(DH)
    for g in range(MYNB):
        for h in range(H):
            kv = h // GQ
            KT = {p: 4 * (my_par[p][g] + 1) for p in (0, 1)}
            bases = {p: dict(CN[p]) for p in (0, 1)}

            @blk.tensor
            def _(te, g=g, h=h, kv=kv, KT=KT, bases=bases):
                r = parity_reg(te)
                for p in (0, 1):
                    ctxm = te.If_eq(r, 0) if p == 0 else te.Else()
                    with ctxm:
                        B, n = bases[p], KT[p]
                        te.wait_ge(sems["s_dv"], B["s_dv"])
                        for j in range(n):
                            if j >= 2:
                                te.wait_ge(sems["s_ac"], B["s_ac"] + j - 1)
                            mm = te.matmul(
                                pb[j % 2][:, :],
                                Kres[:, kv * S + j * 128:kv * S + (j + 1) * 128],
                                Qres[:, h * MYT + g * TB:h * MYT + (g + 1) * TB],
                                start=True, stop=True)
                            binc(p, mm, "s_pe")
                            if j >= 1:
                                jj = j - 1
                                te.wait_ge(sems["s_ac"], B["s_ac"] + jj + 1)
                                mm = te.matmul(
                                    pb[2][:, :],
                                    Vres[:, kv * S + jj * 128:kv * S + (jj + 1) * 128],
                                    ptile[:, (jj % 2) * TB:(jj % 2 + 1) * TB],
                                    start=(jj == 0), stop=(jj == n - 1))
                                binc(p, mm, "s_ko")
                                mm = te.matmul(
                                    pb[3][:, :], onesb[:, :],
                                    ptile[:, (jj % 2) * TB:(jj % 2 + 1) * TB],
                                    start=(jj == 0), stop=(jj == n - 1))
                                if jj < n - 1:
                                    binc(p, mm, "s_ko")
                        jj = n - 1
                        te.wait_ge(sems["s_ac"], B["s_ac"] + n)
                        mm = te.matmul(
                            pb[2][:, :],
                            Vres[:, kv * S + jj * 128:kv * S + (jj + 1) * 128],
                            ptile[:, (jj % 2) * TB:(jj % 2 + 1) * TB],
                            start=(jj == 0), stop=True)
                        binc(p, mm, "s_ko")
                        mm = te.matmul(
                            pb[3][:, :], onesb[:, :],
                            ptile[:, (jj % 2) * TB:(jj % 2 + 1) * TB],
                            start=(jj == 0), stop=True)
                        binc(p, mm, "s_fb")

            @blk.vector
            def _(v, g=g, h=h, KT=KT, bases=bases):
                r = parity_reg(v)
                for p in (0, 1):
                    ctxm = v.If_eq(r, 0) if p == 0 else v.Else()
                    with ctxm:
                        B, n = bases[p], KT[p]
                        for ridx, j in enumerate(range(n - 4, n)):
                            v.wait_ge(sems["s_pe"], B["s_pe"] + j + 1)
                            mask_r = j - (n - 4)
                            tt = v.tensor_tensor(
                                pb[j % 2][:, :], pb[j % 2][:, :],
                                masks[:, mask_r * TB:(mask_r + 1) * TB],
                                op=OP.add)
                            binc(p, tt, "s_dv")
                        v.wait_ge(sems["s_fb"], B["s_fb"] + 1)
                        v.reciprocal(rsum[:, :], pb[3][:, :])
                        v.drain()
                        tt = v.tensor_tensor(
                            attst[:, h * TB:(h + 1) * TB],
                            pb[2][:, :], rsum[:, :], op=OP.mult)
                        binc(p, tt, "s_dv")

            @blk.scalar
            def _(sc, g=g, h=h, KT=KT, bases=bases):
                r = parity_reg(sc)
                for p in (0, 1):
                    ctxm = sc.If_eq(r, 0) if p == 0 else sc.Else()
                    with ctxm:
                        B, n = bases[p], KT[p]
                        for j in range(n):
                            band = j >= n - 4
                            if band:
                                sc.wait_ge(sems["s_dv"],
                                           B["s_dv"] + (j - (n - 4)) + 1)
                            else:
                                sc.wait_ge(sems["s_pe"], B["s_pe"] + j + 1)
                            if j >= 2:
                                sc.wait_ge(sems["s_ko"], B["s_ko"] + 2 * (j - 2) + 1)
                            a = sc.activation(
                                ptile[:, (j % 2) * TB:(j % 2 + 1) * TB],
                                pb[j % 2][:, :], AF.Exp, scale=SCL,
                                bias=negB[:, h:h + 1])
                            binc(p, a, "s_ac")

            pump_side_jobs(g * H + h)

        att_dv = CN[0]["s_dv"]
        assert att_dv == CN[1]["s_dv"]
        QN = max(1, EC // 4)
        for eg in range(EC // 2):
            qld = []
            for q in range(QN):
                slot = wo_qctr[0] % 2
                wsem = "s_wo0" if slot == 0 else "s_wo1"
                wo_qctr[0] += 1

                @blk.sync
                def _(s, g=g, eg=eg, q=q, slot=slot,
                      w=(wo_q_marks[-2] if len(wo_q_marks) >= 2 else None)):
                    s.wait_ge(sems["c_aw"], caw_wo)
                    if w is not None:
                        s.wait_ge(sems["s_wt"], w)
                    for cq in range(4):
                        c = q * 4 + cq
                        u = s.dma_start_transpose(
                            wob[:, slot * 1024 + cq * 256:
                                slot * 1024 + (cq + 1) * 256],
                            wo16[eg * 256:(eg + 1) * 256,
                                 c * 128:(c + 1) * 128])
                        uinc(u, wsem, 16)
                qld.append((wsem, CN[0][wsem]))
                wo_q_marks.append(CN[0]["s_wt"] + q + 1)

            @blk.tensor
            def _(te, g=g, eg=eg, att_dv=att_dv, qld=qld, qbase=wo_qctr[0] - QN,
                  w=CN[0]["s_dv"]):
                te.wait_ge(sems["s_dv"], max(att_dv, w))
                for q in range(QN):
                    te.wait_ge(sems[qld[q][0]], qld[q][1])
                    slot = (qbase + q) % 2
                    for ee in range(2):
                        for cq in range(4):
                            c = q * 4 + cq
                            mm = te.matmul(
                                pb[4 + ee][:, :],
                                wob[:, slot * 1024 + cq * 256 + ee * 128:
                                    slot * 1024 + cq * 256 + (ee + 1) * 128],
                                attst[:, c * TB:(c + 1) * TB],
                                start=(c == 0), stop=(c == EC - 1))
                    uinc(mm, "s_wt")
            wo_pe = CN[0]["s_wt"]

            for ee in range(2):
                e = eg * 2 + ee

                @blk.vector
                def _(v, g=g, e=e, ee=ee, wo_pe=wo_pe):
                    v.wait_ge(sems["s_wt"], wo_pe)
                    st = v.scalar_tensor_tensor(
                        xres[:, e * MYT + g * TB:e * MYT + (g + 1) * TB],
                        pb[4 + ee][:, :], modc[:, 2 * EC + e:2 * EC + e + 1],
                        xres[:, e * MYT + g * TB:e * MYT + (g + 1) * TB],
                        op0=OP.mult, op1=OP.add)
                    v.drain()
                    uinc(v.nop(), "s_dv")

    while side_done[0] < len(side_jobs):
        side_jobs[side_done[0]]()
        side_done[0] += 1

    assert cnt0 == cnt1, (cnt0, cnt1)
    for _k in ("s_pe", "s_dv", "s_ac", "s_ko", "s_fb", "s_wt", "s_qw"):
        cnt[_k] = cnt0[_k]

    if "gcols" in dbg:
        @blk.scalar
        def _(sc, w=cnt["s_wd"]):
            sc.wait_ge(sems["s_wd"], w)
            tap(sc, "gcols", gcols[:, :], 128, 12)

    if "attst" in dbg:
        @blk.scalar
        def _(sc, w=cnt["s_dv"]):
            sc.wait_ge(sems["s_dv"], w)
            tap(sc, "attst", attst[:, :], 128, H * TB)
    if "xmid" in dbg:
        @blk.scalar
        def _(sc, w=cnt["s_dv"]):
            sc.wait_ge(sems["s_dv"], w)
            tap(sc, "xmid", xres[:, :], 128, EC * MYT)

    @blk.gpsimd
    def _(g):
        g.wait_ge(sems["s_tq"], cnt["s_tq"])
        g.wait_ge(sems["s_tq2"], cnt["s_tq2"])

    cnt_tq = (cnt["s_tq"], cnt["s_tq2"])
    esB.close()

    esP2.close()
    esP2w.close()
    esKV.close()
    esP3 = contextlib.ExitStack()
    gvb = mk(esP3, "gvb", [128, HC * TB], F16)
    w8b = mk(esP3, "w8b", [128, 2 * EC * TB], FP8)
    h2b = mk(esP3, "h2b", [128, EC * TB], BF16)
    rqb = mk(esP3, "rqb", [128, EC * TB], BF16)
    sgam1 = mk(esP3, "sgam1", [128, TB], F32)
    bgam1 = mk(esP3, "bgam1", [128, TB], F32)
    sgam2 = sgam1
    bgam2 = bgam1
    ostg = mk(esP3, "ostg", [128, 2 * 512], F32)
    tA2 = mk(esP3, "tA2", [128, TB], F32)
    tB2 = mk(esP3, "tB2", [128, TB], F32)
    tC2 = mk(esP3, "tC2", [128, TB], F32)
    Gpad = mk(esP3, "Gpad", [128, 128], F32)
    stagb = ostg[:, 0:512]
    ident32 = mk(esP3, "ident32", [128, 128], F32)
    w8o = mk(esP3, "w8o", [128, 4 * TB], FP8)
    bC1c = mk(esP3, "bC1c", [128, 1], F32)
    bC2c = mk(esP3, "bC2c", [128, 1], F32)
    bC3c = mk(esP3, "bC3c", [128, 1], F32)

    esB = contextlib.ExitStack()
    blk = esB.enter_context(nc.Block("p3", no_gpsimd_drain=True))

    @blk.vector
    def _(v):
        v.tensor_scalar(cacol[:, :], modc[:, 5 * EC:6 * EC],
                        gcols[:, 5:6], None, op0=OP.mult)
        v.tensor_tensor(bias2[:, :], modc[:, 5 * EC:6 * EC], biaso[:, :],
                        op=OP.mult)
        v.memset(Gpad[:, :], 0.0)
        v.memset(stagb[:, :], 0.0)
        v.memset(bC1c[:, :], C1)
        v.memset(bC2c[:, :], C2)
        v.memset(bC3c[:, :], C3)
        v.tensor_copy(ident32[:, :], identb[:, :])
        v.drain()
        inc(v.nop(), "s_dv")

    def emit_gamma2(src_ap, sg_ap, bg_ap):
        @blk.tensor
        def _(te, w=cnt["s_dv"]):
            te.wait_ge(sems["s_dv"], w)
            for i in range(4):
                mm = te.transpose(pb[4][:, i * 128:(i + 1) * 128],
                                  src_ap[:, i * 128:(i + 1) * 128],
                                  ident32[:, :])
            inc(mm, "s_pe")

        @blk.vector
        def _(v, w=cnt["s_pe"]):
            v.wait_ge(sems["s_pe"], w)
            for i in range(4):
                v.tensor_reduce(Gpad[:, 32 * i:32 * i + 1],
                                pb[4][:, i * 128:(i + 1) * 128],
                                axis=AX.X, op=OP.max)
            v.drain()
            inc(v.nop(), "s_dv")

        @blk.tensor
        def _(te, w=cnt["s_dv"]):
            te.wait_ge(sems["s_dv"], w)
            inc(te.transpose(pb[5][:, 0:128], Gpad[:, :], ident32[:, :]),
                "s_pe")

        @blk.vector
        def _(v, w=cnt["s_pe"],
              stw=(st_marks[0][-1] if st_marks[0] else None)):
            v.wait_ge(sems["s_pe"], w)
            if stw is not None:
                v.wait_ge(sems["s_st"], stw)
            v.memset(stagb[:, :], 0.0)
            v.drain()
            for i in range(4):
                v.tensor_copy(stagb[32 * i:32 * i + 1, i * 128:(i + 1) * 128],
                              pb[5][32 * i:32 * i + 1, 0:128])
            v.drain()
            inc(v.nop(), "s_dv")

        @blk.tensor
        def _(te, w=cnt["s_dv"]):
            te.wait_ge(sems["s_dv"], w)
            inc(te.matmul(pb[6][:, :], onesf[:, :], stagb[:, :],
                          start=True, stop=True), "s_pe")

        @blk.vector
        def _(v, w=cnt["s_pe"]):
            v.wait_ge(sems["s_pe"], w)
            v.tensor_scalar(tA2[:, :], pb[6][:, :], 1e-5, None, op0=OP.max)
            v.drain()
            v.reciprocal(tB2[:, :], tA2[:, :])
            v.drain()
            v.tensor_scalar(sg_ap, tB2[:, :], 127.0, None, op0=OP.mult)
            inc(v.tensor_scalar(bg_ap, tA2[:, :], 1.0 / 127.0, None,
                                op0=OP.mult), "s_dv")

    tD2 = ostg[:, 512:1024]
    rnd_sets = [(tA, tB, tC, tD), (tA2, tB2, tC2, tD2)]
    gv_half_pe, gv_epi1, gv_silu, out_pe_marks = [], [], [], []
    st_marks, st_qctr = {0: [], 1: []}, [0]

    def emit_rounds(src_of, dst_of, sx_ap, n):
        marks, u_marks, ac_marks = [], [], []
        stw2 = st_marks[1][-1] if st_marks[1] else None
        for c in range(n):
            s0, s1, s2, s3 = rnd_sets[c % 2]

            @blk.vector
            def _(v, s0=s0, src=src_of(c),
                  war=(ac_marks[c - 2] if c >= 2 else None)):
                if war is not None:
                    v.wait_ge(sems["s_ac"], war)
                inc(v.tensor_tensor(s0[:, :], src, sx_ap, op=OP.mult),
                    "s_dv")
            u_marks.append(cnt["s_dv"])

            @blk.scalar
            def _(sc, s0=s0, s1=s1, s2=s2, s3=s3, w=u_marks[c],
                  stw2=(stw2 if c == 1 else None)):
                sc.wait_ge(sems["s_dv"], w)
                if stw2 is not None:
                    sc.wait_ge(sems["s_st2"], stw2)
                sc.activation(s1[:, :], s0[:, :], AF.Identity,
                              bias=bC1c[:, :1])
                sc.activation(s2[:, :], s0[:, :], AF.Identity,
                              bias=bC2c[:, :1])
                inc(sc.activation(s3[:, :], s0[:, :], AF.Identity,
                                  bias=bC3c[:, :1]), "s_ac")
            ac_marks.append(cnt["s_ac"])

            @blk.vector
            def _(v, s1=s1, s2=s2, s3=s3, dst=dst_of(c), w=ac_marks[c]):
                v.wait_ge(sems["s_ac"], w)
                v.scalar_tensor_tensor(s1[:, :], s2[:, :], -C2, s1[:, :],
                                       op0=OP.add, op1=OP.add)
                v.drain()
                inc(v.scalar_tensor_tensor(dst, s3[:, :], C1 - C3,
                                           s1[:, :], op0=OP.add,
                                           op1=OP.subtract), "s_dv")
            marks.append(cnt["s_dv"])
        return marks

    rqm_of = {}
    stats_mark_of = {}

    def emit_prep_a(g):
        @blk.scalar
        def _(sc, g=g, w=cnt["s_dv"], wdb=cnt["s_db"]):
            sc.wait_ge(sems["s_dv"], w)
            sc.wait_ge(sems["s_db"], wdb)
            for c in range(EC):
                a = sc.activation(h2b[:, c * TB:(c + 1) * TB],
                                  xres[:, c * MYT + g * TB:c * MYT + (g + 1) * TB],
                                  AF.Square)
            inc(a, "s_ac")
        sq_ac = cnt["s_ac"]

        @blk.tensor
        def _(te, g=g, sq_ac=sq_ac, w=cnt["s_dv"]):
            te.wait_ge(sems["s_ac"], sq_ac)
            te.wait_ge(sems["s_dv"], w)
            for c in range(EC):
                mm = te.matmul(pb[0][:, :], onesb[:, :],
                               xres[:, c * MYT + g * TB:c * MYT + (g + 1) * TB],
                               start=(c == 0), stop=(c == EC - 1))
            inc(mm, "s_pe")
            for c in range(EC):
                mm = te.matmul(pb[1][:, :], onesb[:, :],
                               h2b[:, c * TB:(c + 1) * TB],
                               start=(c == 0), stop=(c == EC - 1))
            inc(mm, "s_pe")

        @blk.vector
        def _(v, w=cnt["s_pe"]):
            v.wait_ge(sems["s_pe"], w)
            v.tensor_scalar(tA[:, :], pb[0][:, :], 1.0 / E, None, op0=OP.mult)
            v.drain()
            v.tensor_tensor(tB[:, :], tA[:, :], tA[:, :], op=OP.mult)
            v.drain()
            v.scalar_tensor_tensor(tC[:, :], pb[1][:, :], 1.0 / E, tB[:, :],
                                   op0=OP.mult, op1=OP.subtract)
            v.drain()
            inc(v.tensor_scalar(tB[:, :], tC[:, :], cfg.eps, None, op0=OP.add),
                "s_dv")
        stats_mark_of[g] = cnt["s_dv"]

        @blk.scalar
        def _(sc, w=cnt["s_dv"]):
            sc.wait_ge(sems["s_dv"], w)
            inc(sc.activation(tC[:, :], tB[:, :], AF.Sqrt), "s_ac")

        @blk.vector
        def _(v, g=g, w=cnt["s_ac"]):
            v.wait_ge(sems["s_ac"], w)
            v.reciprocal(tB[:, :], tC[:, :])
            v.drain()
            for c in range(EC):
                v.tensor_tensor(tC[:, :],
                                xres[:, c * MYT + g * TB:c * MYT + (g + 1) * TB],
                                tA[:, :], op=OP.subtract)
                v.drain()
                v.tensor_tensor(tD[:, :], tC[:, :], tB[:, :], op=OP.mult)
                v.drain()
                v.tensor_scalar(h2b[:, c * TB:(c + 1) * TB], tD[:, :],
                                modc[:, 4 * EC + c:4 * EC + c + 1],
                                modc[:, 3 * EC + c:3 * EC + c + 1],
                                op0=OP.mult, op1=OP.add)
                v.drain()
            v.tensor_reduce(tC[:, :],
                            h2b.rearrange("p (c t) -> p t c", c=EC),
                            axis=AX.X, op=OP.max, apply_absolute_value=True)
            v.drain()
            inc(v.nop(), "s_dv")

    def emit_prep_b(g):
        emit_gamma2(tC[:, :], sgam1[:, :], bgam1[:, :])

    def emit_prep_c(g):
        @blk.vector
        def _(v, wdb=cnt["s_db"], wpe=cnt["s_pe"]):
            v.wait_ge(sems["s_db"], wdb)
            v.wait_ge(sems["s_pe"], wpe)
            v.drain()
            inc(v.nop(), "s_dv")
        rqm_of[g] = emit_rounds(
            lambda c: h2b[:, c * TB:(c + 1) * TB],
            lambda c: rqb[:, c * TB:(c + 1) * TB],
            sgam1[:, :], EC)
        if g == 0 and "rqb" in dbg:
            @blk.scalar
            def _(sc, w=rqm_of[g][-1]):
                sc.wait_ge(sems["s_dv"], w)
                tap(sc, "rqb", rqb[:, :], 128, EC * TB)
        if g == 0 and "h2b" in dbg:
            @blk.scalar
            def _(sc, w=rqm_of[g][-1]):
                sc.wait_ge(sems["s_dv"], w)
                tap(sc, "h2b", h2b[:, :], 128, EC * TB)
        if g == 0 and "sgam" in dbg:
            @blk.scalar
            def _(sc, w=rqm_of[g][-1]):
                sc.wait_ge(sems["s_dv"], w)
                tap(sc, "sgam", sgam1[:, :], 128, TB)

    emit_prep_a(0)
    emit_prep_b(0)
    emit_prep_c(0)

    for g in range(MYNB):
        rq_marks = rqm_of[g]
        rq_dv = rq_marks[-1]

        HALF = EC * TB
        for hg in range(HC // 4):
            for k in range(2):
                idx = g * (HC // 2) + hg * 2 + k

                gsem = "s_gv0" if (len(gv_half_pe) % 2 == 0) else "s_gv1"

                @blk.sync
                def _(s, hg=hg, k=k, gsem=gsem,
                      w=(gv_half_pe[-2] if len(gv_half_pe) >= 2 else None),
                      first=(len(gv_half_pe) == 0)):
                    if first:
                        s.wait_ge(sems["s_tq"], cnt_tq[0])
                        s.wait_ge(sems["s_tq2"], cnt_tq[1])
                    if w is not None:
                        s.wait_ge(sems["s_pe"], w)
                    for d in range(EC):
                        inc(s.dma_start(
                            w8b[:, k * HALF + d * 256:k * HALF + d * 256 + 256],
                            tg8[d * 128:(d + 1) * 128,
                                hg * 512 + k * 256:hg * 512 + k * 256 + 256]),
                            gsem, 16)
                    for d in range(EC):
                        inc(s.dma_start(
                            w8b[:, k * HALF + (EC + d) * 256:
                                k * HALF + (EC + d) * 256 + 256],
                            tv8[d * 128:(d + 1) * 128,
                                hg * 512 + k * 256:hg * 512 + k * 256 + 256]),
                            gsem, 16)
                gv_ld = cnt[gsem]

                for hp in range(2):
                    h = hg * 4 + k * 2 + hp

                    @blk.tensor
                    def _(te, h=h, k=k, hp=hp, gv_ld=gv_ld, gsem=gsem,
                          rq_dv=rq_dv,
                          bankw=(gv_epi1[-2] if len(gv_epi1) >= 2 else None)):
                        te.wait_ge(sems[gsem], gv_ld)
                        if h == 0:
                            te.wait_ge(sems["s_dv"], rq_dv)
                        if bankw is not None:
                            te.wait_ge(sems["s_dv"], bankw)
                        pg = pb[2 * (h % 2)]
                        pv = pb[2 * (h % 2) + 1]
                        for d in range(EC):
                            mm = te.matmul(
                                pg[:, :],
                                w8b[:, k * HALF + d * 256 + hp * 128:
                                    k * HALF + d * 256 + (hp + 1) * 128],
                                rqb[:, d * TB:(d + 1) * TB],
                                start=(d == 0), stop=(d == EC - 1))
                        inc(mm, "s_pe")
                        for d in range(EC):
                            mm = te.matmul(
                                pv[:, :],
                                w8b[:, k * HALF + (EC + d) * 256 + hp * 128:
                                    k * HALF + (EC + d) * 256 + (hp + 1) * 128],
                                rqb[:, d * TB:(d + 1) * TB],
                                start=(d == 0), stop=(d == EC - 1))
                        inc(mm, "s_pe")
                    gv_pe = cnt["s_pe"]

                    @blk.vector
                    def _(v, h=h, gv_pe=gv_pe,
                          silu_war=(gv_silu[-2] if len(gv_silu) >= 2
                                    else None)):
                        v.wait_ge(sems["s_pe"], gv_pe)
                        if silu_war is not None:
                            v.wait_ge(sems["s_ac"], silu_war)
                        pg = pb[2 * (h % 2)]
                        pv = pb[2 * (h % 2) + 1]
                        gp = [tA, tC][h % 2]
                        vp = [tB, tD][h % 2]
                        v.scalar_tensor_tensor(gp[:, :], pg[:, :],
                                               gcols[:, 3:4], bgam1[:, :],
                                               op0=OP.mult, op1=OP.mult)
                        v.drain()
                        inc(v.scalar_tensor_tensor(vp[:, :], pv[:, :],
                                                   gcols[:, 4:5],
                                                   bgam1[:, :],
                                                   op0=OP.mult,
                                                   op1=OP.mult), "s_dv")
                    gv_epi1.append(cnt["s_dv"])

                    @blk.scalar
                    def _(sc, h=h, w=gv_epi1[-1]):
                        sc.wait_ge(sems["s_dv"], w)
                        inc(sc.activation([tE16, tF16][h % 2][:, :],
                                          [tA, tC][h % 2][:, :], AF.Silu,
                                          bias=biasg[:, h:h + 1]), "s_ac")
                    gv_silu.append(cnt["s_ac"])

                    @blk.vector
                    def _(v, h=h, w=gv_silu[-1]):
                        v.wait_ge(sems["s_ac"], w)
                        inc(v.scalar_tensor_tensor(
                            gvb[:, h * TB:(h + 1) * TB],
                            [tB, tD][h % 2][:, :], biasvb[:, h:h + 1],
                            [tE16, tF16][h % 2][:, :],
                            op0=OP.add, op1=OP.mult), "s_dv")
                gv_half_pe.append(gv_pe)

        if g == 0 and "gvb" in dbg:
            @blk.scalar
            def _(sc, w=cnt["s_dv"]):
                sc.wait_ge(sems["s_dv"], w)
                tap(sc, "gvb", gvb[:, :], 128, HC * TB)

        @blk.vector
        def _(v):
            v.drain()
            v.tensor_reduce(tC[:, :],
                            gvb.rearrange("p (c t) -> p t c", c=HC),
                            axis=AX.X, op=OP.max, apply_absolute_value=True)
            v.drain()
            inc(v.nop(), "s_dv")

        emit_gamma2(tC[:, :], sgam2[:, :], bgam2[:, :])

        @blk.vector
        def _(v, wdb=cnt["s_db"]):
            v.wait_ge(sems["s_db"], wdb)
            v.drain()
            inc(v.nop(), "s_dv")
        r2_marks = emit_rounds(
            lambda c: gvb[:, c * TB:(c + 1) * TB],
            lambda c: gvb[:, c * TB:(c + 1) * TB],
            sgam2[:, :], HC)
        r2_dv = r2_marks[-1]

        if g + 1 < MYNB:
            emit_prep_a(g + 1)

        for sw in range(EC // 4):
            sw_pe0 = cnt["s_pe"]
            for h in range(HC):
                slot = h % 4
                osem = ["s_ld", "s_l3", "s_l2", "s_l4"][slot]
                war = (out_pe_marks[-4] if len(out_pe_marks) >= 4 else None)

                def _ld(e, h=h, sw=sw, slot=slot, osem=osem, war=war):
                    if war is not None:
                        e.wait_ge(sems["s_pe"], war)
                    inc(e.dma_start(
                        w8o[:, slot * TB:(slot + 1) * TB],
                        to8[h * 128:(h + 1) * 128, sw * 512:(sw + 1) * 512]),
                        osem, 16)
                if h % 2 == 0:
                    blk.sync(_ld)
                else:
                    blk.scalar(_ld)

                @blk.tensor
                def _(te, sw=sw, h=h, slot=slot, osem=osem, ld=cnt[osem],
                      r2m=(r2_marks[h] if sw == 0 else None),
                      evacw=((stats_mark_of.get(g + 1, cnt["s_dv"])
                              if sw == 0 else cnt["s_dv"])
                             if h == 0 else None)):
                    te.wait_ge(sems[osem], ld)
                    if r2m is not None:
                        te.wait_ge(sems["s_dv"], r2m)
                    if evacw is not None:
                        te.wait_ge(sems["s_dv"], evacw)
                    for ee in range(4):
                        mm = te.matmul(
                            pb[ee][:, :],
                            w8o[:, slot * TB + ee * 128:
                                slot * TB + (ee + 1) * 128],
                            gvb[:, h * TB:(h + 1) * TB],
                            start=(h == 0), stop=(h == HC - 1))
                    inc(mm, "s_pe")
                out_pe_marks.append(cnt["s_pe"])

            osw_pe = cnt["s_pe"]
            for ee in range(4):
                e = sw * 4 + ee

                @blk.vector
                def _(v, g=g, e=e, ee=ee, osw_pe=osw_pe, wdb=cnt["s_db"]):
                    v.wait_ge(sems["s_pe"], osw_pe)
                    v.wait_ge(sems["s_db"], wdb)
                    v.scalar_tensor_tensor(tA2[:, :], pb[ee][:, :],
                                           cacol[:, e:e + 1], bgam2[:, :],
                                           op0=OP.mult, op1=OP.mult)
                    v.drain()
                    v.scalar_tensor_tensor(
                        rqb[:, e * TB:(e + 1) * TB], tA2[:, :],
                        bias2[:, e:e + 1],
                        xres[:, e * MYT + g * TB:e * MYT + (g + 1) * TB],
                        op0=OP.add, op1=OP.add)
                    v.drain()
                    inc(v.nop(), "s_dv")
        xout_dv = cnt["s_dv"]
        if g == 0 and "xoutb" in dbg:
            @blk.scalar
            def _(sc, w=xout_dv):
                sc.wait_ge(sems["s_dv"], w)
                tap(sc, "xoutb", rqb[:, :], 128, EC * TB)

        n_q = max(1, EC // 4)
        qt = min(4, EC)
        for sub in range(TB // 128):
            for q in range(n_q):
                pp = st_qctr[0] % 2
                st_qctr[0] += 1

                @blk.tensor
                def _(te, g=g, sub=sub, q=q, pp=pp, xout_dv=xout_dv,
                      w=cnt["s_dv"]):
                    te.wait_ge(sems["s_dv"], max(xout_dv, w))
                    for e4 in range(qt):
                        e = q * qt + e4
                        mm = te.transpose(
                            pbT[:, pp * 512 + e4 * 128:
                                pp * 512 + (e4 + 1) * 128],
                            rqb[:, e * TB + sub * 128:e * TB + (sub + 1) * 128],
                            identb[:, :])
                    inc(mm, "s_pe")

                @blk.vector
                def _(v, pp=pp, w=cnt["s_pe"],
                      stw=(st_marks[pp][-1] if st_marks[pp] else None)):
                    v.wait_ge(sems["s_pe"], w)
                    if stw is not None:
                        v.wait_ge(sems["s_st" if pp == 0 else "s_st2"],
                                  stw)
                    v.tensor_copy(ostg[:, pp * 512:(pp + 1) * 512],
                                  pbT[:, pp * 512:(pp + 1) * 512])
                    v.drain()
                    inc(v.nop(), "s_dv")

                @blk.scalar
                def _(sc, g=g, sub=sub, q=q, pp=pp, w=cnt["s_dv"]):
                    sc.wait_ge(sems["s_dv"], w)
                    inc(sc.dma_start(
                        y_out[g * TB + sub * 128:g * TB + (sub + 1) * 128,
                              q * qt * 128:(q + 1) * qt * 128],
                        ostg[:, pp * 512:pp * 512 + qt * 128]),
                        "s_st" if pp == 0 else "s_st2", 16)
                st_marks[pp].append(cnt["s_st" if pp == 0 else "s_st2"])

        if g + 1 < MYNB:
            emit_prep_b(g + 1)
            emit_prep_c(g + 1)

    @blk.sync
    def _(s):
        s.wait_ge(sems["s_st"], cnt["s_st"])
        s.wait_ge(sems["s_st2"], cnt["s_st2"])
        s.wait_ge(sems["s_db"], cnt["s_db"])

    esB.close()
    esP3.close()
    es0.close()
    return nc



def kernel(L0_attn_x, L0_attn_wq, L0_attn_wk, L0_attn_wv, L0_attn_wo,
           L0_attn_bq, L0_attn_bk, L0_attn_bv, L0_ffn_gate_w, L0_ffn_gate_b,
           L0_ffn_val_w, L0_ffn_val_b, L0_ffn_out_w, L0_ffn_out_b, L0_c,
           L0_wmod):
    cfg = Cfg()
    nc = build_program(cfg)
    A = np.ascontiguousarray

    def row(v):
        return A(np.asarray(v, dtype=np.float32).reshape(1, -1))

    shared = {
        "wmod": A(np.asarray(L0_wmod, np.float32)),
        "wq": A(np.asarray(L0_attn_wq, np.float32)),
        "wk": A(np.asarray(L0_attn_wk, np.float32)),
        "wv": A(np.asarray(L0_attn_wv, np.float32)),
        "wo": A(np.asarray(L0_attn_wo, np.float32)),
        "bq": row(L0_attn_bq), "bk": row(L0_attn_bk), "bv": row(L0_attn_bv),
        "gate_w": A(np.asarray(L0_ffn_gate_w, np.float32)),
        "val_w": A(np.asarray(L0_ffn_val_w, np.float32)),
        "out_w": A(np.asarray(L0_ffn_out_w, np.float32)),
        "gate_b": row(L0_ffn_gate_b), "val_b": row(L0_ffn_val_b),
        "ob": row(L0_ffn_out_b),
    }
    x_np = np.asarray(L0_attn_x, np.float32)
    c_np = np.asarray(L0_c, np.float32)
    B = x_np.shape[0]
    n_cores = 2 * B
    in_maps = []
    for core in range(n_cores):
        b, p = core // 2, core % 2
        im = dict(shared)
        im["x"] = A(x_np[b])
        im["c_row"] = A(c_np[b:b + 1])
        im["parity"] = np.array([[p]], np.int32)
        in_maps.append(im)

    res = run_bass_kernel_spmd(nc, in_maps, core_ids=list(range(n_cores)))

    out = np.empty((B, cfg.S, cfg.E), np.float32)
    for core in range(n_cores):
        b, p = core // 2, core % 2
        y = res.results[core]["y"]
        for gi, m in enumerate(cfg.my_blocks(p)):
            out[b, m * cfg.TB:(m + 1) * cfg.TB] = y[gi * cfg.TB:(gi + 1) * cfg.TB]
    return out


# version 11 (final)
# speedup vs baseline: 1.0693x; 1.0167x over previous
import contextlib
import math

import numpy as np

import concourse.bass as bass
import concourse.mybir as mybir
from concourse.alu_op_type import AluOpType as OP
from concourse.bass_utils import run_bass_kernel_spmd

F32 = mybir.dt.float32
F16 = mybir.dt.float16
BF16 = mybir.dt.bfloat16
FP8 = mybir.dt.float8e4
I32 = mybir.dt.int32
AF = mybir.ActivationFunctionType
AX = mybir.AxisListType

C1 = 12582912.0
C2 = 6755399441055744.0
C3 = 6442450944.0

TP1 = float(np.float32(2 * math.pi))
TP2 = float(np.float32(2 * math.pi - TP1))
TP3 = float(2 * math.pi - TP1 - TP2)
INV_2PI = 1.0 / (2 * math.pi)
HALF_PI = math.pi / 2
PI_F = float(np.float32(math.pi))
NEGBIG = 1.0e7


class Cfg:
    def __init__(self, S=4096, E=2048, H=16, HKV=2, theta=10000.0):
        self.S, self.E, self.H, self.HKV, self.theta = S, E, H, HKV, theta
        self.DH = 128
        self.G = H // HKV
        self.KVD = HKV * self.DH
        self.HID = 4 * E
        self.EC = E // 128
        self.HC = self.HID // 128
        self.TB = 512
        self.NB = S // self.TB
        self.MYNB = self.NB // 2
        self.MYT = self.MYNB * self.TB
        self.eps = 1e-5

    def my_blocks(self, parity):
        n, q = self.NB, self.NB // 4
        if parity == 0:
            return list(range(q)) + list(range(n - q, n))
        return list(range(q, n - q))


def build_program(cfg, debug=()):
    nc = bass.Bass("TRN2", target_bir_lowering=False)
    S, E, H, HKV, DH = cfg.S, cfg.E, cfg.H, cfg.HKV, cfg.DH
    EC, HC, HID, KVD = cfg.EC, cfg.HC, cfg.HID, cfg.KVD
    TB, NB, MYNB, MYT = cfg.TB, cfg.NB, cfg.MYNB, cfg.MYT
    E6 = 6 * E
    GQ = cfg.G

    x_in = nc.dram_tensor("x", [S, E], F32, kind="ExternalInput").ap()
    c_in = nc.dram_tensor("c_row", [1, E], F32, kind="ExternalInput").ap()
    wmod_in = nc.dram_tensor("wmod", [E, E6], F32, kind="ExternalInput").ap()
    wq_in = nc.dram_tensor("wq", [E, E], F32, kind="ExternalInput").ap()
    wk_in = nc.dram_tensor("wk", [KVD, E], F32, kind="ExternalInput").ap()
    wv_in = nc.dram_tensor("wv", [KVD, E], F32, kind="ExternalInput").ap()
    wo_in = nc.dram_tensor("wo", [E, E], F32, kind="ExternalInput").ap()
    bq_in = nc.dram_tensor("bq", [1, E], F32, kind="ExternalInput").ap()
    bk_in = nc.dram_tensor("bk", [1, KVD], F32, kind="ExternalInput").ap()
    bv_in = nc.dram_tensor("bv", [1, KVD], F32, kind="ExternalInput").ap()
    gw_in = nc.dram_tensor("gate_w", [HID, E], F32, kind="ExternalInput").ap()
    vw_in = nc.dram_tensor("val_w", [HID, E], F32, kind="ExternalInput").ap()
    ow_in = nc.dram_tensor("out_w", [E, HID], F32, kind="ExternalInput").ap()
    gb_in = nc.dram_tensor("gate_b", [1, HID], F32, kind="ExternalInput").ap()
    vb_in = nc.dram_tensor("val_b", [1, HID], F32, kind="ExternalInput").ap()
    ob_in = nc.dram_tensor("ob", [1, E], F32, kind="ExternalInput").ap()
    par_in = nc.dram_tensor("parity", [1, 1], I32, kind="ExternalInput").ap()
    y_out = nc.dram_tensor("y", [MYT, E], F32, kind="ExternalOutput").ap()

    x16 = nc.dram_tensor("x16", [S, E], BF16, kind="Internal").ap()
    wq16 = nc.dram_tensor("wq16", [E, E], BF16, kind="Internal").ap()
    wk16 = nc.dram_tensor("wk16", [KVD, E], BF16, kind="Internal").ap()
    wv16 = nc.dram_tensor("wv16", [KVD, E], BF16, kind="Internal").ap()
    wo16 = nc.dram_tensor("wo16", [E, E], BF16, kind="Internal").ap()
    gw16 = nc.dram_tensor("gw16", [HID, E], F16, kind="Internal").ap()
    vw16 = nc.dram_tensor("vw16", [HID, E], F16, kind="Internal").ap()
    ow16 = nc.dram_tensor("ow16", [E, HID], F16, kind="Internal").ap()
    tg8 = nc.dram_tensor("tg8", [E, HID], FP8, kind="Internal").ap()
    tv8 = nc.dram_tensor("tv8", [E, HID], FP8, kind="Internal").ap()
    to8 = nc.dram_tensor("to8", [HID, E], FP8, kind="Internal").ap()
    modscr = nc.dram_tensor("modscr", [1, E6], F32, kind="Internal").ap()
    trigscr = nc.dram_tensor("trigscr", [128, 2 * S], BF16, kind="Internal").ap()
    h1scr = nc.dram_tensor("h1scr", [128, MYNB * EC * TB], BF16,
                           kind="Internal").ap()

    dbg = {}
    for spec_ in debug:
        name, shp = spec_[0], list(spec_[1])
        ddt = {"f32": F32, "bf16": BF16, "f16": F16}[spec_[2] if len(spec_) > 2 else "f32"]
        dbg[name] = nc.dram_tensor("dbg_" + name, shp, ddt,
                                   kind="ExternalOutput").ap()
    n_dbg_stores = [0]

    es0 = contextlib.ExitStack()

    def mk(stack, name, shape, dt):
        return stack.enter_context(nc.sbuf_tensor(name, shape, dt))

    xres = mk(es0, "xres", [128, EC * MYT], BF16)
    onesb = mk(es0, "onesb", [128, 128], BF16)
    onesf = mk(es0, "onesf", [128, 128], F32)
    identb = mk(es0, "identb", [128, 128], BF16)
    swapb = mk(es0, "swapb", [128, 128], BF16)
    modc = mk(es0, "modc", [128, 6 * EC], F32)
    scb = mk(es0, "scb", [128, EC], BF16)
    biasq = mk(es0, "biasq", [128, H], F32)
    biask = mk(es0, "biask", [128, HKV], F32)
    bvbank = mk(es0, "bvbank", [128, KVD], F32)
    biasg = mk(es0, "biasg", [128, HC], F32)
    biasvb = mk(es0, "biasvb", [128, HC], F32)
    biaso = mk(es0, "biaso", [128, EC], F32)
    gcols = mk(es0, "gcols", [128, 12], F32)
    cacol = mk(es0, "cacol", [128, EC], F32)
    bias2 = mk(es0, "bias2", [128, EC], F32)
    negB = mk(es0, "negB", [128, H], F32)
    qmax2 = mk(es0, "qmax2", [128, H], F32)
    kmax2 = mk(es0, "kmax2", [128, HKV], F32)
    tA = mk(es0, "tA", [128, TB], F32)
    tB = mk(es0, "tB", [128, TB], F32)
    tC = mk(es0, "tC", [128, TB], F32)
    tD = mk(es0, "tD", [128, TB], F32)
    tE16 = mk(es0, "tE16", [128, TB], BF16)
    tF16 = mk(es0, "tF16", [128, TB], BF16)
    colA = mk(es0, "colA", [128, 8], F32)
    parsb = mk(es0, "parsb", [1, 1], I32)

    pb = [es0.enter_context(nc.psum_tensor(f"pb{i}", [128, 512], F32))
          for i in range(7)]
    pbT = es0.enter_context(nc.psum_tensor("pbT", [128, 1024], BF16))

    sem_names = ["c_wm", "c_wm2", "c_x", "c_aw", "c_ff", "s_ld", "s_l2",
                 "s_wt", "s_pe", "s_dv", "s_ac", "s_gp", "s_st", "s_h1",
                 "s_qw", "s_tq", "s_tq2", "s_ko", "s_fb", "s_db", "s_x0",
                 "s_x1", "s_qt", "s_l3", "s_l4", "s_gv0", "s_gv1", "s_st2",
                 "s_wa", "s_wd", "s_wp", "s_tl0", "s_tl1", "s_wo0", "s_wo1"]
    sems = {n: es0.enter_context(nc.semaphore(n)) for n in sem_names}
    cnt = {n: 0 for n in sem_names}

    def inc(instr, name, n=1):
        cnt[name] += n
        instr.then_inc(sems[name], n)
        return instr

    def tap(eng, name, src_ap, rows, cols):
        if name not in dbg:
            return
        inc(eng.dma_start(dbg[name][0:rows, 0:cols], src_ap), "s_db", 16)
        n_dbg_stores[0] += 1

    def col_of(t1row, n):
        return bass.AP(t1row.tensor, 0, [[1, 128], [128, n]])

    pregs = {}
    _bias_ld = [0]

    def magic_round(v_, src_, dst, t1, t2, t3):
        v_.tensor_scalar(t1, src_, C1, -C1, op0=OP.add, op1=OP.add)
        v_.tensor_scalar(t2, src_, C2, -C2, op0=OP.add, op1=OP.add)
        v_.tensor_scalar(t3, src_, C3, -C3, op0=OP.add, op1=OP.add)
        v_.drain()
        v_.tensor_tensor(t1, t1, t2, op=OP.add)
        v_.drain()
        v_.tensor_tensor(dst, t1, t3, op=OP.subtract)

    def parity_reg(e):
        en = e.engine
        if en not in pregs:
            e.wait_ge(sems["s_ld"], _bias_ld[0])
            pregs[en] = e.alloc_register(f"preg{len(pregs)}")
            e.reg_load(pregs[en], parsb[:1, :1])
        return pregs[en]

    esKV = contextlib.ExitStack()
    Kres = mk(esKV, "Kres", [128, HKV * S], BF16)
    Vres = mk(esKV, "Vres", [128, HKV * S], BF16)

    esP0 = contextlib.ExitStack()
    strip = mk(esP0, "strip", [1, 2048], F32)
    trigb = mk(esP0, "trigb", [128, 2 * 2048], BF16)
    colB = mk(esP0, "colB", [128, 64], F32)
    wmodbuf = mk(esP0, "wmodbuf", [128, 2 * 2048], BF16)
    Tg1 = mk(esP0, "Tg1", [128, 2048], F32)
    Tg2 = mk(esP0, "Tg2", [128, 2048], F32)
    Tg3 = mk(esP0, "Tg3", [128, 2048], F32)
    Tg4 = mk(esP0, "Tg4", [128, 2048], F32)
    Tg5 = mk(esP0, "Tg5", [128, 2048], F32)
    ccol = mk(esP0, "ccol", [128, EC], F32)

    MSUP = 2048 if E6 % 2048 == 0 else 1024
    n_msup = E6 // MSUP
    n_msub = MSUP // 512

    esB = contextlib.ExitStack()
    blk = esB.enter_context(nc.Block("p0", no_gpsimd_drain=True))

    @blk.gpsimd
    def _(g):
        g.iota(tA[:, :128], [[-1, 128]], base=0, channel_multiplier=1,
               allow_small_or_imprecise_dtypes=True)
        inc(g.iota(colA[:, 0:1], [[0, 1]], base=0, channel_multiplier=1,
                   allow_small_or_imprecise_dtypes=True), "s_gp")
    iota_gp = cnt["s_gp"]

    @blk.sync
    def _(s):
        inc(s.dma_start(parsb[:, :], par_in), "s_ld", 16)

    @blk.vector
    def _(v):
        v.wait_ge(sems["s_gp"], iota_gp)
        v.memset(onesb[:, :], 1.0)
        v.memset(onesf[:, :], 1.0)
        v.memset(qmax2[:, :], 0.0)
        v.memset(kmax2[:, :], 0.0)
        v.tensor_single_scalar(identb[:, :], tA[:, :128], 0.0, op=OP.is_equal)
        v.tensor_single_scalar(tB[:, :128], tA[:, :128], 64.0, op=OP.is_equal)
        v.tensor_single_scalar(tC[:, :128], tA[:, :128], -64.0, op=OP.is_equal)
        v.tensor_scalar(colA[:, 1:2], colA[:, 0:1], 64.0, None, op0=OP.is_ge)
        v.drain()
        v.tensor_tensor(swapb[:, :], tB[:, :128], tC[:, :128], op=OP.add)
        v.scalar_tensor_tensor(colA[:, 2:3], colA[:, 1:2], -64.0, colA[:, 0:1],
                               op0=OP.mult, op1=OP.add)
        v.tensor_scalar(colA[:, 3:4], colA[:, 1:2], 2.0, -1.0,
                        op0=OP.mult, op1=OP.add)
        v.drain()
        inc(v.nop(), "s_dv")
    const_dv = cnt["s_dv"]

    @blk.sync
    def _(s):
        A_ = dict(allow_slow_non_contiguous=True)
        inc(s.dma_start(biasq[:, :], col_of(bq_in, H), **A_), "s_ld", 16)
        inc(s.dma_start(biask[:, :], col_of(bk_in, HKV), **A_), "s_ld", 16)
        inc(s.dma_start(biasg[:, :], col_of(gb_in, HC), **A_), "s_ld", 16)
        inc(s.dma_start(biasvb[:, :], col_of(vb_in, HC), **A_), "s_ld", 16)
        inc(s.dma_start(biaso[:, :], col_of(ob_in, EC), **A_), "s_ld", 16)
        inc(s.dma_start(ccol[:, :], col_of(c_in, EC), **A_), "s_ld", 16)
        inc(s.dma_start(strip[:1, :KVD], bv_in), "s_ld", 16)
    bias_ld = cnt["s_ld"]
    _bias_ld[0] = bias_ld

    @blk.scalar
    def _(sc):
        sc.wait_ge(sems["s_ld"], bias_ld)
        sc.activation(scb[:, :], ccol[:, :], AF.Silu)
        sc.wait_ge(sems["s_dv"], const_dv)
        inc(sc.activation(colA[:, 4:5], colA[:, 2:3], AF.Exp,
                          scale=-math.log(cfg.theta) / (DH // 2)), "s_ac")
    silu_ac = cnt["s_ac"]

    @blk.vector
    def _(v):
        v.wait_ge(sems["s_ld"], bias_ld)
        v.memset(tE16[:, :KVD], 0.0)
        v.drain()
        inc(v.tensor_copy(tE16[:1, :KVD], strip[:1, :KVD]), "s_dv")
    bv_dv = cnt["s_dv"]

    @blk.tensor
    def _(te):
        te.wait_ge(sems["s_dv"], bv_dv)
        inc(te.matmul(pb[6][:, :KVD], onesb[:, :], tE16[:, :KVD],
                      start=True, stop=True), "s_pe")
    bv_pe = cnt["s_pe"]

    @blk.vector
    def _(v):
        v.wait_ge(sems["s_pe"], bv_pe)
        inc(v.tensor_copy(bvbank[:, :], pb[6][:, :KVD]), "s_dv")

    mm_pe0 = cnt["s_pe"]
    n_wm_jobs = n_msup * EC

    side_jobs = []
    for b0 in range(NB):
        side_jobs.append(("c_x", x16[b0 * TB:(b0 + 1) * TB, :],
                          x_in[b0 * TB:(b0 + 1) * TB, :]))
    side_jobs.append(("c_aw", wk16, wk_in))
    side_jobs.append(("c_aw", wv16, wv_in))
    for i in range(4):
        side_jobs.append(("c_aw", wq16[i * (E // 4):(i + 1) * (E // 4), :],
                          wq_in[i * (E // 4):(i + 1) * (E // 4), :]))
    for i in range(4):
        side_jobs.append(("c_aw", wo16[i * (E // 4):(i + 1) * (E // 4), :],
                          wo_in[i * (E // 4):(i + 1) * (E // 4), :]))
    NFF = 8
    for w_in_, w16_, R in ((gw_in, gw16, HID), (vw_in, vw16, HID),
                           (ow_in, ow16, E)):
        for i in range(NFF):
            side_jobs.append(("c_ff",
                              w16_[i * (R // NFF):(i + 1) * (R // NFF), :],
                              w_in_[i * (R // NFF):(i + 1) * (R // NFF), :]))

    @blk.gpsimd
    def _(g):
        si = 0
        for j in range(n_wm_jobs):
            ms, c = divmod(j, EC)
            if j >= 2:
                g.wait_ge(sems["s_pe"], mm_pe0 + (j - 1))
            buf = (j % 2) * MSUP
            inc(g.dma_start(wmodbuf[:, buf:buf + MSUP],
                            wmod_in[c * 128:(c + 1) * 128,
                                    ms * MSUP:(ms + 1) * MSUP]),
                "c_wm" if j % 2 == 0 else "c_wm2", 16)
            while si * n_wm_jobs < (j + 1) * len(side_jobs):
                sj = side_jobs[si]
                inc(g.dma_start(sj[1], sj[2]), sj[0], 16)
                si += 1
        while si < len(side_jobs):
            sj = side_jobs[si]
            inc(g.dma_start(sj[1], sj[2]), sj[0], 16)
            si += 1
    cx_total = cnt["c_x"]
    caw_total = cnt["c_aw"]
    cff_total = cnt["c_ff"]
    caw_wkv = caw_total
    caw_wq = caw_total
    caw_wo = caw_total

    for ms in range(n_msup):
        @blk.tensor
        def _(te, ms=ms):
            te.wait_ge(sems["s_ac"], silu_ac)
            if ms > 0:
                te.wait_ge(sems["s_dv"], cnt["s_dv"])
            for c in range(EC):
                j = ms * EC + c
                te.wait_ge(sems["c_wm" if j % 2 == 0 else "c_wm2"],
                           16 * (j // 2 + 1))
                buf = (j % 2) * MSUP
                for msub in range(n_msub):
                    m_ = te.matmul(
                        pb[msub][:1, :], scb[:, c:c + 1],
                        wmodbuf[:, buf + msub * 512:buf + (msub + 1) * 512],
                        start=(c == 0), stop=(c == EC - 1))
                if True:
                    inc(m_, "s_pe")

        @blk.vector
        def _(v, ms=ms):
            v.wait_ge(sems["s_pe"], cnt["s_pe"])
            if ms > 0:
                v.wait_ge(sems["s_st"], 16 * ms)
            for msub in range(n_msub):
                v.tensor_copy(strip[:, msub * 512:(msub + 1) * 512],
                              pb[msub][:1, :])
            v.drain()
            inc(v.nop(), "s_dv")

        @blk.scalar
        def _(sc, ms=ms):
            sc.wait_ge(sems["s_dv"], cnt["s_dv"])
            inc(sc.dma_start(modscr[:, ms * MSUP:(ms + 1) * MSUP],
                             strip[:, :MSUP]), "s_st", 16)

    @blk.sync
    def _(s):
        s.wait_ge(sems["s_st"], 16 * n_msup)
        inc(s.dma_start(modc[:, :], col_of(modscr, 6 * EC),
                        allow_slow_non_contiguous=True), "s_ld", 16)
    modc_ld = cnt["s_ld"]

    @blk.scalar
    def _(sc):
        sc.wait_ge(sems["s_ld"], modc_ld)
        sc.activation(tB[:, 0:EC], modc[:, 2 * EC:3 * EC], AF.Sign)
        sc.activation(tB[:, EC:2 * EC], modc[:, 5 * EC:6 * EC], AF.Sign)
        sc.activation(tD[:, 0:EC], modc[:, 2 * EC:3 * EC], AF.Abs)
        inc(sc.activation(tD[:, EC:2 * EC], modc[:, 5 * EC:6 * EC], AF.Abs),
            "s_ac")
    sign_ac = cnt["s_ac"]

    @blk.vector
    def _(v):
        v.wait_ge(sems["s_ac"], sign_ac)
        v.tensor_scalar(tC[:, 0:EC], tD[:, 0:EC], 0.25, None, op0=OP.max)
        v.tensor_scalar(tC[:, EC:2 * EC], tD[:, EC:2 * EC], 0.25, None,
                        op0=OP.max)
        v.drain()
        v.tensor_tensor(modc[:, 2 * EC:3 * EC], tB[:, 0:EC], tC[:, 0:EC],
                        op=OP.mult)
        v.tensor_tensor(modc[:, 5 * EC:6 * EC], tB[:, EC:2 * EC],
                        tC[:, EC:2 * EC], op=OP.mult)
        v.drain()
        v.tensor_scalar(tC[:, 0:EC], modc[:, EC:2 * EC], 1.0, None, op0=OP.add)
        v.tensor_scalar(tC[:, EC:2 * EC], modc[:, 4 * EC:5 * EC], 1.0, None,
                        op0=OP.add)
        v.drain()
        v.tensor_copy(modc[:, EC:2 * EC], tC[:, 0:EC])
        v.tensor_copy(modc[:, 4 * EC:5 * EC], tC[:, EC:2 * EC])
        v.drain()
        inc(v.nop(), "s_dv")
    mod_done_dv = cnt["s_dv"]

    def cody3(v, k_ap, ang_ap, o1, o2, o3):
        v.scalar_tensor_tensor(o1, k_ap, -TP1, ang_ap, op0=OP.mult, op1=OP.add)
        v.drain()
        v.scalar_tensor_tensor(o2, k_ap, -TP2, o1, op0=OP.mult, op1=OP.add)
        v.drain()
        v.scalar_tensor_tensor(o3, k_ap, -TP3, o2, op0=OP.mult, op1=OP.add)
        v.drain()

    def ladder_round(v, a_ap, u_ap, k_ap, bit_ap, tmp_ap, nbits):
        v.tensor_copy(u_ap, a_ap)
        v.memset(k_ap, 0.0)
        v.drain()
        for j in range(nbits - 1, -1, -1):
            Bj = float(np.float32((1 << j) * 2 * math.pi))
            v.tensor_scalar(bit_ap, u_ap, Bj, None, op0=OP.is_ge)
            v.drain()
            v.scalar_tensor_tensor(tmp_ap, bit_ap, -Bj, u_ap,
                                   op0=OP.mult, op1=OP.add)
            v.drain()
            v.tensor_copy(u_ap, tmp_ap)
            v.drain()
            v.scalar_tensor_tensor(tmp_ap, bit_ap, float(1 << j), k_ap,
                                   op0=OP.mult, op1=OP.add)
            v.drain()
            v.tensor_copy(k_ap, tmp_ap)
            v.drain()
        v.tensor_scalar(bit_ap, u_ap, PI_F, None, op0=OP.is_ge)
        v.drain()
        v.tensor_tensor(tmp_ap, k_ap, bit_ap, op=OP.add)
        v.drain()
        v.tensor_copy(k_ap, tmp_ap)
        v.drain()

    def wrap_cos3(v, m_ap, c_ap, t1_ap, out_ap):
        v.tensor_scalar(t1_ap, m_ap, HALF_PI, None, op0=OP.add)
        v.tensor_scalar(c_ap, m_ap, HALF_PI, None, op0=OP.is_gt)
        v.drain()
        v.scalar_tensor_tensor(out_ap, c_ap, -TP1, t1_ap,
                               op0=OP.mult, op1=OP.add)
        v.drain()
        v.scalar_tensor_tensor(t1_ap, c_ap, -TP2, out_ap,
                               op0=OP.mult, op1=OP.add)
        v.drain()
        v.scalar_tensor_tensor(out_ap, c_ap, -TP3, t1_ap,
                               op0=OP.mult, op1=OP.add)
        v.drain()
        v.tensor_scalar(out_ap, out_ap, PI_F, -PI_F, op0=OP.min, op1=OP.max)
        v.drain()

    @blk.gpsimd
    def _(g, w=cnt["s_dv"]):
        g.wait_ge(sems["s_dv"], w)
        inc(g.iota(Tg1[:, :TB], [[1, TB]], base=0, channel_multiplier=0,
                   allow_small_or_imprecise_dtypes=True), "s_gp")

    @blk.vector
    def _(v, w=cnt["s_gp"]):
        v.wait_ge(sems["s_gp"], w)
        v.wait_ge(sems["s_ac"], silu_ac)
        v.tensor_scalar(Tg3[:, :TB], Tg1[:, :TB], colA[:, 4:5], None,
                        op0=OP.mult)
        v.drain()
        ladder_round(v, Tg3[:, :TB], Tg1[:, :TB], Tg4[:, :TB],
                     Tg2[:, :TB], Tg5[:, :TB], 7)
        cody3(v, Tg4[:, :TB], Tg3[:, :TB], Tg1[:, :TB], Tg2[:, :TB],
              Tg5[:, :TB])
        v.tensor_scalar(Tg4[:, :TB], Tg5[:, :TB], PI_F, -PI_F,
                        op0=OP.min, op1=OP.max)
        v.drain()
        wrap_cos3(v, Tg4[:, :TB], Tg3[:, :TB], Tg5[:, :TB], Tg1[:, :TB])
        inc(v.nop(), "s_dv")

    @blk.scalar
    def _(sc, w=cnt["s_dv"]):
        sc.wait_ge(sems["s_dv"], w)
        sc.activation(Tg2[:, :TB], Tg4[:, :TB], AF.Sin)
        inc(sc.activation(Tg5[:, :TB], Tg1[:, :TB], AF.Sin), "s_ac")
    lo_ac = cnt["s_ac"]

    n_tch = S // 2048
    n_tb = 2048 // TB
    CB = lambda i: colB[:, i * n_tb:(i + 1) * n_tb]
    for ch in range(n_tch):
        @blk.gpsimd
        def _(g, ch=ch, w=cnt["s_st"], w2=cnt["s_ac"]):
            g.wait_ge(sems["s_st"], w)
            g.wait_ge(sems["s_ac"], w2)
            inc(g.iota(CB(0), [[512, n_tb]], base=2048 * ch,
                       channel_multiplier=0,
                       allow_small_or_imprecise_dtypes=True), "s_gp")

        @blk.vector
        def _(v, ch=ch, w=cnt["s_gp"], wl=lo_ac):
            v.wait_ge(sems["s_gp"], w)
            v.wait_ge(sems["s_ac"], wl)
            v.tensor_scalar(CB(4), CB(0), colA[:, 4:5], None, op0=OP.mult)
            v.drain()
            v.tensor_copy(CB(0), CB(4))
            v.drain()
            ladder_round(v, CB(0), CB(1), CB(2), CB(3), CB(4), 10)
            cody3(v, CB(2), CB(0), CB(1), CB(3), CB(5))
            v.tensor_scalar(CB(5), CB(5), PI_F, -PI_F, op0=OP.min, op1=OP.max)
            v.drain()
            wrap_cos3(v, CB(5), CB(3), CB(4), CB(11))
            inc(v.nop(), "s_dv")

        @blk.scalar
        def _(sc, w=cnt["s_dv"]):
            sc.wait_ge(sems["s_dv"], w)
            sc.activation(CB(6), CB(5), AF.Sin)
            inc(sc.activation(CB(7), CB(11), AF.Sin), "s_ac")

        @blk.vector
        def _(v, ch=ch, w=cnt["s_ac"]):
            v.wait_ge(sems["s_ac"], w)
            v.tensor_scalar(CB(8), CB(6), -1.0, None, op0=OP.mult)
            v.tensor_scalar(CB(9), CB(6), colA[:, 3:4], None, op0=OP.mult)
            v.tensor_scalar(CB(10), CB(7), colA[:, 3:4], None, op0=OP.mult)
            v.drain()
            for j in range(n_tb):
                sl_c = slice(j * TB, (j + 1) * TB)
                v.tensor_scalar(Tg3[:, :TB], Tg5[:, :TB],
                                colB[:, 7 * n_tb + j:7 * n_tb + j + 1],
                                None, op0=OP.mult)
                v.drain()
                v.scalar_tensor_tensor(
                    trigb[:, j * TB:(j + 1) * TB], Tg2[:, :TB],
                    colB[:, 8 * n_tb + j:8 * n_tb + j + 1], Tg3[:, :TB],
                    op0=OP.mult, op1=OP.add)
                v.tensor_scalar(Tg4[:, :TB], Tg5[:, :TB],
                                colB[:, 9 * n_tb + j:9 * n_tb + j + 1],
                                None, op0=OP.mult)
                v.drain()
                v.scalar_tensor_tensor(
                    trigb[:, 2048 + j * TB:2048 + (j + 1) * TB], Tg2[:, :TB],
                    colB[:, 10 * n_tb + j:10 * n_tb + j + 1], Tg4[:, :TB],
                    op0=OP.mult, op1=OP.add)
                v.drain()
            inc(v.nop(), "s_dv")

        @blk.scalar
        def _(sc, ch=ch, w=cnt["s_dv"]):
            sc.wait_ge(sems["s_dv"], w)
            inc(sc.dma_start(trigscr[:, ch * 2048:(ch + 1) * 2048],
                             trigb[:, 0:2048]), "s_st", 16)
            inc(sc.dma_start(trigscr[:, S + ch * 2048:S + (ch + 1) * 2048],
                             trigb[:, 2048:4096]), "s_st", 16)
    trig_st = cnt["s_st"]

    if "modc" in dbg:
        @blk.scalar
        def _(sc):
            sc.wait_ge(sems["s_dv"], mod_done_dv)
            tap(sc, "modc", modc[:, :], 128, 6 * EC)


    esB.close()
    esP0.close()

    esP1 = contextlib.ExitStack()
    xt = mk(esP1, "xt", [128, 2 * EC * TB], BF16)
    h1t = mk(esP1, "h1t", [128, EC * TB], BF16)
    wkv = mk(esP1, "wkv", [128, EC * 2 * KVD], BF16)
    ktrig = mk(esP1, "ktrig", [128, 2 * TB], BF16)

    esB = contextlib.ExitStack()
    blk = esB.enter_context(nc.Block("p1a", no_gpsimd_drain=True))

    my_par = {p: cfg.my_blocks(p) for p in (0, 1)}
    h1_stores = {0: 0, 1: 0}
    dv_end = {}
    pe_end = {}

    for m in range(NB):
        bo = (m % 2) * EC * TB

        @blk.sync
        def _(s, m=m, bo=bo):
            s.wait_ge(sems["c_x"], cx_total)
            if m >= 2:
                s.wait_ge(sems["s_dv"], dv_end[m - 2])
            if m == 0:
                s.wait_ge(sems["s_st"], trig_st)
                s.wait_ge(sems["c_aw"], caw_wkv)
                for c in range(EC):
                    inc(s.dma_start_transpose(
                        wkv[:, c * 2 * KVD:c * 2 * KVD + KVD],
                        wk16[0:KVD, c * 128:(c + 1) * 128]), "s_wt", 16)
                    inc(s.dma_start_transpose(
                        wkv[:, c * 2 * KVD + KVD:(c + 1) * 2 * KVD],
                        wv16[0:KVD, c * 128:(c + 1) * 128]), "s_wt", 16)
            for c in range(EC):
                inc(s.dma_start_transpose(
                    xt[:, bo + c * TB:bo + (c + 1) * TB],
                    x16[m * TB:(m + 1) * TB, c * 128:(c + 1) * 128]),
                    "s_x0" if m % 2 == 0 else "s_x1", 16)
            if m > 0:
                s.wait_ge(sems["s_dv"], dv_end[m - 1])
            inc(s.dma_start(ktrig[:, 0:TB],
                            trigscr[:, m * TB:(m + 1) * TB]), "s_wt", 16)
            inc(s.dma_start(ktrig[:, TB:2 * TB],
                            trigscr[:, S + m * TB:S + (m + 1) * TB]),
                "s_wt", 16)
        xsem = "s_x0" if m % 2 == 0 else "s_x1"
        xld = cnt[xsem]
        ktrig_wt = cnt["s_wt"]

        @blk.scalar
        def _(sc, m=m, bo=bo, xld=xld, xsem=xsem):
            sc.wait_ge(sems[xsem], xld)
            if m > 0:
                sc.wait_ge(sems["s_pe"], pe_end[m - 1])
                reg = parity_reg(sc)
                with sc.If_eq(reg, 0):
                    sc.wait_ge(sems["s_h1"], 16 * h1_stores[0])
                with sc.Else():
                    sc.wait_ge(sems["s_h1"], 16 * h1_stores[1])
            for c in range(EC):
                a = sc.activation(h1t[:, c * TB:(c + 1) * TB],
                                  xt[:, bo + c * TB:bo + (c + 1) * TB],
                                  AF.Square)
            inc(a, "s_ac")
        sq_ac = cnt["s_ac"]

        @blk.tensor
        def _(te, m=m, bo=bo, sq_ac=sq_ac):
            te.wait_ge(sems["s_ac"], sq_ac)
            if m > 0:
                te.wait_ge(sems["s_dv"], dv_end[m - 1])
            for c in range(EC):
                mm = te.matmul(pb[0][:, :], onesb[:, :],
                               xt[:, bo + c * TB:bo + (c + 1) * TB],
                               start=(c == 0), stop=(c == EC - 1))
            inc(mm, "s_pe")
            for c in range(EC):
                mm = te.matmul(pb[1][:, :], onesb[:, :],
                               h1t[:, c * TB:(c + 1) * TB],
                               start=(c == 0), stop=(c == EC - 1))
            inc(mm, "s_pe")
        sums_pe = cnt["s_pe"]

        @blk.vector
        def _(v, sums_pe=sums_pe):
            v.wait_ge(sems["s_pe"], sums_pe)
            v.tensor_scalar(tA[:, :], pb[0][:, :], 1.0 / E, None, op0=OP.mult)
            v.drain()
            v.tensor_tensor(tB[:, :], tA[:, :], tA[:, :], op=OP.mult)
            v.drain()
            v.scalar_tensor_tensor(tC[:, :], pb[1][:, :], 1.0 / E, tB[:, :],
                                   op0=OP.mult, op1=OP.subtract)
            v.drain()
            inc(v.tensor_scalar(tB[:, :], tC[:, :], cfg.eps, None, op0=OP.add),
                "s_dv")
        var_dv = cnt["s_dv"]

        @blk.scalar
        def _(sc, var_dv=var_dv):
            sc.wait_ge(sems["s_dv"], var_dv)
            inc(sc.activation(tC[:, :], tB[:, :], AF.Sqrt), "s_ac")
        sqrt_ac = cnt["s_ac"]

        @blk.vector
        def _(v, m=m, bo=bo, sqrt_ac=sqrt_ac):
            v.wait_ge(sems["s_ac"], sqrt_ac)
            v.reciprocal(tB[:, :], tC[:, :])
            v.drain()
            for c in range(EC):
                v.tensor_tensor(tC[:, :], xt[:, bo + c * TB:bo + (c + 1) * TB],
                                tA[:, :], op=OP.subtract)
                v.drain()
                v.tensor_tensor(tD[:, :], tC[:, :], tB[:, :], op=OP.mult)
                v.drain()
                v.tensor_scalar(h1t[:, c * TB:(c + 1) * TB], tD[:, :],
                                modc[:, EC + c:EC + c + 1],
                                modc[:, c:c + 1], op0=OP.mult, op1=OP.add)
                v.drain()
            inc(v.nop(), "s_dv")
        h1_dv = cnt["s_dv"]

        @blk.vector
        def _(v, m=m, bo=bo):
            reg = parity_reg(v)
            for p in (0, 1):
                ctxm = v.If_eq(reg, 0) if p == 0 else v.Else()
                with ctxm:
                    if m in my_par[p]:
                        gslot = my_par[p].index(m)
                        for c in range(EC):
                            v.tensor_copy(
                                xres[:, c * MYT + gslot * TB:
                                     c * MYT + (gslot + 1) * TB],
                                xt[:, bo + c * TB:bo + (c + 1) * TB])
                        v.drain()
            inc(v.nop(), "s_dv")

        @blk.scalar
        def _(sc, m=m, h1_dv=h1_dv):
            sc.wait_ge(sems["s_dv"], h1_dv)
            reg = parity_reg(sc)
            for p in (0, 1):
                ctxm = sc.If_eq(reg, 0) if p == 0 else sc.Else()
                with ctxm:
                    if m in my_par[p]:
                        gslot = my_par[p].index(m)
                        sc.dma_start(
                            h1scr[:, gslot * EC * TB:(gslot + 1) * EC * TB],
                            h1t[:, :]).then_inc(sems["s_h1"], 16)
                        h1_stores[p] += 1

        k_chain_pe = {}
        v_chain_pe = {}

        @blk.tensor
        def _(te, m=m, h1_dv=h1_dv, ktw=ktrig_wt):
            te.wait_ge(sems["s_dv"], h1_dv)
            if m == 0:
                te.wait_ge(sems["s_wt"], ktw)
            for kv in range(HKV):
                for c in range(EC):
                    mm = te.matmul(
                        pb[kv][:, :],
                        wkv[:, c * 2 * KVD + kv * DH:c * 2 * KVD + (kv + 1) * DH],
                        h1t[:, c * TB:(c + 1) * TB],
                        start=(c == 0), stop=(c == EC - 1))
                inc(mm, "s_pe")
                k_chain_pe[kv] = cnt["s_pe"]
            for tc in range(2):
                for c in range(EC):
                    mm = te.matmul(
                        pb[2 + tc][:, :KVD],
                        h1t[:, c * TB + tc * 128:c * TB + (tc + 1) * 128],
                        wkv[:, c * 2 * KVD + KVD:(c + 1) * 2 * KVD],
                        start=(c == 0), stop=(c == EC - 1))
                inc(mm, "s_pe")
                v_chain_pe[tc] = cnt["s_pe"]

        def emit_krope(kv):
            @blk.vector
            def _(v, m=m, kv=kv, w=cnt["s_pe"]):
                v.wait_ge(sems["s_pe"], w)
                inc(v.tensor_scalar(tE16[:, :], pb[kv][:, :],
                                    biask[:, kv:kv + 1], None, op0=OP.add),
                    "s_dv")
            ksb_dv = cnt["s_dv"]

            @blk.tensor
            def _(te, ksb_dv=ksb_dv):
                te.wait_ge(sems["s_dv"], ksb_dv)
                inc(te.matmul(pb[4][:, :], swapb[:, :], tE16[:, :],
                              start=True, stop=True), "s_pe")
            swp_pe = cnt["s_pe"]

            @blk.vector
            def _(v, m=m, kv=kv, swp_pe=swp_pe, ktw=ktrig_wt):
                v.wait_ge(sems["s_wt"], ktw)
                v.tensor_tensor(tA[:, :], tE16[:, :],
                                ktrig[:, 0:TB], op=OP.mult)
                v.wait_ge(sems["s_pe"], swp_pe)
                v.tensor_tensor(tC[:, :], pb[4][:, :],
                                ktrig[:, TB:2 * TB], op=OP.mult)
                v.drain()
                inc(v.tensor_tensor(
                    Kres[:, kv * S + m * TB:kv * S + (m + 1) * TB],
                    tA[:, :], tC[:, :], op=OP.add), "s_dv")
            krope_dv = cnt["s_dv"]

            @blk.scalar
            def _(sc, m=m, kv=kv, krope_dv=krope_dv):
                sc.wait_ge(sems["s_dv"], krope_dv)
                inc(sc.activation(
                    tF16[:, :],
                    Kres[:, kv * S + m * TB:kv * S + (m + 1) * TB],
                    AF.Square), "s_ac")
            ksq_ac = cnt["s_ac"]

            @blk.tensor
            def _(te, ksq_ac=ksq_ac):
                te.wait_ge(sems["s_ac"], ksq_ac)
                inc(te.matmul(pb[5][:, :], onesb[:, :], tF16[:, :],
                              start=True, stop=True), "s_pe")
            ksum_pe = cnt["s_pe"]

            @blk.vector
            def _(v, kv=kv, ksum_pe=ksum_pe):
                v.wait_ge(sems["s_pe"], ksum_pe)
                v.tensor_reduce(colA[:, 5:6], pb[5][:, :], axis=AX.X,
                                op=OP.max)
                v.drain()
                v.tensor_tensor(colA[:, 6:7], kmax2[:, kv:kv + 1],
                                colA[:, 5:6], op=OP.max)
                v.drain()
                v.tensor_copy(kmax2[:, kv:kv + 1], colA[:, 6:7])
                v.drain()
                inc(v.nop(), "s_dv")

        def emit_vevac(tc):
            @blk.vector
            def _(v, m=m, tc=tc, vpe=cnt["s_pe"]):
                v.wait_ge(sems["s_pe"], vpe)
                v.drain()
                kt = 4 * m + tc
                v.tensor_tensor(tF16[:, :KVD], pb[2 + tc % 2][:, :KVD],
                                bvbank[:, :], op=OP.add)
                v.drain()
                v.tensor_copy(Vres[:, kt * 128:(kt + 1) * 128],
                              tF16[:, 0:DH])
                inc(v.tensor_copy(Vres[:, S + kt * 128:S + (kt + 1) * 128],
                                  tF16[:, DH:KVD]), "s_dv")

        emit_krope(0)
        emit_vevac(0)
        emit_krope(1)
        emit_vevac(1)
        evac01_dv = cnt["s_dv"]

        @blk.tensor
        def _(te, m=m, evac01_dv=evac01_dv):
            te.wait_ge(sems["s_dv"], evac01_dv)
            for tc in range(2, 4):
                for c in range(EC):
                    mm = te.matmul(
                        pb[2 + tc % 2][:, :KVD],
                        h1t[:, c * TB + tc * 128:c * TB + (tc + 1) * 128],
                        wkv[:, c * 2 * KVD + KVD:(c + 1) * 2 * KVD],
                        start=(c == 0), stop=(c == EC - 1))
                inc(mm, "s_pe")
                v_chain_pe[tc] = cnt["s_pe"]
        pe_end[m] = cnt["s_pe"]

        emit_vevac(2)
        emit_vevac(3)
        dv_end[m] = cnt["s_dv"]

    esB.close()

    if "h1scr" in dbg or "Kres" in dbg:
        esB = contextlib.ExitStack()
        blk = esB.enter_context(nc.Block("p1dbg", no_gpsimd_drain=True))

        @blk.scalar
        def _(sc):
            tap(sc, "Kres", Kres[:, :], 128, HKV * S)
            tap(sc, "Vres", Vres[:, :], 128, HKV * S)
            tap(sc, "xres", xres[:, :], 128, EC * MYT)
        esB.close()

    esP1.close()
    Qres = mk(esKV, "Qres", [128, H * MYT], BF16)
    esP1b = contextlib.ExitStack()
    h1tb = mk(esP1b, "h1tb", [128, EC * TB], BF16)
    wqb = mk(esP1b, "wqb", [128, 2 * TB], BF16)
    qtrig = mk(esP1b, "qtrig", [128, 2 * TB], BF16)

    esB = contextlib.ExitStack()
    blk = esB.enter_context(nc.Block("p1b", no_gpsimd_drain=True))

    for g in range(MYNB):
        @blk.sync
        def _(s, g=g, w=cnt["s_pe"], wdv=cnt["s_dv"]):
            s.wait_ge(sems["s_h1"], 16 * MYNB)
            if g > 0:
                s.wait_ge(sems["s_pe"], w)
                s.wait_ge(sems["s_dv"], wdv)
            inc(s.dma_start(h1tb[:, :],
                            h1scr[:, g * EC * TB:(g + 1) * EC * TB]),
                "s_ld", 16)
            r = parity_reg(s)
            for p in (0, 1):
                ctxm = s.If_eq(r, 0) if p == 0 else s.Else()
                mb = my_par[p][g]
                with ctxm:
                    s.dma_start(qtrig[:, 0:TB],
                                trigscr[:, mb * TB:(mb + 1) * TB]
                                ).then_inc(sems["s_qt"], 16)
                    s.dma_start(qtrig[:, TB:2 * TB],
                                trigscr[:, S + mb * TB:S + (mb + 1) * TB]
                                ).then_inc(sems["s_qt"], 16)
            cnt["s_qt"] += 32
        h1_ld = cnt["s_ld"]
        qtrig_wt = cnt["s_qt"]

        for qg in range(H // 4):
            pe_base = cnt["s_pe"]

            for d in range(EC):
                @blk.sync
                def _(s, g=g, qg=qg, d=d, pe_base=pe_base, w=cnt["s_dv"]):
                    if d == 0:
                        s.wait_ge(sems["c_aw"], caw_wq)
                        s.wait_ge(sems["s_dv"], w)
                    if d < 2:
                        s.wait_ge(sems["s_pe"], pe_base)
                    else:
                        s.wait_ge(sems["s_pe"], pe_base + d - 1)
                    inc(s.dma_start_transpose(
                        wqb[:, (d % 2) * TB:(d % 2 + 1) * TB],
                        wq16[qg * 512:(qg + 1) * 512, d * 128:(d + 1) * 128]),
                        "s_qw" if d % 2 == 0 else "s_wt", 16)

            qw_even = cnt["s_qw"]
            qw_odd = cnt["s_wt"]

            @blk.tensor
            def _(te, g=g, qg=qg, h1_ld=h1_ld, pe_base=pe_base,
                  qw_even=qw_even, qw_odd=qw_odd, w=cnt["s_dv"]):
                te.wait_ge(sems["s_ld"], h1_ld)
                te.wait_ge(sems["s_dv"], w)
                for d in range(EC):
                    if d % 2 == 0:
                        te.wait_ge(sems["s_qw"],
                                   qw_even - 16 * ((EC + 1) // 2 - d // 2 - 1))
                    else:
                        te.wait_ge(sems["s_wt"],
                                   qw_odd - 16 * (EC // 2 - d // 2 - 1))
                    for hh in range(4):
                        mm = te.matmul(
                            pb[hh][:, :],
                            wqb[:, (d % 2) * TB + hh * 128:
                                (d % 2) * TB + (hh + 1) * 128],
                            h1tb[:, d * TB:(d + 1) * TB],
                            start=(d == 0), stop=(d == EC - 1))
                    inc(mm, "s_pe")

            for hh in range(4):
                h = qg * 4 + hh
                kv = h // GQ
                qmm_pe = cnt["s_pe"]

                @blk.vector
                def _(v, h=h, hh=hh, qmm_pe=qmm_pe):
                    v.wait_ge(sems["s_pe"], qmm_pe)
                    inc(v.tensor_scalar(tE16[:, :], pb[hh][:, :],
                                        biasq[:, h:h + 1], None, op0=OP.add),
                        "s_dv")
                qsb_dv = cnt["s_dv"]

                @blk.tensor
                def _(te, qsb_dv=qsb_dv):
                    te.wait_ge(sems["s_dv"], qsb_dv)
                    inc(te.matmul(pb[4][:, :], swapb[:, :], tE16[:, :],
                                  start=True, stop=True), "s_pe")
                qswp_pe = cnt["s_pe"]

                @blk.vector
                def _(v, g=g, h=h, qswp_pe=qswp_pe, qtw=qtrig_wt):
                    v.wait_ge(sems["s_qt"], qtw)
                    v.tensor_tensor(tA[:, :], tE16[:, :], qtrig[:, 0:TB],
                                    op=OP.mult)
                    v.wait_ge(sems["s_pe"], qswp_pe)
                    v.tensor_tensor(tC[:, :], pb[4][:, :], qtrig[:, TB:2 * TB],
                                    op=OP.mult)
                    v.drain()
                    v.tensor_tensor(
                        Qres[:, h * MYT + g * TB:h * MYT + (g + 1) * TB],
                        tA[:, :], tC[:, :], op=OP.add)
                    v.drain()
                    inc(v.nop(), "s_dv")
                qrope_dv = cnt["s_dv"]

                @blk.scalar
                def _(sc, g=g, h=h, qrope_dv=qrope_dv):
                    sc.wait_ge(sems["s_dv"], qrope_dv)
                    inc(sc.activation(
                        tF16[:, :],
                        Qres[:, h * MYT + g * TB:h * MYT + (g + 1) * TB],
                        AF.Square), "s_ac")
                qsq_ac = cnt["s_ac"]

                @blk.tensor
                def _(te, qsq_ac=qsq_ac):
                    te.wait_ge(sems["s_ac"], qsq_ac)
                    inc(te.matmul(pb[5][:, :], onesb[:, :], tF16[:, :],
                                  start=True, stop=True), "s_pe")
                qsum_pe = cnt["s_pe"]

                @blk.vector
                def _(v, h=h, qsum_pe=qsum_pe):
                    v.wait_ge(sems["s_pe"], qsum_pe)
                    v.tensor_reduce(colA[:, 5:6], pb[5][:, :], axis=AX.X,
                                    op=OP.max)
                    v.drain()
                    v.tensor_tensor(colA[:, 6:7], qmax2[:, h:h + 1],
                                    colA[:, 5:6], op=OP.max)
                    v.drain()
                    v.tensor_copy(qmax2[:, h:h + 1], colA[:, 6:7])
                    v.drain()
                    inc(v.nop(), "s_dv")

    @blk.vector
    def _(v, w=cnt["s_dv"]):
        v.wait_ge(sems["s_dv"], w)
        for h in range(H):
            v.tensor_tensor(tB[:, h:h + 1], qmax2[:, h:h + 1],
                            kmax2[:, h // GQ:h // GQ + 1], op=OP.mult)
        v.drain()
        inc(v.nop(), "s_dv")
    nb_dv = cnt["s_dv"]

    @blk.scalar
    def _(sc, nb_dv=nb_dv):
        sc.wait_ge(sems["s_dv"], nb_dv)
        inc(sc.activation(tC[:, 0:H], tB[:, 0:H], AF.Sqrt, scale=1.0 / 128.0),
            "s_ac")
    nb_ac = cnt["s_ac"]

    @blk.vector
    def _(v, nb_ac=nb_ac):
        v.wait_ge(sems["s_ac"], nb_ac)
        v.tensor_scalar(negB[:, :], tC[:, 0:H], -1.0, None, op0=OP.mult)
        v.drain()
        inc(v.nop(), "s_dv")

    if "Qres" in dbg:
        @blk.scalar
        def _(sc, w=cnt["s_dv"]):
            sc.wait_ge(sems["s_dv"], w)
            tap(sc, "Qres", Qres[:, :], 128, H * MYT)
            tap(sc, "negB", negB[:, :], 128, H)

    esB.close()
    esP1b.close()

    esP2w = contextlib.ExitStack()
    PBW = min(512, E)
    PCW = 512
    wtb = mk(esP2w, "wtb", [128, 2 * PBW], F16)
    wtc = mk(esP2w, "wtc", [128, 2 * PCW], F16)
    tqc = mk(esP2w, "tqc", [128, 2 * PCW], BF16)
    ttmp = mk(esP2w, "ttmp", [128, PCW], BF16)

    esP2 = contextlib.ExitStack()
    wob = mk(esP2, "wob", [128, 2048], BF16)
    attst = mk(esP2, "attst", [128, H * TB], BF16)
    ptile = mk(esP2, "ptile", [128, 2 * TB], BF16)
    rsum = tC
    masks = mk(esP2, "masks", [128, 4 * TB], BF16)

    esB = contextlib.ExitStack()
    blk = esB.enter_context(nc.Block("p2a", no_gpsimd_drain=True))

    for r in range(4):
        @blk.gpsimd
        def _(g, r=r, w=cnt["s_dv"]):
            g.wait_ge(sems["s_dv"], w)
            inc(g.iota(tC[:, :TB], [[1, TB]], base=-128 * r,
                       channel_multiplier=-1,
                       allow_small_or_imprecise_dtypes=True), "s_gp")

        @blk.vector
        def _(v, r=r, w=cnt["s_gp"]):
            v.wait_ge(sems["s_gp"], w)
            v.drain()
            inc(v.tensor_scalar(masks[:, r * TB:(r + 1) * TB], tC[:, :TB],
                                0.0, NEGBIG, op0=OP.min, op1=OP.mult), "s_dv")

    cnt0 = dict(cnt)
    cnt1 = dict(cnt)
    CN = {0: cnt0, 1: cnt1}

    def binc(p, instr, name, n=1):
        CN[p][name] += n
        instr.then_inc(sems[name], n)

    def uinc(instr, name, n=1):
        CN[0][name] += n
        CN[1][name] += n
        instr.then_inc(sems[name], n)

    W3 = [(gw16, tg8, 0), (vw16, tv8, 1), (ow16, to8, 2)]
    gtot = HID * E

    @blk.vector
    def _(v):
        v.memset(tD[:, 0:3], 0.0)
        v.drain()
        inc(v.nop(), "s_wd")

    wa_marks, wd_marks, wd2_marks = [], [], []
    wo_q_marks, wo_qctr = [], [0]
    side_jobs = []

    def mk_p2b_tile(src_ap, mi, tg, wid):
        buf = tg % 2
        lsem = "s_ld" if buf == 0 else "s_l2"

        def emit():
            @blk.sync
            def _(s, war=(wd_marks[tg - 2] if tg >= 2 else None)):
                s.wait_ge(sems["c_ff"], cff_total)
                if war is not None:
                    s.wait_ge(sems["s_wd"], war)
                inc(s.dma_start(wtb[:, buf * PBW:buf * PBW + wid], src_ap),
                    lsem, 16)
            ldm = cnt[lsem]

            @blk.vector
            def _(v, ldm=ldm):
                v.wait_ge(sems[lsem], ldm)
                v.tensor_reduce(colA[:, 5 + buf:6 + buf],
                                wtb[:, buf * PBW:buf * PBW + wid],
                                axis=AX.X, op=OP.add,
                                apply_absolute_value=True)
                v.drain()
                v.tensor_tensor(tD[:, mi:mi + 1], tD[:, mi:mi + 1],
                                colA[:, 5 + buf:6 + buf], op=OP.add)
                v.drain()
                inc(v.nop(), "s_wd")
            wd_marks.append(cnt["s_wd"])
        return emit

    def mk_gcols(mi):
        def emit():
            @blk.tensor
            def _(te, w=cnt["s_wd"]):
                te.wait_ge(sems["s_wd"], w)
                inc(te.matmul(pb[6][:, 0:1], onesf[:, :], tD[:, mi:mi + 1],
                              start=True, stop=True), "s_wp")
            wpm = cnt["s_wp"]

            @blk.vector
            def _(v, wpm=wpm):
                v.wait_ge(sems["s_wp"], wpm)
                v.tensor_scalar(gcols[:, 3 + mi:4 + mi], pb[6][:, 0:1],
                                1.0 / gtot, 1e-5, op0=OP.mult, op1=OP.max)
                v.drain()
                v.tensor_scalar(gcols[:, mi:mi + 1],
                                gcols[:, 3 + mi:4 + mi], 0.5, None,
                                op0=OP.mult)
                v.tensor_scalar(gcols[:, 6 + mi:7 + mi],
                                gcols[:, 3 + mi:4 + mi], -0.5, None,
                                op0=OP.mult)
                v.tensor_scalar(gcols[:, 9 + mi:10 + mi],
                                gcols[:, 3 + mi:4 + mi], -1.0, None,
                                op0=OP.mult)
                v.drain()
                inc(v.nop(), "s_wd")
        return emit

    def mk_p2c_tile(job, tg2):
        w16_, t8_, mi, a0, colbase, wid = job
        buf = tg2 % 2
        lsem = "s_tl0" if buf == 0 else "s_tl1"
        qsem = "s_tq" if buf == 0 else "s_tq2"

        def emit():
            @blk.sync
            def _(s, war=(wd2_marks[tg2 - 2] if tg2 >= 2 else None)):
                if war is not None:
                    s.wait_ge(sems["s_wd"], war)
                src = w16_[colbase:colbase + wid,
                           a0 * 128:(a0 + 1) * 128]
                inc(s.dma_start_transpose(
                    wtc[:, buf * PCW:buf * PCW + wid], src), lsem, 16)
            ldm = cnt[lsem]

            @blk.vector
            def _(v, ldm=ldm, tqw=cnt[qsem], first=(tg2 < 2)):
                v.wait_ge(sems[lsem], ldm)
                if not first:
                    v.wait_ge(sems[qsem], tqw)
                v.tensor_scalar(ttmp[:, :wid],
                                wtc[:, buf * PCW:buf * PCW + wid],
                                gcols[:, mi:mi + 1], None, op0=OP.is_gt)
                v.drain()
                v.scalar_tensor_tensor(
                    tqc[:, buf * PCW:buf * PCW + wid],
                    wtc[:, buf * PCW:buf * PCW + wid],
                    gcols[:, 6 + mi:7 + mi], ttmp[:, :wid],
                    op0=OP.is_lt, op1=OP.subtract)
                v.drain()
                inc(v.nop(), "s_wd")
            wd2_marks.append(cnt["s_wd"])

            @blk.gpsimd
            def _(g, w=wd2_marks[-1]):
                g.wait_ge(sems["s_wd"], w)
                dst = t8_[a0 * 128:(a0 + 1) * 128, colbase:colbase + wid]
                inc(g.dma_start(dst,
                                tqc[:, buf * PCW:buf * PCW + wid]),
                    qsem, 16)
        return emit

    _tgc = [0]
    for w16_, _t8, mi in W3:
        if mi < 2:
            for t in range(HID // 128):
                for hf in range(max(1, E // PBW)):
                    wid = min(PBW, E)
                    src = w16_[t * 128:(t + 1) * 128,
                               hf * PBW:hf * PBW + wid]
                    side_jobs.append(mk_p2b_tile(src, mi, _tgc[0], wid))
                    _tgc[0] += 1
        else:
            for r4 in range(E // 128):
                for part in range(HID // PBW):
                    src = w16_[r4 * 128:(r4 + 1) * 128,
                               part * PBW:(part + 1) * PBW]
                    side_jobs.append(mk_p2b_tile(src, mi, _tgc[0], PBW))
                    _tgc[0] += 1
        side_jobs.append(mk_gcols(mi))
    tern_jobs = []
    for w16_, t8_, mi in W3:
        if mi < 2:
            for a0 in range(EC):
                for p_ in range(max(1, HID // PCW)):
                    tern_jobs.append((w16_, t8_, mi, a0, p_ * PCW,
                                      min(PCW, HID)))
        else:
            for a0 in range(HC):
                for p_ in range(max(1, E // PCW)):
                    tern_jobs.append((w16_, t8_, mi, a0, p_ * PCW,
                                      min(PCW, E)))

    n_units = MYNB * H
    side_done = [0]

    def pump_side_jobs(unit_idx):
        want = (unit_idx + 1) * len(side_jobs) // n_units
        while side_done[0] < want:
            side_jobs[side_done[0]]()
            side_done[0] += 1

    SCL = 1.0 / math.sqrt(DH)
    for g in range(MYNB):
        for h in range(H):
            kv = h // GQ
            KT = {p: 4 * (my_par[p][g] + 1) for p in (0, 1)}
            bases = {p: dict(CN[p]) for p in (0, 1)}

            @blk.tensor
            def _(te, g=g, h=h, kv=kv, KT=KT, bases=bases):
                r = parity_reg(te)
                for p in (0, 1):
                    ctxm = te.If_eq(r, 0) if p == 0 else te.Else()
                    with ctxm:
                        B, n = bases[p], KT[p]
                        te.wait_ge(sems["s_dv"], B["s_dv"])
                        for j in range(n):
                            if j >= 2:
                                te.wait_ge(sems["s_ac"], B["s_ac"] + j - 1)
                            mm = te.matmul(
                                pb[j % 2][:, :],
                                Kres[:, kv * S + j * 128:kv * S + (j + 1) * 128],
                                Qres[:, h * MYT + g * TB:h * MYT + (g + 1) * TB],
                                start=True, stop=True)
                            binc(p, mm, "s_pe")
                            if j >= 1:
                                jj = j - 1
                                te.wait_ge(sems["s_ac"], B["s_ac"] + jj + 1)
                                mm = te.matmul(
                                    pb[2][:, :],
                                    Vres[:, kv * S + jj * 128:kv * S + (jj + 1) * 128],
                                    ptile[:, (jj % 2) * TB:(jj % 2 + 1) * TB],
                                    start=(jj == 0), stop=(jj == n - 1))
                                binc(p, mm, "s_ko")
                                mm = te.matmul(
                                    pb[3][:, :], onesb[:, :],
                                    ptile[:, (jj % 2) * TB:(jj % 2 + 1) * TB],
                                    start=(jj == 0), stop=(jj == n - 1))
                                if jj < n - 1:
                                    binc(p, mm, "s_ko")
                        jj = n - 1
                        te.wait_ge(sems["s_ac"], B["s_ac"] + n)
                        mm = te.matmul(
                            pb[2][:, :],
                            Vres[:, kv * S + jj * 128:kv * S + (jj + 1) * 128],
                            ptile[:, (jj % 2) * TB:(jj % 2 + 1) * TB],
                            start=(jj == 0), stop=True)
                        binc(p, mm, "s_ko")
                        mm = te.matmul(
                            pb[3][:, :], onesb[:, :],
                            ptile[:, (jj % 2) * TB:(jj % 2 + 1) * TB],
                            start=(jj == 0), stop=True)
                        binc(p, mm, "s_fb")

            @blk.vector
            def _(v, g=g, h=h, KT=KT, bases=bases):
                r = parity_reg(v)
                for p in (0, 1):
                    ctxm = v.If_eq(r, 0) if p == 0 else v.Else()
                    with ctxm:
                        B, n = bases[p], KT[p]
                        for ridx, j in enumerate(range(n - 4, n)):
                            v.wait_ge(sems["s_pe"], B["s_pe"] + j + 1)
                            mask_r = j - (n - 4)
                            tt = v.tensor_tensor(
                                pb[j % 2][:, :], pb[j % 2][:, :],
                                masks[:, mask_r * TB:(mask_r + 1) * TB],
                                op=OP.add)
                            binc(p, tt, "s_dv")
                        v.wait_ge(sems["s_fb"], B["s_fb"] + 1)
                        v.reciprocal(rsum[:, :], pb[3][:, :])
                        v.drain()
                        tt = v.tensor_tensor(
                            attst[:, h * TB:(h + 1) * TB],
                            pb[2][:, :], rsum[:, :], op=OP.mult)
                        binc(p, tt, "s_dv")

            @blk.scalar
            def _(sc, g=g, h=h, KT=KT, bases=bases):
                r = parity_reg(sc)
                for p in (0, 1):
                    ctxm = sc.If_eq(r, 0) if p == 0 else sc.Else()
                    with ctxm:
                        B, n = bases[p], KT[p]
                        for j in range(n):
                            band = j >= n - 4
                            if band:
                                sc.wait_ge(sems["s_dv"],
                                           B["s_dv"] + (j - (n - 4)) + 1)
                            else:
                                sc.wait_ge(sems["s_pe"], B["s_pe"] + j + 1)
                            if j >= 2:
                                sc.wait_ge(sems["s_ko"], B["s_ko"] + 2 * (j - 2) + 1)
                            a = sc.activation(
                                ptile[:, (j % 2) * TB:(j % 2 + 1) * TB],
                                pb[j % 2][:, :], AF.Exp, scale=SCL,
                                bias=negB[:, h:h + 1])
                            binc(p, a, "s_ac")

            pump_side_jobs(g * H + h)

        att_dv = CN[0]["s_dv"]
        assert att_dv == CN[1]["s_dv"]
        QN = max(1, EC // 4)
        for eg in range(EC // 2):
            qld = []
            for q in range(QN):
                slot = wo_qctr[0] % 2
                wsem = "s_wo0" if slot == 0 else "s_wo1"
                wo_qctr[0] += 1

                @blk.sync
                def _(s, g=g, eg=eg, q=q, slot=slot,
                      w=(wo_q_marks[-2] if len(wo_q_marks) >= 2 else None)):
                    s.wait_ge(sems["c_aw"], caw_wo)
                    if w is not None:
                        s.wait_ge(sems["s_wt"], w)
                    for cq in range(4):
                        c = q * 4 + cq
                        u = s.dma_start_transpose(
                            wob[:, slot * 1024 + cq * 256:
                                slot * 1024 + (cq + 1) * 256],
                            wo16[eg * 256:(eg + 1) * 256,
                                 c * 128:(c + 1) * 128])
                        uinc(u, wsem, 16)
                qld.append((wsem, CN[0][wsem]))
                wo_q_marks.append(CN[0]["s_wt"] + q + 1)

            @blk.tensor
            def _(te, g=g, eg=eg, att_dv=att_dv, qld=qld, qbase=wo_qctr[0] - QN,
                  w=CN[0]["s_dv"]):
                te.wait_ge(sems["s_dv"], max(att_dv, w))
                for q in range(QN):
                    te.wait_ge(sems[qld[q][0]], qld[q][1])
                    slot = (qbase + q) % 2
                    for ee in range(2):
                        for cq in range(4):
                            c = q * 4 + cq
                            mm = te.matmul(
                                pb[4 + ee][:, :],
                                wob[:, slot * 1024 + cq * 256 + ee * 128:
                                    slot * 1024 + cq * 256 + (ee + 1) * 128],
                                attst[:, c * TB:(c + 1) * TB],
                                start=(c == 0), stop=(c == EC - 1))
                    uinc(mm, "s_wt")
            wo_pe = CN[0]["s_wt"]

            for ee in range(2):
                e = eg * 2 + ee

                @blk.vector
                def _(v, g=g, e=e, ee=ee, wo_pe=wo_pe):
                    v.wait_ge(sems["s_wt"], wo_pe)
                    st = v.scalar_tensor_tensor(
                        xres[:, e * MYT + g * TB:e * MYT + (g + 1) * TB],
                        pb[4 + ee][:, :], modc[:, 2 * EC + e:2 * EC + e + 1],
                        xres[:, e * MYT + g * TB:e * MYT + (g + 1) * TB],
                        op0=OP.mult, op1=OP.add)
                    v.drain()
                    uinc(v.nop(), "s_dv")

    while side_done[0] < len(side_jobs):
        side_jobs[side_done[0]]()
        side_done[0] += 1

    assert cnt0 == cnt1, (cnt0, cnt1)
    for _k in ("s_pe", "s_dv", "s_ac", "s_ko", "s_fb", "s_wt", "s_qw"):
        cnt[_k] = cnt0[_k]

    if "gcols" in dbg:
        @blk.scalar
        def _(sc, w=cnt["s_wd"]):
            sc.wait_ge(sems["s_wd"], w)
            tap(sc, "gcols", gcols[:, :], 128, 12)

    if "attst" in dbg:
        @blk.scalar
        def _(sc, w=cnt["s_dv"]):
            sc.wait_ge(sems["s_dv"], w)
            tap(sc, "attst", attst[:, :], 128, H * TB)
    if "xmid" in dbg:
        @blk.scalar
        def _(sc, w=cnt["s_dv"]):
            sc.wait_ge(sems["s_dv"], w)
            tap(sc, "xmid", xres[:, :], 128, EC * MYT)

    esB.close()

    esB = contextlib.ExitStack()
    blk = esB.enter_context(nc.Block("p2c", no_gpsimd_drain=True))
    for tg2, job in enumerate(tern_jobs):
        mk_p2c_tile(job, tg2)()

    @blk.gpsimd
    def _(g):
        g.wait_ge(sems["s_tq"], cnt["s_tq"])
        g.wait_ge(sems["s_tq2"], cnt["s_tq2"])

    cnt_tq = (cnt["s_tq"], cnt["s_tq2"])
    esB.close()

    esP2.close()
    esP2w.close()
    esKV.close()
    esP3 = contextlib.ExitStack()
    gvb = mk(esP3, "gvb", [128, HC * TB], F16)
    w8b = mk(esP3, "w8b", [128, 2 * EC * TB], FP8)
    h2b = mk(esP3, "h2b", [128, EC * TB], BF16)
    rqb = mk(esP3, "rqb", [128, EC * TB], BF16)
    sgam1 = mk(esP3, "sgam1", [128, TB], F32)
    bgam1 = mk(esP3, "bgam1", [128, TB], F32)
    sgam2 = sgam1
    bgam2 = bgam1
    ostg = mk(esP3, "ostg", [128, 2 * 512], F32)
    tA2 = mk(esP3, "tA2", [128, TB], F32)
    tB2 = mk(esP3, "tB2", [128, TB], F32)
    tC2 = mk(esP3, "tC2", [128, TB], F32)
    Gpad = mk(esP3, "Gpad", [128, 128], F32)
    stagb = ostg[:, 0:512]
    ident32 = mk(esP3, "ident32", [128, 128], F32)
    w8o = mk(esP3, "w8o", [128, 4 * TB], FP8)
    bC1c = mk(esP3, "bC1c", [128, 1], F32)
    bC2c = mk(esP3, "bC2c", [128, 1], F32)
    bC3c = mk(esP3, "bC3c", [128, 1], F32)

    esB = contextlib.ExitStack()
    blk = esB.enter_context(nc.Block("p3", no_gpsimd_drain=True))

    @blk.vector
    def _(v):
        v.tensor_scalar(cacol[:, :], modc[:, 5 * EC:6 * EC],
                        gcols[:, 5:6], None, op0=OP.mult)
        v.tensor_tensor(bias2[:, :], modc[:, 5 * EC:6 * EC], biaso[:, :],
                        op=OP.mult)
        v.memset(Gpad[:, :], 0.0)
        v.memset(stagb[:, :], 0.0)
        v.memset(bC1c[:, :], C1)
        v.memset(bC2c[:, :], C2)
        v.memset(bC3c[:, :], C3)
        v.tensor_copy(ident32[:, :], identb[:, :])
        v.drain()
        inc(v.nop(), "s_dv")

    def emit_gamma2(src_ap, sg_ap, bg_ap):
        @blk.tensor
        def _(te, w=cnt["s_dv"]):
            te.wait_ge(sems["s_dv"], w)
            for i in range(4):
                mm = te.transpose(pb[4][:, i * 128:(i + 1) * 128],
                                  src_ap[:, i * 128:(i + 1) * 128],
                                  ident32[:, :])
            inc(mm, "s_pe")

        @blk.vector
        def _(v, w=cnt["s_pe"]):
            v.wait_ge(sems["s_pe"], w)
            for i in range(4):
                v.tensor_reduce(Gpad[:, 32 * i:32 * i + 1],
                                pb[4][:, i * 128:(i + 1) * 128],
                                axis=AX.X, op=OP.max)
            v.drain()
            inc(v.nop(), "s_dv")

        @blk.tensor
        def _(te, w=cnt["s_dv"]):
            te.wait_ge(sems["s_dv"], w)
            inc(te.transpose(pb[5][:, 0:128], Gpad[:, :], ident32[:, :]),
                "s_pe")

        @blk.vector
        def _(v, w=cnt["s_pe"],
              stw=(st_marks[0][-1] if st_marks[0] else None)):
            v.wait_ge(sems["s_pe"], w)
            if stw is not None:
                v.wait_ge(sems["s_st"], stw)
            v.memset(stagb[:, :], 0.0)
            v.drain()
            for i in range(4):
                v.tensor_copy(stagb[32 * i:32 * i + 1, i * 128:(i + 1) * 128],
                              pb[5][32 * i:32 * i + 1, 0:128])
            v.drain()
            inc(v.nop(), "s_dv")

        @blk.tensor
        def _(te, w=cnt["s_dv"]):
            te.wait_ge(sems["s_dv"], w)
            inc(te.matmul(pb[6][:, :], onesf[:, :], stagb[:, :],
                          start=True, stop=True), "s_pe")

        @blk.vector
        def _(v, w=cnt["s_pe"]):
            v.wait_ge(sems["s_pe"], w)
            v.tensor_scalar(tA2[:, :], pb[6][:, :], 1e-5, None, op0=OP.max)
            v.drain()
            v.reciprocal(tB2[:, :], tA2[:, :])
            v.drain()
            v.tensor_scalar(sg_ap, tB2[:, :], 127.0, None, op0=OP.mult)
            inc(v.tensor_scalar(bg_ap, tA2[:, :], 1.0 / 127.0, None,
                                op0=OP.mult), "s_dv")

    tD2 = ostg[:, 512:1024]
    rnd_sets = [(tA, tB, tC, tD), (tA2, tB2, tC2, tD2)]
    gv_half_pe, gv_epi1, gv_silu, out_pe_marks = [], [], [], []
    st_marks, st_qctr = {0: [], 1: []}, [0]

    def emit_rounds(src_of, dst_of, sx_ap, n):
        marks, u_marks, ac_marks = [], [], []
        stw2 = st_marks[1][-1] if st_marks[1] else None
        for c in range(n):
            s0, s1, s2, s3 = rnd_sets[c % 2]

            @blk.vector
            def _(v, s0=s0, src=src_of(c),
                  war=(ac_marks[c - 2] if c >= 2 else None)):
                if war is not None:
                    v.wait_ge(sems["s_ac"], war)
                inc(v.tensor_tensor(s0[:, :], src, sx_ap, op=OP.mult),
                    "s_dv")
            u_marks.append(cnt["s_dv"])

            @blk.scalar
            def _(sc, s0=s0, s1=s1, s2=s2, s3=s3, w=u_marks[c],
                  stw2=(stw2 if c == 1 else None)):
                sc.wait_ge(sems["s_dv"], w)
                if stw2 is not None:
                    sc.wait_ge(sems["s_st2"], stw2)
                sc.activation(s1[:, :], s0[:, :], AF.Identity,
                              bias=bC1c[:, :1])
                sc.activation(s2[:, :], s0[:, :], AF.Identity,
                              bias=bC2c[:, :1])
                inc(sc.activation(s3[:, :], s0[:, :], AF.Identity,
                                  bias=bC3c[:, :1]), "s_ac")
            ac_marks.append(cnt["s_ac"])

            @blk.vector
            def _(v, s1=s1, s2=s2, s3=s3, dst=dst_of(c), w=ac_marks[c]):
                v.wait_ge(sems["s_ac"], w)
                v.scalar_tensor_tensor(s1[:, :], s2[:, :], -C2, s1[:, :],
                                       op0=OP.add, op1=OP.add)
                v.drain()
                inc(v.scalar_tensor_tensor(dst, s3[:, :], C1 - C3,
                                           s1[:, :], op0=OP.add,
                                           op1=OP.subtract), "s_dv")
            marks.append(cnt["s_dv"])
        return marks

    rqm_of = {}
    stats_mark_of = {}

    def emit_prep_a(g):
        @blk.scalar
        def _(sc, g=g, w=cnt["s_dv"], wdb=cnt["s_db"]):
            sc.wait_ge(sems["s_dv"], w)
            sc.wait_ge(sems["s_db"], wdb)
            for c in range(EC):
                a = sc.activation(h2b[:, c * TB:(c + 1) * TB],
                                  xres[:, c * MYT + g * TB:c * MYT + (g + 1) * TB],
                                  AF.Square)
            inc(a, "s_ac")
        sq_ac = cnt["s_ac"]

        @blk.tensor
        def _(te, g=g, sq_ac=sq_ac, w=cnt["s_dv"]):
            te.wait_ge(sems["s_ac"], sq_ac)
            te.wait_ge(sems["s_dv"], w)
            for c in range(EC):
                mm = te.matmul(pb[0][:, :], onesb[:, :],
                               xres[:, c * MYT + g * TB:c * MYT + (g + 1) * TB],
                               start=(c == 0), stop=(c == EC - 1))
            inc(mm, "s_pe")
            for c in range(EC):
                mm = te.matmul(pb[1][:, :], onesb[:, :],
                               h2b[:, c * TB:(c + 1) * TB],
                               start=(c == 0), stop=(c == EC - 1))
            inc(mm, "s_pe")

        @blk.vector
        def _(v, w=cnt["s_pe"]):
            v.wait_ge(sems["s_pe"], w)
            v.tensor_scalar(tA[:, :], pb[0][:, :], 1.0 / E, None, op0=OP.mult)
            v.drain()
            v.tensor_tensor(tB[:, :], tA[:, :], tA[:, :], op=OP.mult)
            v.drain()
            v.scalar_tensor_tensor(tC[:, :], pb[1][:, :], 1.0 / E, tB[:, :],
                                   op0=OP.mult, op1=OP.subtract)
            v.drain()
            inc(v.tensor_scalar(tB[:, :], tC[:, :], cfg.eps, None, op0=OP.add),
                "s_dv")
        stats_mark_of[g] = cnt["s_dv"]

        @blk.scalar
        def _(sc, w=cnt["s_dv"]):
            sc.wait_ge(sems["s_dv"], w)
            inc(sc.activation(tC[:, :], tB[:, :], AF.Sqrt), "s_ac")

        @blk.vector
        def _(v, g=g, w=cnt["s_ac"]):
            v.wait_ge(sems["s_ac"], w)
            v.reciprocal(tB[:, :], tC[:, :])
            v.drain()
            for c in range(EC):
                v.tensor_tensor(tC[:, :],
                                xres[:, c * MYT + g * TB:c * MYT + (g + 1) * TB],
                                tA[:, :], op=OP.subtract)
                v.drain()
                v.tensor_tensor(tD[:, :], tC[:, :], tB[:, :], op=OP.mult)
                v.drain()
                v.tensor_scalar(h2b[:, c * TB:(c + 1) * TB], tD[:, :],
                                modc[:, 4 * EC + c:4 * EC + c + 1],
                                modc[:, 3 * EC + c:3 * EC + c + 1],
                                op0=OP.mult, op1=OP.add)
                v.drain()
            v.tensor_reduce(tC[:, :],
                            h2b.rearrange("p (c t) -> p t c", c=EC),
                            axis=AX.X, op=OP.max, apply_absolute_value=True)
            v.drain()
            inc(v.nop(), "s_dv")

    def emit_prep_b(g):
        emit_gamma2(tC[:, :], sgam1[:, :], bgam1[:, :])

    def emit_prep_c(g):
        @blk.vector
        def _(v, wdb=cnt["s_db"], wpe=cnt["s_pe"]):
            v.wait_ge(sems["s_db"], wdb)
            v.wait_ge(sems["s_pe"], wpe)
            v.drain()
            inc(v.nop(), "s_dv")
        rqm_of[g] = emit_rounds(
            lambda c: h2b[:, c * TB:(c + 1) * TB],
            lambda c: rqb[:, c * TB:(c + 1) * TB],
            sgam1[:, :], EC)
        if g == 0 and "rqb" in dbg:
            @blk.scalar
            def _(sc, w=rqm_of[g][-1]):
                sc.wait_ge(sems["s_dv"], w)
                tap(sc, "rqb", rqb[:, :], 128, EC * TB)
        if g == 0 and "h2b" in dbg:
            @blk.scalar
            def _(sc, w=rqm_of[g][-1]):
                sc.wait_ge(sems["s_dv"], w)
                tap(sc, "h2b", h2b[:, :], 128, EC * TB)
        if g == 0 and "sgam" in dbg:
            @blk.scalar
            def _(sc, w=rqm_of[g][-1]):
                sc.wait_ge(sems["s_dv"], w)
                tap(sc, "sgam", sgam1[:, :], 128, TB)

    emit_prep_a(0)
    emit_prep_b(0)
    emit_prep_c(0)

    for g in range(MYNB):
        rq_marks = rqm_of[g]
        rq_dv = rq_marks[-1]

        HALF = EC * TB
        for hg in range(HC // 4):
            for k in range(2):
                idx = g * (HC // 2) + hg * 2 + k

                gsem = "s_gv0" if (len(gv_half_pe) % 2 == 0) else "s_gv1"

                @blk.sync
                def _(s, hg=hg, k=k, gsem=gsem,
                      w=(gv_half_pe[-2] if len(gv_half_pe) >= 2 else None),
                      first=(len(gv_half_pe) == 0)):
                    if first:
                        s.wait_ge(sems["s_tq"], cnt_tq[0])
                        s.wait_ge(sems["s_tq2"], cnt_tq[1])
                    if w is not None:
                        s.wait_ge(sems["s_pe"], w)
                    for d in range(EC):
                        inc(s.dma_start(
                            w8b[:, k * HALF + d * 256:k * HALF + d * 256 + 256],
                            tg8[d * 128:(d + 1) * 128,
                                hg * 512 + k * 256:hg * 512 + k * 256 + 256]),
                            gsem, 16)
                    for d in range(EC):
                        inc(s.dma_start(
                            w8b[:, k * HALF + (EC + d) * 256:
                                k * HALF + (EC + d) * 256 + 256],
                            tv8[d * 128:(d + 1) * 128,
                                hg * 512 + k * 256:hg * 512 + k * 256 + 256]),
                            gsem, 16)
                gv_ld = cnt[gsem]

                for hp in range(2):
                    h = hg * 4 + k * 2 + hp

                    @blk.tensor
                    def _(te, h=h, k=k, hp=hp, gv_ld=gv_ld, gsem=gsem,
                          rq_dv=rq_dv,
                          bankw=(gv_epi1[-2] if len(gv_epi1) >= 2 else None)):
                        te.wait_ge(sems[gsem], gv_ld)
                        if h == 0:
                            te.wait_ge(sems["s_dv"], rq_dv)
                        if bankw is not None:
                            te.wait_ge(sems["s_dv"], bankw)
                        pg = pb[2 * (h % 2)]
                        pv = pb[2 * (h % 2) + 1]
                        for d in range(EC):
                            mm = te.matmul(
                                pg[:, :],
                                w8b[:, k * HALF + d * 256 + hp * 128:
                                    k * HALF + d * 256 + (hp + 1) * 128],
                                rqb[:, d * TB:(d + 1) * TB],
                                start=(d == 0), stop=(d == EC - 1))
                        inc(mm, "s_pe")
                        for d in range(EC):
                            mm = te.matmul(
                                pv[:, :],
                                w8b[:, k * HALF + (EC + d) * 256 + hp * 128:
                                    k * HALF + (EC + d) * 256 + (hp + 1) * 128],
                                rqb[:, d * TB:(d + 1) * TB],
                                start=(d == 0), stop=(d == EC - 1))
                        inc(mm, "s_pe")
                    gv_pe = cnt["s_pe"]

                    @blk.vector
                    def _(v, h=h, gv_pe=gv_pe,
                          silu_war=(gv_silu[-2] if len(gv_silu) >= 2
                                    else None)):
                        v.wait_ge(sems["s_pe"], gv_pe)
                        if silu_war is not None:
                            v.wait_ge(sems["s_ac"], silu_war)
                        pg = pb[2 * (h % 2)]
                        pv = pb[2 * (h % 2) + 1]
                        gp = [tA, tC][h % 2]
                        vp = [tB, tD][h % 2]
                        v.scalar_tensor_tensor(gp[:, :], pg[:, :],
                                               gcols[:, 3:4], bgam1[:, :],
                                               op0=OP.mult, op1=OP.mult)
                        v.drain()
                        inc(v.scalar_tensor_tensor(vp[:, :], pv[:, :],
                                                   gcols[:, 4:5],
                                                   bgam1[:, :],
                                                   op0=OP.mult,
                                                   op1=OP.mult), "s_dv")
                    gv_epi1.append(cnt["s_dv"])

                    @blk.scalar
                    def _(sc, h=h, w=gv_epi1[-1]):
                        sc.wait_ge(sems["s_dv"], w)
                        inc(sc.activation([tE16, tF16][h % 2][:, :],
                                          [tA, tC][h % 2][:, :], AF.Silu,
                                          bias=biasg[:, h:h + 1]), "s_ac")
                    gv_silu.append(cnt["s_ac"])

                    @blk.vector
                    def _(v, h=h, w=gv_silu[-1]):
                        v.wait_ge(sems["s_ac"], w)
                        inc(v.scalar_tensor_tensor(
                            gvb[:, h * TB:(h + 1) * TB],
                            [tB, tD][h % 2][:, :], biasvb[:, h:h + 1],
                            [tE16, tF16][h % 2][:, :],
                            op0=OP.add, op1=OP.mult), "s_dv")
                gv_half_pe.append(gv_pe)

        if g == 0 and "gvb" in dbg:
            @blk.scalar
            def _(sc, w=cnt["s_dv"]):
                sc.wait_ge(sems["s_dv"], w)
                tap(sc, "gvb", gvb[:, :], 128, HC * TB)

        @blk.vector
        def _(v):
            v.drain()
            v.tensor_reduce(tC[:, :],
                            gvb.rearrange("p (c t) -> p t c", c=HC),
                            axis=AX.X, op=OP.max, apply_absolute_value=True)
            v.drain()
            inc(v.nop(), "s_dv")

        emit_gamma2(tC[:, :], sgam2[:, :], bgam2[:, :])

        @blk.vector
        def _(v, wdb=cnt["s_db"]):
            v.wait_ge(sems["s_db"], wdb)
            v.drain()
            inc(v.nop(), "s_dv")
        r2_marks = emit_rounds(
            lambda c: gvb[:, c * TB:(c + 1) * TB],
            lambda c: gvb[:, c * TB:(c + 1) * TB],
            sgam2[:, :], HC)
        r2_dv = r2_marks[-1]

        if g + 1 < MYNB:
            emit_prep_a(g + 1)

        for sw in range(EC // 4):
            sw_pe0 = cnt["s_pe"]
            for h in range(HC):
                slot = h % 4
                osem = ["s_ld", "s_l3", "s_l2", "s_l4"][slot]
                war = (out_pe_marks[-4] if len(out_pe_marks) >= 4 else None)

                def _ld(e, h=h, sw=sw, slot=slot, osem=osem, war=war):
                    if war is not None:
                        e.wait_ge(sems["s_pe"], war)
                    inc(e.dma_start(
                        w8o[:, slot * TB:(slot + 1) * TB],
                        to8[h * 128:(h + 1) * 128, sw * 512:(sw + 1) * 512]),
                        osem, 16)
                if h % 2 == 0:
                    blk.sync(_ld)
                else:
                    blk.scalar(_ld)

                @blk.tensor
                def _(te, sw=sw, h=h, slot=slot, osem=osem, ld=cnt[osem],
                      r2m=(r2_marks[h] if sw == 0 else None),
                      evacw=((stats_mark_of.get(g + 1, cnt["s_dv"])
                              if sw == 0 else cnt["s_dv"])
                             if h == 0 else None)):
                    te.wait_ge(sems[osem], ld)
                    if r2m is not None:
                        te.wait_ge(sems["s_dv"], r2m)
                    if evacw is not None:
                        te.wait_ge(sems["s_dv"], evacw)
                    for ee in range(4):
                        mm = te.matmul(
                            pb[ee][:, :],
                            w8o[:, slot * TB + ee * 128:
                                slot * TB + (ee + 1) * 128],
                            gvb[:, h * TB:(h + 1) * TB],
                            start=(h == 0), stop=(h == HC - 1))
                    inc(mm, "s_pe")
                out_pe_marks.append(cnt["s_pe"])

            osw_pe = cnt["s_pe"]
            for ee in range(4):
                e = sw * 4 + ee

                @blk.vector
                def _(v, g=g, e=e, ee=ee, osw_pe=osw_pe, wdb=cnt["s_db"]):
                    v.wait_ge(sems["s_pe"], osw_pe)
                    v.wait_ge(sems["s_db"], wdb)
                    v.scalar_tensor_tensor(tA2[:, :], pb[ee][:, :],
                                           cacol[:, e:e + 1], bgam2[:, :],
                                           op0=OP.mult, op1=OP.mult)
                    v.drain()
                    v.scalar_tensor_tensor(
                        rqb[:, e * TB:(e + 1) * TB], tA2[:, :],
                        bias2[:, e:e + 1],
                        xres[:, e * MYT + g * TB:e * MYT + (g + 1) * TB],
                        op0=OP.add, op1=OP.add)
                    v.drain()
                    inc(v.nop(), "s_dv")
        xout_dv = cnt["s_dv"]
        if g == 0 and "xoutb" in dbg:
            @blk.scalar
            def _(sc, w=xout_dv):
                sc.wait_ge(sems["s_dv"], w)
                tap(sc, "xoutb", rqb[:, :], 128, EC * TB)

        n_q = max(1, EC // 4)
        qt = min(4, EC)
        for sub in range(TB // 128):
            for q in range(n_q):
                pp = st_qctr[0] % 2
                st_qctr[0] += 1

                @blk.tensor
                def _(te, g=g, sub=sub, q=q, pp=pp, xout_dv=xout_dv,
                      w=cnt["s_dv"]):
                    te.wait_ge(sems["s_dv"], max(xout_dv, w))
                    for e4 in range(qt):
                        e = q * qt + e4
                        mm = te.transpose(
                            pbT[:, pp * 512 + e4 * 128:
                                pp * 512 + (e4 + 1) * 128],
                            rqb[:, e * TB + sub * 128:e * TB + (sub + 1) * 128],
                            identb[:, :])
                    inc(mm, "s_pe")

                @blk.vector
                def _(v, pp=pp, w=cnt["s_pe"],
                      stw=(st_marks[pp][-1] if st_marks[pp] else None)):
                    v.wait_ge(sems["s_pe"], w)
                    if stw is not None:
                        v.wait_ge(sems["s_st" if pp == 0 else "s_st2"],
                                  stw)
                    v.tensor_copy(ostg[:, pp * 512:(pp + 1) * 512],
                                  pbT[:, pp * 512:(pp + 1) * 512])
                    v.drain()
                    inc(v.nop(), "s_dv")

                @blk.scalar
                def _(sc, g=g, sub=sub, q=q, pp=pp, w=cnt["s_dv"]):
                    sc.wait_ge(sems["s_dv"], w)
                    inc(sc.dma_start(
                        y_out[g * TB + sub * 128:g * TB + (sub + 1) * 128,
                              q * qt * 128:(q + 1) * qt * 128],
                        ostg[:, pp * 512:pp * 512 + qt * 128]),
                        "s_st" if pp == 0 else "s_st2", 16)
                st_marks[pp].append(cnt["s_st" if pp == 0 else "s_st2"])

        if g + 1 < MYNB:
            emit_prep_b(g + 1)
            emit_prep_c(g + 1)

    @blk.sync
    def _(s):
        s.wait_ge(sems["s_st"], cnt["s_st"])
        s.wait_ge(sems["s_st2"], cnt["s_st2"])
        s.wait_ge(sems["s_db"], cnt["s_db"])

    esB.close()
    esP3.close()
    es0.close()
    return nc



def kernel(L0_attn_x, L0_attn_wq, L0_attn_wk, L0_attn_wv, L0_attn_wo,
           L0_attn_bq, L0_attn_bk, L0_attn_bv, L0_ffn_gate_w, L0_ffn_gate_b,
           L0_ffn_val_w, L0_ffn_val_b, L0_ffn_out_w, L0_ffn_out_b, L0_c,
           L0_wmod):
    cfg = Cfg()
    nc = build_program(cfg)
    A = np.ascontiguousarray

    def row(v):
        return A(np.asarray(v, dtype=np.float32).reshape(1, -1))

    shared = {
        "wmod": A(np.asarray(L0_wmod, np.float32)),
        "wq": A(np.asarray(L0_attn_wq, np.float32)),
        "wk": A(np.asarray(L0_attn_wk, np.float32)),
        "wv": A(np.asarray(L0_attn_wv, np.float32)),
        "wo": A(np.asarray(L0_attn_wo, np.float32)),
        "bq": row(L0_attn_bq), "bk": row(L0_attn_bk), "bv": row(L0_attn_bv),
        "gate_w": A(np.asarray(L0_ffn_gate_w, np.float32)),
        "val_w": A(np.asarray(L0_ffn_val_w, np.float32)),
        "out_w": A(np.asarray(L0_ffn_out_w, np.float32)),
        "gate_b": row(L0_ffn_gate_b), "val_b": row(L0_ffn_val_b),
        "ob": row(L0_ffn_out_b),
    }
    x_np = np.asarray(L0_attn_x, np.float32)
    c_np = np.asarray(L0_c, np.float32)
    B = x_np.shape[0]
    n_cores = 2 * B
    in_maps = []
    for core in range(n_cores):
        b, p = core // 2, core % 2
        im = dict(shared)
        im["x"] = A(x_np[b])
        im["c_row"] = A(c_np[b:b + 1])
        im["parity"] = np.array([[p]], np.int32)
        in_maps.append(im)

    res = run_bass_kernel_spmd(nc, in_maps, core_ids=list(range(n_cores)))

    out = np.empty((B, cfg.S, cfg.E), np.float32)
    for core in range(n_cores):
        b, p = core // 2, core % 2
        y = res.results[core]["y"]
        for gi, m in enumerate(cfg.my_blocks(p)):
            out[b, m * cfg.TB:(m + 1) * cfg.TB] = y[gi * cfg.TB:(gi + 1) * cfg.TB]
    return out
